# version 15
# baseline (speedup 1.0000x reference)
import time
from contextlib import ExitStack
import numpy as np
import ml_dtypes
import concourse.bass as bass
import concourse.mybir as mybir
from concourse.alu_op_type import AluOpType
from concourse.bass_utils import run_bass_kernel_spmd

F32 = mybir.dt.float32
BF16 = mybir.dt.bfloat16
I32 = mybir.dt.int32
AF = mybir.ActivationFunctionType
BFNP = ml_dtypes.bfloat16

B, SEQ, DM = 4, 4096, 1024
DI, NST, FF = 2048, 16, 4096
TOK = 2048
PRE = 256
CH = 512
NCH = 4
XROWS = 512 + TOK + 256
DT = DM // 128
DIT = DI // 128
FT = FF // 128
NEG = -960000.0
EPS = 1e-5
DEBUG = False
RACE_CHECK = False

CHUNKS = [(-256, 256)] + [(c * CH, CH) for c in range(NCH)]


def _f32(x):
    return np.ascontiguousarray(np.asarray(x), dtype=np.float32)


def _bf(x):
    return np.ascontiguousarray(np.asarray(x).astype(BFNP))


class Sem:
    def __init__(self, nc, name):
        self.h = nc.alloc_semaphore(name)
        self.n = 0

    def inc(self, inst, k):
        inst.then_inc(self.h, k)
        self.n += k


class G:
    pass


class _RInst:
    __slots__ = ("incs",)
    def __init__(self):
        self.incs = []
    def then_inc(self, sem, k):
        self.incs.append((sem, k))
        return self


class _REng:
    def __init__(self, rb, name):
        self._rb = rb
        self._name = name
    def __getattr__(self, meth):
        def call(*a, **kw):
            ri = _RInst()
            self._rb.q[self._name].append((meth, a, kw, ri))
            return ri
        return call


class RBlk:
    ENGINES = ("sync", "vector", "scalar", "tensor", "gpsimd")
    def __init__(self, blk):
        self._blk = blk
        self.q = {e: [] for e in self.ENGINES}
        for e in self.ENGINES:
            setattr(self, e, self._mk(e))
        self.dsync = self._mk_direct("sync")
        self.dgpsimd = self._mk_direct("gpsimd")
    def _mk(self, name):
        def deco(fn):
            fn(_REng(self, name))
            return fn
        return deco
    def _mk_direct(self, name):
        def deco(fn):
            self.flush(name)
            real = getattr(self._blk, name)
            @real
            def _(eng):
                fn(eng)
            return fn
        return deco
    def flush(self, engine=None):
        engines = [engine] if engine else list(self.ENGINES)
        for e in engines:
            items = self.q[e]
            if not items:
                continue
            self.q[e] = []
            real = getattr(self._blk, e)
            @real
            def _(eng, items=items):
                for meth, a, kw, ri in items:
                    inst = getattr(eng, meth)(*a, **kw)
                    for sem, k in ri.incs:
                        inst.then_inc(sem, k)
    def flush_all(self):
        self.flush()




def build_masks(half):
    def block_mask(gb):
        r = np.arange(256)[:, None]
        c = np.arange(768)[None, :]
        i = gb * 256 + r
        j = (gb - 1) * 256 + c
        rel = j - i
        valid = (rel >= -128) & (rel <= 128) & (j >= 0) & (j < SEQ)
        return np.where(valid, 0.0, NEG).astype(np.float32)

    def interior():
        r = np.arange(256)[:, None]
        c = np.arange(768)[None, :]
        rel = c - 256 - r
        return np.where((rel >= -128) & (rel <= 128), 0.0, NEG).astype(np.float32)

    g0 = half * 8
    blocks = [block_mask(g0 - 1) if g0 - 1 >= 0 else interior()]
    for gb in range(g0, g0 + 8):
        blocks.append(block_mask(gb))
    m = np.stack(blocks).reshape(9, 2, 128, 768)
    m = m.transpose(2, 0, 1, 3).reshape(128, 18 * 768)
    return _bf(m)


def tile_cols(vec, ntiles):
    return _f32(np.asarray(vec).reshape(ntiles, 128).T)


def stage_inputs(inputs):
    x = _f32(inputs["L0_attn_a_x"])
    winT = _f32(np.asarray(inputs["L0_attn_b_W_in"]).T)
    wxT = _f32(np.asarray(inputs["L0_attn_b_W_x"]).T)
    woT = _f32(np.asarray(inputs["L0_attn_b_W_out"]).T)
    wc = _f32(inputs["L0_attn_b_W_conv"][:, 0, :])
    alog = _f32(inputs["L0_attn_b_A_log"])
    dpar = _f32(inputs["L0_attn_b_D"])
    b_in = _f32(inputs["L0_attn_b_b_in"])
    bcv = _f32(inputs["L0_attn_b_b_conv"])
    b_x = _f32(inputs["L0_attn_b_b_x"])

    nonce_val = (int(time.time_ns()) % (2 ** 24)) | 1
    common = {
        "wq": _f32(inputs["L0_attn_a_Wq"]),
        "wk": _f32(inputs["L0_attn_a_Wk"]),
        "wv": _f32(inputs["L0_attn_a_Wv"]),
        "bq": tile_cols(inputs["L0_attn_a_bq"], DT),
        "bk": tile_cols(inputs["L0_attn_a_bk"], DT),
        "bv_row": _bf(np.asarray(inputs["L0_attn_a_bv"]).reshape(1, DM)),
        "w1T": _f32(np.asarray(inputs["L0_ffn_W1"]).T),
        "b1": tile_cols(inputs["L0_ffn_b1"], FT),
        "w2T": _f32(np.asarray(inputs["L0_ffn_W2"]).T),
        "b2_row": _bf(np.asarray(inputs["L0_ffn_b2"]).reshape(1, DM)),
        "ident": np.eye(128, dtype=BFNP),
        "nonce_i": np.array([[nonce_val]], np.int32),
        "nonce_row": np.full((128, 8), nonce_val, np.int32),
        "krow": np.tile(np.array([1e-5, 1.0 / 32.0, 1.0, 0.5, 2.0, 0.25, 0.0, 0.0],
                                 np.float32), (128, 1)),
    }
    ims = []
    for c in range(8):
        b_ = c // 2
        half = c % 2
        hs = half * TOK
        xp = np.zeros((XROWS, DM), np.float32)
        lo, hi = hs - 512, hs + TOK + 256
        s0, s1 = max(lo, 0), min(hi, SEQ)
        xp[s0 - lo: s1 - lo] = x[b_, s0:s1]
        perm = np.arange(DI) if half == 0 else np.concatenate(
            [np.arange(1024, 2048), np.arange(1024)])
        im = dict(common)
        im.update({
            "x_in": xp,
            "winT": np.ascontiguousarray(
                np.concatenate([winT[:, perm], winT[:, 2048 + perm]], axis=1)),
            "b_in_xi": tile_cols(b_in[perm], DIT),
            "b_in_res": tile_cols(b_in[2048 + perm], DIT),
            "wc": _f32(wc[perm].reshape(DIT, 128, 4).transpose(1, 0, 2).reshape(128, DIT * 4)),
            "bcv": tile_cols(bcv[perm], DIT),
            "wxT_d": np.ascontiguousarray(wxT[perm][:, :DI][:, perm]),
            "wxT_bc": np.ascontiguousarray(wxT[perm][:, DI:]),
            "b_x_d": tile_cols(b_x[:DI][perm], DIT),
            "b_x_bc": _f32(b_x[DI:].reshape(32, 1)),
            "woT": np.ascontiguousarray(woT[perm]),
            "bo_row": _bf(np.asarray(inputs["L0_attn_b_b_out"]).reshape(1, DM)),
            "alog": _f32(alog[perm].reshape(DIT, 128, NST).transpose(1, 0, 2).reshape(128, DIT * NST)),
            "dpar": tile_cols(dpar[perm], DIT),
            "masks": build_masks(half),
            "role": np.array([[half]], np.int32),
            "role_f": np.full((128, 1), float(half), np.float32),
        })
        ims.append(im)
    return ims



def build_program():
    nc = bass.Bass("TRN2", target_bir_lowering=False,
                   detect_race_conditions=RACE_CHECK)
    g = G()
    g.nc = nc
    din = {}

    def dram_in(name, shape, dt=F32):
        din[name] = nc.dram_tensor(name, shape, dt, kind="ExternalInput")

    dram_in("x_in", [XROWS, DM])
    for n in ("wq", "wk", "wv"):
        dram_in(n, [DM, DM])
    dram_in("bq", [128, DT]); dram_in("bk", [128, DT]); dram_in("bv_row", [1, DM], BF16)
    dram_in("w1T", [DM, FF]); dram_in("b1", [128, FT])
    dram_in("w2T", [FF, DM]); dram_in("b2_row", [1, DM], BF16)
    dram_in("winT", [DM, 2 * DI])
    dram_in("b_in_xi", [128, DIT]); dram_in("b_in_res", [128, DIT])
    dram_in("wc", [128, DIT * 4]); dram_in("bcv", [128, DIT])
    dram_in("wxT_d", [DI, DI]); dram_in("wxT_bc", [DI, 32])
    dram_in("b_x_d", [128, DIT]); dram_in("b_x_bc", [32, 1])
    dram_in("woT", [DI, DM]); dram_in("bo_row", [1, DM], BF16)
    dram_in("alog", [128, DIT * NST]); dram_in("dpar", [128, DIT])
    dram_in("masks", [128, 18 * 768], BF16)
    dram_in("ident", [128, 128], BF16)
    dram_in("role", [1, 1], I32); dram_in("role_f", [128, 1])
    dram_in("nonce_i", [1, 1], I32); dram_in("nonce_row", [128, 8], I32)
    dram_in("krow", [128, 8])
    g.din = din

    g.out = nc.dram_tensor("out", [TOK, DM], F32, kind="ExternalOutput")
    if DEBUG:
        g.dbg_x1 = nc.dram_tensor("dbg_x1", [PRE + TOK, DM], F32, kind="ExternalOutput")
        g.dbg_x2 = nc.dram_tensor("dbg_x2", [PRE + TOK, DM], F32, kind="ExternalOutput")
        g.dbg_x3 = nc.dram_tensor("dbg_x3", [TOK, DM], F32, kind="ExternalOutput")
        g.dbg_xi = nc.dram_tensor("dbg_xi", [DI, TOK], BF16, kind="ExternalOutput")
        g.dbg_y = nc.dram_tensor("dbg_y", [DI, TOK], BF16, kind="ExternalOutput")

    g.x1_h = nc.dram_tensor("x1_h", [PRE + TOK, DM], F32, kind="Internal")
    g.x2_h = nc.dram_tensor("x2_h", [PRE + TOK, DM], F32, kind="Internal")
    g.x3_h = nc.dram_tensor("x3_h", [TOK, DM], F32, kind="Internal")
    g.res_h = nc.dram_tensor("res_h", [DI, TOK], BF16, kind="Internal")
    g.w1t_h = nc.dram_tensor("w1t_h", [DM, FF], BF16, kind="Internal")
    g.w2t_h = nc.dram_tensor("w2t_h", [FF, DM], BF16, kind="Internal")
    g.xi_h = nc.dram_tensor("xi_h", [DI, TOK], BF16, kind="Internal")
    g.y3_h = nc.dram_tensor("y3_h", [DI, TOK], BF16, kind="Internal")

    DUW = DIT * CH
    g.du_sh, g.du_sn, g.y_sh, g.y_sn = {}, {}, {}, {}
    for r in range(2):
        for cc in range(NCH):
            g.du_sh[(r, cc)] = nc.dram_tensor(f"du_sh_{r}_{cc}", [128, 2 * DUW + 2 * CH],
                                              BF16, kind="Internal", addr_space="Shared")
            g.du_sn[(r, cc)] = nc.dram_tensor(f"du_sn_{r}_{cc}", [128, 8], I32,
                                              kind="Internal", addr_space="Shared")
    for gc in range(2 * NCH):
        for h in range(2):
            g.y_sh[(gc, h)] = nc.dram_tensor(f"y_sh_{gc}_{h}", [128, 8 * CH], BF16,
                                             kind="Internal", addr_space="Shared")
            g.y_sn[(gc, h)] = nc.dram_tensor(f"y_sn_{gc}_{h}", [128, 8], I32,
                                             kind="Internal", addr_space="Shared")

    for s in ("sLD", "sLG", "sLS", "sTE", "sVE", "sSE", "sST", "sBAR", "sGP",
              "sDU0", "sDU1", "sGT"):
        setattr(g, s, Sem(nc, s))
    g.sSP = nc.alloc_semaphore("sSP")

    A = nc.alloc_sbuf_tensor
    g.c_ident = A("c_ident", [128, 128], BF16)
    g.c_bq = A("c_bq", [128, DT], F32)
    g.c_bk = A("c_bk", [128, DT], F32)
    g.c_bvrow = A("c_bvrow", [1, DM], BF16)
    g.c_b1 = A("c_b1", [128, FT], F32)
    g.c_b2row = A("c_b2row", [1, DM], BF16)
    g.c_binxi = A("c_binxi", [128, DIT], F32)
    g.c_binres = A("c_binres", [128, DIT], F32)
    g.c_wc = A("c_wc", [128, DIT * 4], F32)
    g.c_bcv = A("c_bcv", [128, DIT], F32)
    g.c_bxd = A("c_bxd", [128, DIT], F32)
    g.c_bxbc = A("c_bxbc", [32, 1], F32)
    g.c_borow = A("c_borow", [1, DM], BF16)
    g.c_A = A("c_A", [128, DIT * NST], F32)
    g.c_D = A("c_D", [128, DIT], F32)
    g.c_rolef = A("c_rolef", [128, 1], F32)
    g.c_rolei = A("c_rolei", [1, 1], I32)
    g.c_noni = A("c_noni", [1, 1], I32)
    g.c_nonrow = A("c_nonrow", [128, 8], I32)
    g.c_ones = A("c_ones", [1, 1024], BF16)
    g.c_ones2 = A("c_ones2", [128, CH], BF16)
    g.c_onescol = A("c_onescol", [128, 1], F32)
    g.c_onesrow = A("c_onesrow", [1, 128], F32)
    g.c_probe = A("c_probe", [128, 8], I32)
    g.c_stats = A("c_stats", [128, 12], F32)
    g.c_mv = A("c_mv", [128, 2], F32)
    g.c_mvs = A("c_mvs", [128, 2 * 12], F32)
    g.c_sqs = A("c_sqs", [128, 12], F32)
    g.c_rsds = A("c_rsds", [128, 12], F32)
    g.ln_ev = [0]
    g.c_rstd = A("c_rstd", [128, 1], F32)
    g.c_sq = A("c_sq", [128, 1], F32)
    g.c_thr1 = A("c_thr1", [128, 2], F32)
    g.c_thr2 = A("c_thr2", [128, 2], F32)
    g.c_racc = A("c_racc", [128, 1], F32)
    g.c_hprev = A("c_hprev", [128, 8 * NST], F32)
    g.c_krow = A("c_krow", [128, 8], F32)
    for i, vv in enumerate([1e-5, 1.0 / 32.0, 1.0, 0.5, 2.0, 0.25]):
        nc.const_aps.register(F32, vv, g.c_krow[:, i:i + 1])

    emit(g)

    return nc, g


def barrier(g, blk):
    blk.flush_all()
    dma_waits = [(g.sLD, g.sLD.n), (g.sLG, g.sLG.n), (g.sLS, g.sLS.n),
                 (g.sST, g.sST.n), (g.sDU0, g.sDU0.n), (g.sDU1, g.sDU1.n),
                 (g.sGT, g.sGT.n)]
    target = g.sBAR.n + 5
    for name in ("sync", "vector", "scalar", "tensor", "gpsimd"):
        @getattr(blk, name)
        def _(eng, _t=target):
            if name == "sync":
                for s_, n_ in dma_waits:
                    eng.wait_ge(s_.h, n_)
            eng.nop().then_inc(g.sBAR.h, 1)
            eng.wait_ge(g.sBAR.h, _t)
    g.sBAR.n = target



def ln_group(g, blk, tiles, xntm, ps_tr):
    n = len(tiles)
    for sl, (src_fn, _d) in enumerate(tiles):
        @blk.vector
        def _(v, src_fn=src_fn, sl=sl):
            src = src_fn()
            v.bn_stats(g.c_stats[:, 0:6], src[:, 0:512])
            v.bn_stats(g.c_stats[:, 6:12], src[:, 512:1024])
            v.drain()
            v.bn_aggr(g.c_mvs[:, 2 * sl:2 * sl + 2],
                      g.c_stats[:, 0:12]).then_inc(g.sVE.h, 1)
        g.sVE.n += 1
    m1 = g.sVE.n
    @blk.scalar
    def _(s):
        s.wait_ge(g.sVE.h, m1)
        s.activation(g.c_sqs[:, 0:n],
                     g.c_mvs[:, 1:2 * n].rearrange("p (a b) -> p a b", b=2)[:, :, 0:1]
                     if False else g.c_mvs[:].rearrange("p (a b) -> p a b", b=2)[:, 0:n, 1:2],
                     AF.Sqrt, bias=EPS, scale=1.0).then_inc(g.sSE.h, 1)
    g.sSE.n += 1
    m2 = g.sSE.n
    @blk.vector
    def _(v):
        v.wait_ge(g.sSE.h, m2)
        v.reciprocal(g.c_rsds[:, 0:n], g.c_sqs[:, 0:n]).then_inc(g.sVE.h, 1)
    g.sVE.n += 1
    rdone = g.sVE.n
    prev_tr = [0]
    for sl, (src_fn, dst_fn) in enumerate(tiles):
        @blk.vector
        def _(v, src_fn=src_fn, sl=sl, ptr=prev_tr[0]):
            src = src_fn()
            v.wait_ge(g.sVE.h, rdone)
            if ptr:
                v.wait_ge(g.sTE.h, ptr)
            v.drain()
            v.tensor_scalar(xntm[:], src, g.c_mvs[:, 2 * sl:2 * sl + 1],
                            g.c_rsds[:, sl:sl + 1],
                            op0=AluOpType.subtract, op1=AluOpType.mult).then_inc(g.sVE.h, 1)
        g.sVE.n += 1
        m3 = g.sVE.n
        @blk.tensor
        def _(t, prev_ev=g.ln_ev[0]):
            t.wait_ge(g.sVE.h, m3)
            if prev_ev:
                t.wait_ge(g.sSE.h, prev_ev)
            for dc in range(DT):
                inst = t.transpose(ps_tr[:, dc * 128:(dc + 1) * 128],
                                   xntm[:, dc * 128:(dc + 1) * 128], g.c_ident[:])
            inst.then_inc(g.sTE.h, 1)
        g.sTE.n += 1
        m4 = g.sTE.n
        prev_tr[0] = m4
        @blk.scalar
        def _(s, dst_fn=dst_fn):
            s.wait_ge(g.sTE.h, m4)
            for dc in range(DT):
                inst = s.activation(dst_fn(dc), ps_tr[:, dc * 128:(dc + 1) * 128],
                                    AF.Copy, bias=0.0, scale=1.0)
            inst.then_inc(g.sSE.h, 1)
        g.sSE.n += 1
        g.ln_ev[0] = g.sSE.n



def emit_p0(g, blk):
    din = g.din
    @blk.sync
    def _(sync):
        for name, dst in (
            ("ident", g.c_ident), ("bq", g.c_bq), ("bk", g.c_bk),
            ("bv_row", g.c_bvrow), ("b1", g.c_b1), ("b2_row", g.c_b2row),
            ("b_in_xi", g.c_binxi), ("b_in_res", g.c_binres),
            ("wc", g.c_wc), ("bcv", g.c_bcv),
            ("b_x_d", g.c_bxd), ("b_x_bc", g.c_bxbc), ("bo_row", g.c_borow),
            ("alog", g.c_A), ("dpar", g.c_D), ("role_f", g.c_rolef),
            ("role", g.c_rolei), ("nonce_i", g.c_noni), ("nonce_row", g.c_nonrow),
            ("krow", g.c_krow),
        ):
            g.sLD.inc(sync.dma_start(dst[:], din[name].ap()), 16)
    ld0 = g.sLD.n
    @blk.gpsimd
    def _(gp):
        gp.memset(g.c_ones[:], 1.0)
        gp.memset(g.c_onescol[:], 1.0)
        gp.memset(g.c_onesrow[:], 1.0)
        gp.memset(g.c_ones2[:], 1.0)
        gp.memset(g.c_hprev[:], 0.0).then_inc(g.sGP.h, 1)
    g.sGP.n += 1
    @blk.scalar
    def _(s):
        s.wait_ge(g.sLD.h, ld0)
        s.activation(g.c_A[:], g.c_A[:], AF.Exp, bias=0.0, scale=1.0).then_inc(g.sSE.h, 1)
    g.sSE.n += 1
    mA = g.sSE.n
    @blk.vector
    def _(v):
        v.wait_ge(g.sSE.h, mA)
        v.tensor_scalar(g.c_A[:], g.c_A[:], -1.0, None, op0=AluOpType.mult)
        v.drain()
        v.nop().then_inc(g.sVE.h, 1)
    g.sVE.n += 1



def emit_p1(g, blk):
    nc = g.nc
    din = g.din
    with ExitStack() as _es:
        sb_xwin = _es.enter_context(nc.sbuf_tensor("sb_xwin", [128, 8 * DM], F32))
        sb_xntm = _es.enter_context(nc.sbuf_tensor("sb_xntm", [128, DM], BF16))
        sb_xnt = _es.enter_context(nc.sbuf_tensor("sb_xnt", [128, DT * 1024], BF16))
        sb_wq = _es.enter_context(nc.sbuf_tensor("sb_wq", [128, DT * DM], BF16))
        sb_wk = _es.enter_context(nc.sbuf_tensor("sb_wk", [128, DT * DM], BF16))
        sb_wv = _es.enter_context(nc.sbuf_tensor("sb_wv", [128, DT * DM], BF16))
        sb_qT = _es.enter_context(nc.sbuf_tensor("sb_qT", [128, DT * CH], BF16))
        sb_kT = _es.enter_context(nc.sbuf_tensor("sb_kT", [128, DT * 1024], BF16))
        sb_v = _es.enter_context(nc.sbuf_tensor("sb_v", [128, 8 * DM], BF16))
        sb_mask = _es.enter_context(nc.sbuf_tensor("sb_mask", [128, 4 * 768], BF16))
        sb_exp = _es.enter_context(nc.sbuf_tensor("sb_exp", [128, 768], BF16))
        sb_rsum = _es.enter_context(nc.sbuf_tensor("sb_rsum", [128, 1], F32))
        sb_rrec = _es.enter_context(nc.sbuf_tensor("sb_rrec", [128, 1], F32))
        sb_wT = _es.enter_context(nc.sbuf_tensor("sb_wT", [128, 768], BF16))
        sb_x1c = _es.enter_context(nc.sbuf_tensor("sb_x1c", [128, 4 * DM], F32))
        ps_tr = _es.enter_context(nc.psum_tensor("ps_tr", [128, 1024], BF16))
        ps_a = _es.enter_context(nc.psum_tensor("ps_a", [128, 1024], F32))
        ps_b = _es.enter_context(nc.psum_tensor("ps_b", [128, 1024], F32))
        ps_c = _es.enter_context(nc.psum_tensor("ps_c", [128, 1024], F32))
        ps_wt = _es.enter_context(nc.psum_tensor("ps_wt", [128, 768], BF16))

        @blk.gpsimd
        def _(gp):
            for name, dst in (("wq", sb_wq), ("wk", sb_wk), ("wv", sb_wv)):
                for kd in range(DT):
                    g.sLG.inc(gp.dma_start(dst[:, kd * DM:(kd + 1) * DM],
                                           din[name].ap()[kd * 128:(kd + 1) * 128, :]), 16)
        w_loaded = g.sLG.n

        for ci, (t0, ntok) in enumerate(CHUNKS):
            winn = ntok + 512
            wtiles = winn // 128
            nqb = ntok // 256
            row0 = t0 - 256 + 512
            ve_prev = g.sVE.n
            te_prev = g.sTE.n
            st_prev = g.sST.n
            @blk.sync
            def _(sync, row0=row0, wtiles=wtiles, ci=ci, nqb=nqb):
                sync.wait_ge(g.sVE.h, ve_prev)
                sync.wait_ge(g.sTE.h, te_prev)
                for r in range(wtiles):
                    g.sLD.inc(sync.dma_start(
                        sb_xwin[:, r * DM:(r + 1) * DM],
                        din["x_in"].ap()[row0 + r * 128: row0 + (r + 1) * 128, :]), 16)
                qb0 = 0 if ci == 0 else 1 + (ci - 1) * 2
                for qb in range(nqb):
                    for qt in range(2):
                        sc = ((qb0 + qb) * 2 + qt) * 768
                        g.sLD.inc(sync.dma_start(
                            sb_mask[:, (qb * 2 + qt) * 768:(qb * 2 + qt + 1) * 768],
                            din["masks"].ap()[:, sc:sc + 768]), 16)
            xm_loaded = g.sLD.n

            @blk.vector
            def _(v):
                v.wait_ge(g.sLD.h, xm_loaded)
                v.nop().then_inc(g.sVE.h, 1)
            g.sVE.n += 1

            ln_group(g, blk,
                     [((lambda r=r: sb_xwin[:, r * DM:(r + 1) * DM]),
                       (lambda dc, r=r: sb_xnt[:, dc * 1024 + r * 128: dc * 1024 + r * 128 + 128]))
                      for r in range(wtiles)],
                     sb_xntm, ps_tr)
            ln_done = g.sSE.n

            q_evacs = []
            for m in range(DT):
                @blk.tensor
                def _(t, m=m, ntok=ntok):
                    if m == 0:
                        t.wait_ge(g.sSE.h, ln_done)
                        t.wait_ge(g.sLG.h, w_loaded)
                        t.wait_ge(g.sGP.h, 1)
                    if m >= 2:
                        t.wait_ge(g.sSE.h, q_evacs[m - 2])
                    p0 = (m % 2) * 512
                    for kd in range(DT):
                        inst = t.matmul(ps_a[:, p0:p0 + ntok],
                                 sb_wq[:, kd * DM + m * 128: kd * DM + m * 128 + 128],
                                 sb_xnt[:, kd * 1024 + 256: kd * 1024 + 256 + ntok],
                                 start=(kd == 0), stop=(kd == DT - 1))
                    inst.then_inc(g.sTE.h, 1)
                g.sTE.n += 1
                mm = g.sTE.n
                @blk.scalar
                def _(s, m=m, ntok=ntok):
                    s.wait_ge(g.sTE.h, mm)
                    p0 = (m % 2) * 512
                    s.activation(sb_qT[:, m * CH: m * CH + ntok], ps_a[:, p0:p0 + ntok],
                                 AF.Identity, bias=g.c_bq[:, m:m + 1], scale=1.0).then_inc(g.sSE.h, 1)
                g.sSE.n += 1
                q_evacs.append(g.sSE.n)

            k_evacs = []
            for m in range(DT):
                @blk.tensor
                def _(t, m=m, winn=winn):
                    if m == 0:
                        t.wait_ge(g.sSE.h, q_evacs[-1])
                    if m >= 2:
                        t.wait_ge(g.sSE.h, k_evacs[m - 2])
                    ps = ps_b if (m % 2 == 0) else ps_a
                    for f0 in range(0, winn, 512):
                        fw = min(512, winn - f0)
                        for kd in range(DT):
                            inst = t.matmul(ps[:, f0:f0 + fw],
                                     sb_wk[:, kd * DM + m * 128: kd * DM + m * 128 + 128],
                                     sb_xnt[:, kd * 1024 + f0: kd * 1024 + f0 + fw],
                                     start=(kd == 0), stop=(kd == DT - 1))
                    inst.then_inc(g.sTE.h, 1)
                g.sTE.n += 1
                mm = g.sTE.n
                @blk.scalar
                def _(s, m=m, winn=winn):
                    s.wait_ge(g.sTE.h, mm)
                    ps = ps_b if (m % 2 == 0) else ps_a
                    s.activation(sb_kT[:, m * 1024: m * 1024 + winn], ps[:, 0:winn],
                                 AF.Identity, bias=g.c_bk[:, m:m + 1], scale=1.0).then_inc(g.sSE.h, 1)
                g.sSE.n += 1
                k_evacs.append(g.sSE.n)

            v_evacs = []
            for r in range(wtiles):
                @blk.tensor
                def _(t, r=r):
                    if r == 0:
                        t.wait_ge(g.sSE.h, k_evacs[-1])
                    if r >= 2:
                        t.wait_ge(g.sSE.h, v_evacs[r - 2])
                    ps = ps_c if (r % 2 == 0) else ps_b
                    for hf in range(2):
                        for kd in range(DT):
                            t.matmul(ps[:, hf * 512:(hf + 1) * 512],
                                     sb_xnt[:, kd * 1024 + r * 128: kd * 1024 + r * 128 + 128],
                                     sb_wv[:, kd * DM + hf * 512: kd * DM + (hf + 1) * 512],
                                     start=(kd == 0), stop=False)
                        inst = t.matmul(ps[:, hf * 512:(hf + 1) * 512],
                                 g.c_ones[0:1, 0:128],
                                 g.c_bvrow[0:1, hf * 512:(hf + 1) * 512],
                                 start=False, stop=True)
                    inst.then_inc(g.sTE.h, 1)
                g.sTE.n += 1
                mm = g.sTE.n
                @blk.scalar
                def _(s, r=r):
                    s.wait_ge(g.sTE.h, mm)
                    ps = ps_c if (r % 2 == 0) else ps_b
                    s.activation(sb_v[:, r * DM:(r + 1) * DM], ps[:, 0:DM],
                                 AF.Copy, bias=0.0, scale=1.0).then_inc(g.sSE.h, 1)
                g.sSE.n += 1
                v_evacs.append(g.sSE.n)

            prev_exp = None
            prev_wts = None
            prev_wts_v = 0
            prev_x1 = None
            prev_wdone = [0]
            for qb in range(nqb):
                for qt in range(2):
                    @blk.tensor
                    def _(t, qb=qb, qt=qt, pe=prev_exp):
                        t.wait_ge(g.sSE.h, v_evacs[-1])
                        if pe is not None:
                            t.wait_ge(g.sSE.h, pe)
                        for f0, fw in ((0, 512), (512, 256)):
                            for kd in range(DT):
                                t.matmul(ps_a[:, f0:f0 + fw],
                                         sb_qT[:, kd * CH + qb * 256 + qt * 128:
                                               kd * CH + qb * 256 + qt * 128 + 128],
                                         sb_kT[:, kd * 1024 + qb * 256 + f0:
                                               kd * 1024 + qb * 256 + f0 + fw],
                                         start=(kd == 0), stop=False)
                            inst = t.matmul(ps_a[:, f0:f0 + fw], g.c_ident[:],
                                     sb_mask[:, (qb * 2 + qt) * 768 + f0:
                                             (qb * 2 + qt) * 768 + f0 + fw],
                                     start=False, stop=True)
                        inst.then_inc(g.sTE.h, 1)
                    g.sTE.n += 1
                    sc_done = g.sTE.n
                    pw2 = prev_wdone[0]
                    @blk.scalar
                    def _(s, pw2=pw2):
                        s.wait_ge(g.sTE.h, sc_done)
                        if pw2:
                            s.wait_ge(g.sVE.h, pw2)
                        s.activation(sb_exp[:], ps_a[:, 0:768], AF.Exp, bias=0.0,
                                     scale=1.0 / 32.0, accum_out=sb_rsum[:]).then_inc(g.sSE.h, 1)
                    g.sSE.n += 1
                    prev_exp = g.sSE.n
                    @blk.vector
                    def _(v):
                        v.wait_ge(g.sSE.h, prev_exp)
                        v.reciprocal(sb_rrec[:], sb_rsum[:])
                        v.drain()
                        v.tensor_scalar(sb_exp[:], sb_exp[:], sb_rrec[:], None,
                                        op0=AluOpType.mult).then_inc(g.sVE.h, 1)
                    g.sVE.n += 1
                    w_done = g.sVE.n
                    prev_wdone[0] = w_done
                    @blk.tensor
                    def _(t, pw=prev_wts_v):
                        t.wait_ge(g.sVE.h, w_done)
                        if pw:
                            t.wait_ge(g.sVE.h, pw)
                        for j in range(6):
                            inst = t.transpose(ps_wt[:, j * 128:(j + 1) * 128],
                                        sb_exp[:, j * 128:(j + 1) * 128], g.c_ident[:])
                        inst.then_inc(g.sTE.h, 1)
                    g.sTE.n += 1
                    wt_done = g.sTE.n
                    @blk.vector
                    def _(v):
                        v.wait_ge(g.sTE.h, wt_done)
                        v.tensor_copy(sb_wT[:], ps_wt[:, 0:768]).then_inc(g.sVE.h, 1)
                    g.sVE.n += 1
                    prev_wts_v = g.sVE.n
                    wts_done = g.sVE.n
                    @blk.tensor
                    def _(t, qb=qb, px=prev_x1):
                        t.wait_ge(g.sVE.h, wts_done)
                        if px is not None:
                            t.wait_ge(g.sVE.h, px)
                        for hf in range(2):
                            for j in range(6):
                                inst = t.matmul(ps_c[:, hf * 512:(hf + 1) * 512],
                                         sb_wT[:, j * 128:(j + 1) * 128],
                                         sb_v[:, (qb * 2 + j) * DM + hf * 512:
                                               (qb * 2 + j) * DM + (hf + 1) * 512],
                                         start=(j == 0), stop=(j == 5))
                        inst.then_inc(g.sTE.h, 1)
                    g.sTE.n += 1
                    av_done = g.sTE.n
                    @blk.vector
                    def _(v, qb=qb, qt=qt, st_prev=st_prev):
                        v.wait_ge(g.sTE.h, av_done)
                        if qb == 0 and qt == 0:
                            v.wait_ge(g.sST.h, st_prev)
                        rt = 2 + qb * 2 + qt
                        dstc = (qb * 2 + qt) * DM
                        v.tensor_tensor(sb_x1c[:, dstc:dstc + DM],
                                        sb_xwin[:, rt * DM:(rt + 1) * DM],
                                        ps_c[:, 0:DM], op=AluOpType.add).then_inc(g.sVE.h, 1)
                    g.sVE.n += 1
                    prev_x1 = g.sVE.n
            x1_done = g.sVE.n
            @blk.sync
            def _(sync, t0=t0, ntok=ntok):
                sync.wait_ge(g.sVE.h, x1_done)
                r0 = t0 + 256
                for rt in range(ntok // 128):
                    g.sST.inc(sync.dma_start(
                        g.x1_h.ap()[r0 + rt * 128: r0 + (rt + 1) * 128, :],
                        sb_x1c[:, rt * DM:(rt + 1) * DM]), 16)
                if DEBUG:
                    for rt in range(ntok // 128):
                        g.sST.inc(sync.dma_start(
                            g.dbg_x1.ap()[r0 + rt * 128: r0 + (rt + 1) * 128, :],
                            sb_x1c[:, rt * DM:(rt + 1) * DM]), 16)
            blk.flush_all()
    g.ms_x1_stored = g.sST.n



SPIN_CELLS = [0, 12, 28, 40, 64, 76, 88, 92]


def emit_spin(g, blk, sent_dram):
    @blk.dsync
    def _(sync):
        e = g.rv_nonce
        pr = g.r_probe
        sync.sem_clear(g.sSP)
        sync.dma_start(g.c_probe[:], sent_dram.ap()).then_inc(g.sSP, 16)
        sync.wait_ge(g.sSP, 16)
        nread = [1]
        for ci, row in enumerate(SPIN_CELLS):
            K = 10 if ci == 0 else 1
            sync.reg_load(pr, g.c_probe[row:row + 1, 0:1])
            for i in range(K):
                with sync.If_ne(pr, e):
                    sync.nop(cycle_cnt=8000)
                    sync.dma_start(g.c_probe[:], sent_dram.ap()).then_inc(g.sSP, 16)
                    nread[0] += 1
                    sync.wait_ge(g.sSP, 16 * nread[0])
                    sync.reg_load(pr, g.c_probe[row:row + 1, 0:1])
                sync.end_ifs()



def emit_ternarize(g, blk, w_dram, ntiles, width, thr_dst):
    uid = g.uid = getattr(g, "uid", 0) + 1
    """Pass A: thr = 0.5*max(mean|W|, 1e-8) into thr_dst ([128,2]: +thr, -thr).
    w_dram streamed as ntiles fp32 tiles of [128, width]."""
    nc = g.nc
    with ExitStack() as _es:
        sb_w = _es.enter_context(nc.sbuf_tensor(f"sb_tw{uid}", [128, width], F32))
        sb_p = _es.enter_context(nc.sbuf_tensor(f"sb_tp{uid}", [128, width // 512], F32))
        ps_t = _es.enter_context(nc.psum_tensor(f"ps_tern{uid}", [128, 2], F32))
        te_now = g.sTE.n
        ve_entry = g.sVE.n
        @blk.gpsimd
        def _(gp):
            gp.wait_ge(g.sTE.h, te_now)
            gp.memset(g.c_racc[:], 0.0).then_inc(g.sGP.h, 1)
        g.sGP.n += 1
        m0 = g.sGP.n
        loads = []
        for i in range(ntiles):
            @blk.sync
            def _(sync, i=i):
                if i >= 1:
                    sync.wait_ge(g.sVE.h, loads[i - 1])
                else:
                    sync.wait_ge(g.sVE.h, ve_entry)
                g.sLD.inc(sync.dma_start(
                    sb_w[:, 0:width],
                    w_dram.ap()[i * 128:(i + 1) * 128, :]), 16)
            ld = g.sLD.n
            @blk.vector
            def _(v, i=i):
                v.wait_ge(g.sLD.h, ld)
                if i == 0:
                    v.wait_ge(g.sGP.h, m0)
                src3 = sb_w[:, 0:width].rearrange("p (a b) -> p a b", b=512)
                v.tensor_reduce(sb_p[:], src3, axis=mybir.AxisListType.X,
                                op=AluOpType.add, apply_absolute_value=True)
                v.drain()
                v.tensor_reduce(g.c_sq[:], sb_p[:], axis=mybir.AxisListType.X,
                                op=AluOpType.add)
                v.drain()
                v.tensor_tensor(g.c_racc[:], g.c_racc[:], g.c_sq[:],
                                op=AluOpType.add).then_inc(g.sVE.h, 1)
            g.sVE.n += 1
            loads.append(g.sVE.n)
        @blk.tensor
        def _(t):
            t.wait_ge(g.sVE.h, loads[-1])
            t.matmul(ps_t[0:1, 0:1], g.c_onescol[:], g.c_racc[:],
                     start=True, stop=True).then_inc(g.sTE.h, 1)
        g.sTE.n += 1
        m1 = g.sTE.n
        @blk.scalar
        def _(s):
            s.wait_ge(g.sTE.h, m1)
            s.activation(g.c_sq[0:1, 0:1], ps_t[0:1, 0:1], AF.Copy, bias=0.0,
                         scale=float(1.0 / (ntiles * 128 * width))).then_inc(g.sSE.h, 1)
        g.sSE.n += 1
        m2 = g.sSE.n
        @blk.vector
        def _(v):
            v.wait_ge(g.sSE.h, m2)
            v.tensor_scalar(g.c_sq[0:1, 0:1], g.c_sq[0:1, 0:1], 1e-8, 0.5,
                            op0=AluOpType.max, op1=AluOpType.mult).then_inc(g.sVE.h, 1)
        g.sVE.n += 1
        m3 = g.sVE.n
        @blk.tensor
        def _(t):
            t.wait_ge(g.sVE.h, m3)
            t.matmul(ps_t[:, 1:2], g.c_onesrow[0:1, :], g.c_sq[0:1, 0:1],
                     start=True, stop=True).then_inc(g.sTE.h, 1)
        g.sTE.n += 1
        m4 = g.sTE.n
        @blk.scalar
        def _(s):
            s.wait_ge(g.sTE.h, m4)
            s.activation(thr_dst[:, 0:1], ps_t[:, 1:2], AF.Copy, bias=0.0, scale=1.0)
            s.activation(thr_dst[:, 1:2], ps_t[:, 1:2], AF.Copy, bias=0.0,
                         scale=-1.0).then_inc(g.sSE.h, 1)
        g.sSE.n += 1
        blk.flush_all()


def emit_tern_apply(g, blk, w_dram, ntiles, width, thr, dst_sb, dst_h):
    nc = g.nc
    uid = g.uid = getattr(g, "uid", 0) + 1
    ve_entry2 = g.sVE.n
    with ExitStack() as _es:
        sb_w = _es.enter_context(nc.sbuf_tensor(f"sb_tw2_{uid}", [128, width], F32))
        sb_pp = _es.enter_context(nc.sbuf_tensor(f"sb_tpp{uid}", [128, width], F32))
        thr_done = g.sSE.n
        dones = []
        for i in range(ntiles):
            @blk.sync
            def _(sync, i=i):
                if i >= 1:
                    sync.wait_ge(g.sVE.h, dones[i - 1])
                else:
                    sync.wait_ge(g.sVE.h, ve_entry2)
                g.sLD.inc(sync.dma_start(
                    sb_w[:, 0:width],
                    w_dram.ap()[i * 128:(i + 1) * 128, :]), 16)
            ld = g.sLD.n
            @blk.vector
            def _(v, i=i):
                v.wait_ge(g.sLD.h, ld)
                if i == 0:
                    v.wait_ge(g.sSE.h, thr_done)
                srcw = sb_w[:, 0:width]
                v.tensor_scalar(sb_pp[:], srcw, thr[:, 0:1], None, op0=AluOpType.is_ge)
                v.tensor_scalar(dst_sb[:, i * width:(i + 1) * width], srcw,
                                thr[:, 1:2], None, op0=AluOpType.is_le)
                v.drain()
                v.tensor_tensor(dst_sb[:, i * width:(i + 1) * width], sb_pp[:],
                                dst_sb[:, i * width:(i + 1) * width],
                                op=AluOpType.subtract).then_inc(g.sVE.h, 1)
            g.sVE.n += 1
            dones.append(g.sVE.n)
            @blk.sync
            def _(sync, i=i):
                sync.wait_ge(g.sVE.h, dones[-1])
                g.sST.inc(sync.dma_start(dst_h.ap()[i * 128:(i + 1) * 128, :],
                                         dst_sb[:, i * width:(i + 1) * width]), 16)



def emit_ffn(g, blk, first):
    nc = g.nc
    din = g.din
    src_h = g.x1_h if first else g.x3_h
    with ExitStack() as _es:
        sb_w1 = _es.enter_context(nc.sbuf_tensor(f"sb_w1t{first}", [128, DT * FF], BF16))
        sb_w2 = _es.enter_context(nc.sbuf_tensor(f"sb_w2t{first}", [128, FT * DM], BF16))

        if first:
            emit_ternarize(g, blk, din["w1T"], DT, FF, g.c_thr1)
            emit_ternarize(g, blk, din["w2T"], FT, DM, g.c_thr2)
            emit_tern_apply(g, blk, din["w1T"], DT, FF, g.c_thr1, sb_w1, g.w1t_h)
            emit_tern_apply(g, blk, din["w2T"], FT, DM, g.c_thr2, sb_w2, g.w2t_h)
        else:
            @blk.sync
            def _(sync):
                for i in range(DT):
                    g.sLD.inc(sync.dma_start(sb_w1[:, i * FF:(i + 1) * FF],
                                             g.w1t_h.ap()[i * 128:(i + 1) * 128, :]), 16)
                for i in range(FT):
                    g.sLD.inc(sync.dma_start(sb_w2[:, i * DM:(i + 1) * DM],
                                             g.w2t_h.ap()[i * 128:(i + 1) * 128, :]), 16)
        w_ready_v = g.sVE.n
        w_ready_l = g.sLD.n
        w_ready_s = g.sST.n
        sb_xc = _es.enter_context(nc.sbuf_tensor(f"sb_fxc{first}", [128, 4 * DM], F32))
        sb_xntm = _es.enter_context(nc.sbuf_tensor(f"sb_fxntm{first}", [128, DM], BF16))
        sb_xnt = _es.enter_context(nc.sbuf_tensor(f"sb_fxnt{first}", [128, DT * CH], BF16))
        sb_hT = _es.enter_context(nc.sbuf_tensor(f"sb_fhT{first}", [128, FT * 256], BF16))
        sb_o = _es.enter_context(nc.sbuf_tensor(f"sb_fo{first}", [128, 4 * DM], F32))
        ps_tr = _es.enter_context(nc.psum_tensor(f"ps_ftr{first}", [128, 1024], BF16))
        ps_h = _es.enter_context(nc.psum_tensor(f"ps_fh{first}", [128, 1024], F32))
        ps_o = _es.enter_context(nc.psum_tensor(f"ps_fo{first}", [128, 2048], F32))

        chunk_list = CHUNKS if first else CHUNKS[1:]
        prev_resid = 0
        prev_store = g.sST.n
        prev_te = g.sTE.n
        for ci, (t0, ntok) in enumerate(chunk_list):
            rtiles = ntok // 128
            r0 = t0 + 256 if first else t0
            ve_prev = g.sVE.n
            te_prev = g.sTE.n
            @blk.sync
            def _(sync, r0=r0, rtiles=rtiles):
                sync.wait_ge(g.sVE.h, ve_prev)
                sync.wait_ge(g.sTE.h, te_prev)
                if first:
                    sync.wait_ge(g.sST.h, g.ms_x1_stored)
                for r in range(rtiles):
                    g.sLD.inc(sync.dma_start(
                        sb_xc[:, r * DM:(r + 1) * DM],
                        src_h.ap()[r0 + r * 128: r0 + (r + 1) * 128, :]), 16)
            xl = g.sLD.n
            @blk.vector
            def _(v):
                v.wait_ge(g.sLD.h, xl)
                v.nop().then_inc(g.sVE.h, 1)
            g.sVE.n += 1
            ln_group(g, blk,
                     [((lambda r=r: sb_xc[:, r * DM:(r + 1) * DM]),
                       (lambda dc, r=r: sb_xnt[:, dc * CH + r * 128: dc * CH + r * 128 + 128]))
                      for r in range(rtiles)],
                     sb_xntm, ps_tr)
            ln_done = g.sSE.n

            for sg in range(ntok // 256):
                h_evacs = []
                for mf in range(FT):
                    @blk.tensor
                    def _(t, mf=mf, sg=sg):
                        if mf == 0 and sg == 0:
                            t.wait_ge(g.sSE.h, ln_done)
                            t.wait_ge(g.sVE.h, w_ready_v)
                            t.wait_ge(g.sLD.h, w_ready_l)
                        if mf >= 2:
                            t.wait_ge(g.sSE.h, h_evacs[mf - 2])
                        p0 = (mf % 2) * 512
                        for kd in range(DT):
                            inst = t.matmul(ps_h[:, p0:p0 + 256],
                                            sb_w1[:, kd * FF + mf * 128: kd * FF + mf * 128 + 128],
                                            sb_xnt[:, kd * CH + sg * 256: kd * CH + sg * 256 + 256],
                                            start=(kd == 0), stop=(kd == DT - 1))
                        inst.then_inc(g.sTE.h, 1)
                    g.sTE.n += 1
                    mm = g.sTE.n
                    @blk.scalar
                    def _(s, mf=mf):
                        s.wait_ge(g.sTE.h, mm)
                        p0 = (mf % 2) * 512
                        s.activation(sb_hT[:, mf * 256:(mf + 1) * 256], ps_h[:, p0:p0 + 256],
                                     AF.Gelu, bias=g.c_b1[:, mf:mf + 1],
                                     scale=1.0).then_inc(g.sSE.h, 1)
                    g.sSE.n += 1
                    h_evacs.append(g.sSE.n)
                for tt in range(2):
                    @blk.tensor
                    def _(t, tt=tt, sg=sg, pr=prev_resid):
                        t.wait_ge(g.sSE.h, h_evacs[-1])
                        if pr:
                            t.wait_ge(g.sVE.h, pr)
                        for fh in range(2):
                            for kf in range(FT):
                                t.matmul(ps_o[:, (tt * 2 + fh) * 512:(tt * 2 + fh) * 512 + 512],
                                         sb_hT[:, kf * 256 + tt * 128: kf * 256 + tt * 128 + 128],
                                         sb_w2[:, kf * DM + fh * 512: kf * DM + (fh + 1) * 512],
                                         start=(kf == 0), stop=False)
                            inst = t.matmul(ps_o[:, (tt * 2 + fh) * 512:(tt * 2 + fh) * 512 + 512],
                                            g.c_ones[0:1, 0:128],
                                            g.c_b2row[0:1, fh * 512:(fh + 1) * 512],
                                            start=False, stop=True)
                        inst.then_inc(g.sTE.h, 1)
                    g.sTE.n += 1
                    mm2 = g.sTE.n
                    @blk.vector
                    def _(v, tt=tt, sg=sg, ps0=prev_store):
                        v.wait_ge(g.sTE.h, mm2)
                        if sg == 0 and tt == 0:
                            v.wait_ge(g.sST.h, ps0)
                        rt = sg * 2 + tt
                        v.tensor_tensor(sb_o[:, rt * DM:(rt + 1) * DM],
                                        sb_xc[:, rt * DM:(rt + 1) * DM],
                                        ps_o[:, tt * 1024: tt * 1024 + 1024],
                                        op=AluOpType.add).then_inc(g.sVE.h, 1)
                    g.sVE.n += 1
                    prev_resid = g.sVE.n
            done_v = g.sVE.n
            @blk.sync
            def _(sync, t0=t0, rtiles=rtiles, ci=ci):
                sync.wait_ge(g.sVE.h, done_v)
                for r in range(rtiles):
                    if first:
                        g.sST.inc(sync.dma_start(
                            g.x2_h.ap()[t0 + 256 + r * 128: t0 + 256 + (r + 1) * 128, :],
                            sb_o[:, r * DM:(r + 1) * DM]), 16)
                        if DEBUG:
                            g.sST.inc(sync.dma_start(
                                g.dbg_x2.ap()[t0 + 256 + r * 128: t0 + 256 + (r + 1) * 128, :],
                                sb_o[:, r * DM:(r + 1) * DM]), 16)
                    else:
                        g.sST.inc(sync.dma_start(
                            g.out.ap()[t0 + r * 128: t0 + (r + 1) * 128, :],
                            sb_o[:, r * DM:(r + 1) * DM]), 16)
            prev_store = g.sST.n
            blk.flush_all()
            blk.flush_all()
    if first:
        g.ms_x2_stored = g.sST.n



def emit_p3a(g, blk):
    nc = g.nc
    din = g.din
    with ExitStack() as _es:
        sb_win = _es.enter_context(nc.sbuf_tensor("sb_win", [128, DT * 2 * DI], BF16))
        sb_xc = _es.enter_context(nc.sbuf_tensor("sb_3xc", [128, 4 * DM], F32))
        sb_xntm = _es.enter_context(nc.sbuf_tensor("sb_3xntm", [128, DM], BF16))
        sb_xnt = _es.enter_context(nc.sbuf_tensor("sb_3xnt", [128, DT * CH], BF16))
        sb_xrh = _es.enter_context(nc.sbuf_tensor("sb_xrh", [128, DIT * (CH + 3)], BF16))
        sb_res = _es.enter_context(nc.sbuf_tensor("sb_3res", [128, 2 * CH], BF16))
        sb_cacc = _es.enter_context(nc.sbuf_tensor("sb_cacc", [128, 2 * CH], F32))
        sb_xi = _es.enter_context(nc.sbuf_tensor("sb_3xi", [128, DIT * CH], BF16))
        ps_tr = _es.enter_context(nc.psum_tensor("ps_3tr", [128, 1024], BF16))
        ps_w = _es.enter_context(nc.psum_tensor("ps_3w", [128, 1024], F32))

        @blk.gpsimd
        def _(gp):
            for i in range(DT):
                g.sLG.inc(gp.dma_start(sb_win[:, i * 2 * DI:(i + 1) * 2 * DI],
                                       din["winT"].ap()[i * 128:(i + 1) * 128, :]), 16)
        win_l = g.sLG.n

        prev_ve = 0
        prev_chunk_st = [0, 0]
        for ci, (t0, ntok) in enumerate(CHUNKS):
            prev_chunk_st[1] = g.sST.n
            rtiles = ntok // 128
            r0 = t0 + 256
            ve_prev = g.sVE.n
            te_prev = g.sTE.n
            @blk.sync
            def _(sync, r0=r0, rtiles=rtiles):
                sync.wait_ge(g.sVE.h, ve_prev)
                sync.wait_ge(g.sTE.h, te_prev)
                sync.wait_ge(g.sST.h, g.ms_x2_stored)
                for r in range(rtiles):
                    g.sLD.inc(sync.dma_start(
                        sb_xc[:, r * DM:(r + 1) * DM],
                        g.x2_h.ap()[r0 + r * 128: r0 + (r + 1) * 128, :]), 16)
            xl = g.sLD.n
            @blk.vector
            def _(v):
                v.wait_ge(g.sLD.h, xl)
                v.nop().then_inc(g.sVE.h, 1)
            g.sVE.n += 1
            rlist = range(rtiles - 1, rtiles) if ci == 0 else range(rtiles)
            ln_group(g, blk,
                     [((lambda r=r: sb_xc[:, r * DM:(r + 1) * DM]),
                       (lambda dc, r=r: sb_xnt[:, dc * CH + r * 128: dc * CH + r * 128 + 128]))
                      for r in rlist],
                     sb_xntm, ps_tr)
            ln_done = g.sSE.n

            if ci == 0:
                evs = []
                for mf in range(DIT):
                    @blk.tensor
                    def _(t, mf=mf):
                        if mf == 0:
                            t.wait_ge(g.sSE.h, ln_done)
                            t.wait_ge(g.sLG.h, win_l)
                        if mf >= 2:
                            t.wait_ge(g.sSE.h, evs[mf - 2])
                        p0 = (mf % 2) * 512
                        for kd in range(DT):
                            inst = t.matmul(ps_w[:, p0:p0 + 128],
                                            sb_win[:, kd * 2 * DI + mf * 128: kd * 2 * DI + mf * 128 + 128],
                                            sb_xnt[:, kd * CH + 128: kd * CH + 256],
                                            start=(kd == 0), stop=(kd == DT - 1))
                        inst.then_inc(g.sTE.h, 1)
                    g.sTE.n += 1
                    mm = g.sTE.n
                    @blk.scalar
                    def _(s, mf=mf):
                        s.wait_ge(g.sTE.h, mm)
                        p0 = (mf % 2) * 512
                        s.activation(sb_xrh[:, mf * (CH + 3): mf * (CH + 3) + 3],
                                     ps_w[:, p0 + 125: p0 + 128], AF.Identity,
                                     bias=g.c_binxi[:, mf:mf + 1], scale=1.0).then_inc(g.sSE.h, 1)
                    g.sSE.n += 1
                    evs.append(g.sSE.n)
                halo_se = g.sSE.n
                @blk.vector
                def _(v):
                    v.wait_ge(g.sSE.h, halo_se)
                    for mf in range(DIT):
                        v.tensor_scalar(sb_xrh[:, mf * (CH + 3): mf * (CH + 3) + 3],
                                        sb_xrh[:, mf * (CH + 3): mf * (CH + 3) + 3],
                                        g.c_rolef[:, 0:1], None, op0=AluOpType.mult)
                    v.nop().then_inc(g.sVE.h, 1)
                g.sVE.n += 1
                prev_ve = g.sVE.n
                continue

            c = ci - 1
            evs = []
            res_stores = []
            prev_chunk_st[0] = prev_chunk_st[1]
            for mf in range(2 * DIT):
                @blk.tensor
                def _(t, mf=mf, pv=prev_ve):
                    if mf == 0:
                        t.wait_ge(g.sSE.h, ln_done)
                        t.wait_ge(g.sLG.h, win_l)
                        t.wait_ge(g.sVE.h, pv)
                    if mf >= 2:
                        t.wait_ge(g.sSE.h, evs[mf - 2])
                    p0 = (mf % 2) * 512
                    for kd in range(DT):
                        inst = t.matmul(ps_w[:, p0:p0 + CH],
                                        sb_win[:, kd * 2 * DI + mf * 128: kd * 2 * DI + mf * 128 + 128],
                                        sb_xnt[:, kd * CH: kd * CH + CH],
                                        start=(kd == 0), stop=(kd == DT - 1))
                    inst.then_inc(g.sTE.h, 1)
                g.sTE.n += 1
                mm = g.sTE.n
                if mf < DIT:
                    @blk.scalar
                    def _(s, mf=mf):
                        s.wait_ge(g.sTE.h, mm)
                        p0 = (mf % 2) * 512
                        s.activation(sb_xrh[:, mf * (CH + 3) + 3: mf * (CH + 3) + 3 + CH],
                                     ps_w[:, p0:p0 + CH], AF.Identity,
                                     bias=g.c_binxi[:, mf:mf + 1], scale=1.0).then_inc(g.sSE.h, 1)
                    g.sSE.n += 1
                else:
                    rb = (mf - DIT) % 2
                    @blk.scalar
                    def _(s, mf=mf, c=c, rb=rb):
                        s.wait_ge(g.sTE.h, mm)
                        if res_stores:
                            s.wait_ge(g.sST.h, res_stores[-1])
                        elif prev_chunk_st[0]:
                            s.wait_ge(g.sST.h, prev_chunk_st[0])
                        p0 = (mf % 2) * 512
                        s.activation(sb_res[:, rb * CH:(rb + 1) * CH], ps_w[:, p0:p0 + CH],
                                     AF.Identity,
                                     bias=g.c_binres[:, mf - DIT:mf - DIT + 1],
                                     scale=1.0).then_inc(g.sSE.h, 1)
                    g.sSE.n += 1
                    se_res = g.sSE.n
                    @blk.sync
                    def _(sync, mf=mf, c=c, se_res=se_res, rb=rb):
                        sync.wait_ge(g.sSE.h, se_res)
                        g.sST.inc(sync.dma_start(
                            g.res_h.ap()[(mf - DIT) * 128:(mf - DIT + 1) * 128,
                                         c * CH:(c + 1) * CH],
                            sb_res[:, rb * CH:(rb + 1) * CH]), 16)
                    res_stores.append(g.sST.n)
                evs.append(g.sSE.n)
            xr_done = g.sSE.n

            cv_dones = []
            for dt in range(DIT):
                @blk.vector
                def _(v, dt=dt):
                    if dt == 0:
                        v.wait_ge(g.sSE.h, xr_done)
                    else:
                        v.wait_ge(g.sSE.h, cv_dones[-1])
                    base = dt * (CH + 3)
                    v.tensor_scalar(sb_cacc[:, 0:CH], sb_xrh[:, base: base + CH],
                                    g.c_wc[:, dt * 4: dt * 4 + 1], None,
                                    op0=AluOpType.mult).then_inc(g.sVE.h, 1)
                    for k in range(1, 4):
                        v.wait_ge(g.sVE.h, g.sVE.n + k)
                        src_half = ((k - 1) % 2) * CH
                        dst_half = (k % 2) * CH
                        v.scalar_tensor_tensor(sb_cacc[:, dst_half:dst_half + CH],
                                               sb_xrh[:, base + k: base + k + CH],
                                               g.c_wc[:, dt * 4 + k: dt * 4 + k + 1],
                                               sb_cacc[:, src_half:src_half + CH],
                                               op0=AluOpType.mult,
                                               op1=AluOpType.add).then_inc(g.sVE.h, 1)
                g.sVE.n += 4
                mcv = g.sVE.n
                @blk.scalar
                def _(s, dt=dt):
                    s.wait_ge(g.sVE.h, mcv)
                    s.activation(sb_xi[:, dt * CH:(dt + 1) * CH], sb_cacc[:, CH:2 * CH],
                                 AF.Silu,
                                 bias=g.c_bcv[:, dt:dt + 1], scale=1.0).then_inc(g.sSE.h, 1)
                g.sSE.n += 1
                cv_dones.append(g.sSE.n)
            @blk.vector
            def _(v):
                v.wait_ge(g.sSE.h, cv_dones[-1])
                for dt in range(DIT):
                    base = dt * (CH + 3)
                    v.tensor_copy(g.c_stats[:, 0:3], sb_xrh[:, base + CH: base + CH + 3])
                    v.drain()
                    v.tensor_copy(sb_xrh[:, base: base + 3], g.c_stats[:, 0:3])
                    v.drain()
                v.nop().then_inc(g.sVE.h, 1)
            g.sVE.n += 1
            prev_ve = g.sVE.n
            @blk.sync
            def _(sync, c=c):
                sync.wait_ge(g.sSE.h, cv_dones[-1])
                for dt in range(DIT):
                    g.sST.inc(sync.dma_start(
                        g.xi_h.ap()[dt * 128:(dt + 1) * 128, c * CH:(c + 1) * CH],
                        sb_xi[:, dt * CH:(dt + 1) * CH]), 16)
                if DEBUG:
                    for dt in range(DIT):
                        g.sST.inc(sync.dma_start(
                            g.dbg_xi.ap()[dt * 128:(dt + 1) * 128, c * CH:(c + 1) * CH],
                            sb_xi[:, dt * CH:(dt + 1) * CH]), 16)
            blk.flush_all()
    g.ms_xi_stored = g.sST.n





def emit_p3b(g, blk):
    nc = g.nc
    din = g.din
    DUW = DIT * CH
    HW = 8 * CH

    with ExitStack() as _es:
        sb_wx = _es.enter_context(nc.sbuf_tensor("sb_wx", [128, DIT * DI], BF16))
        sb_wxbc = _es.enter_context(nc.sbuf_tensor("sb_wxbc", [128, DIT * 32], BF16))
        sb_xic = _es.enter_context(nc.sbuf_tensor("sb_xic", [128, DIT * CH], BF16))
        sb_delta = _es.enter_context(nc.sbuf_tensor("sb_delta", [128, DIT * CH], BF16))
        sb_u = _es.enter_context(nc.sbuf_tensor("sb_u", [128, DIT * CH], BF16))
        sb_t1 = _es.enter_context(nc.sbuf_tensor("sb_t1", [128, DIT * CH], BF16))
        sb_bc = _es.enter_context(nc.sbuf_tensor("sb_bc", [32, 512], BF16))
        ps_p = _es.enter_context(nc.psum_tensor("ps_p", [128, 1024], F32))
        ps_bc = _es.enter_context(nc.psum_tensor("ps_bc", [128, 512], F32))

        @blk.gpsimd
        def _(gp):
            for i in range(DIT):
                g.sLG.inc(gp.dma_start(sb_wx[:, i * DI:(i + 1) * DI],
                                       din["wxT_d"].ap()[i * 128:(i + 1) * 128, :]), 16)
                g.sLG.inc(gp.dma_start(sb_wxbc[:, i * 32:(i + 1) * 32],
                                       din["wxT_bc"].ap()[i * 128:(i + 1) * 128, :]), 16)
        wx_l = g.sLG.n

        for c in range(NCH):
            ve_prev = g.sVE.n
            se_prev = g.sSE.n
            te_prev2 = g.sTE.n
            @blk.sync
            def _(sync, c=c):
                sync.wait_ge(g.sVE.h, ve_prev)
                sync.wait_ge(g.sSE.h, se_prev)
                sync.wait_ge(g.sTE.h, te_prev2)
                sync.wait_ge(g.sST.h, g.ms_xi_stored)
                for dt in range(DIT):
                    g.sLD.inc(sync.dma_start(
                        sb_xic[:, dt * CH:(dt + 1) * CH],
                        g.xi_h.ap()[dt * 128:(dt + 1) * 128, c * CH:(c + 1) * CH]), 16)
            xl = g.sLD.n
            evs = []
            for mt in range(DIT):
                @blk.tensor
                def _(t, mt=mt):
                    if mt == 0:
                        t.wait_ge(g.sLD.h, xl)
                        t.wait_ge(g.sLG.h, wx_l)
                    if mt >= 2:
                        t.wait_ge(g.sSE.h, evs[mt - 2])
                    p0 = (mt % 2) * 512
                    for kdt in range(DIT):
                        inst = t.matmul(ps_p[:, p0:p0 + CH],
                                        sb_wx[:, kdt * DI + mt * 128: kdt * DI + mt * 128 + 128],
                                        sb_xic[:, kdt * CH:(kdt + 1) * CH],
                                        start=(kdt == 0), stop=(kdt == DIT - 1))
                    inst.then_inc(g.sTE.h, 1)
                g.sTE.n += 1
                mm = g.sTE.n
                @blk.scalar
                def _(s, mt=mt):
                    s.wait_ge(g.sTE.h, mm)
                    p0 = (mt % 2) * 512
                    s.activation(sb_t1[:, mt * CH:(mt + 1) * CH], ps_p[:, p0:p0 + CH],
                                 AF.Exp,
                                 bias=g.c_bxd[:, mt:mt + 1], scale=1.0).then_inc(g.sSE.h, 1)
                g.sSE.n += 1
                evs.append(g.sSE.n)
            @blk.scalar
            def _(s):
                s.activation(sb_delta[:], sb_t1[:], AF.Ln, bias=1.0,
                             scale=1.0).then_inc(g.sSE.h, 1)
            g.sSE.n += 1
            evs.append(g.sSE.n)
            @blk.tensor
            def _(t):
                t.wait_ge(g.sSE.h, evs[-1])
                for kdt in range(DIT):
                    inst = t.matmul(ps_bc[0:32, 0:CH],
                                    sb_wxbc[:, kdt * 32:(kdt + 1) * 32],
                                    sb_xic[:, kdt * CH:(kdt + 1) * CH],
                                    start=(kdt == 0), stop=(kdt == DIT - 1))
                inst.then_inc(g.sTE.h, 1)
            g.sTE.n += 1
            mmbc = g.sTE.n
            @blk.scalar
            def _(s):
                s.wait_ge(g.sTE.h, mmbc)
                s.activation(sb_bc[:], ps_bc[0:32, 0:CH], AF.Identity,
                             bias=g.c_bxbc[:, 0:1], scale=1.0).then_inc(g.sSE.h, 1)
            g.sSE.n += 1
            bc_done = g.sSE.n
            @blk.vector
            def _(v):
                v.wait_ge(g.sSE.h, evs[-1])
                for dt in range(DIT):
                    v.tensor_tensor(sb_u[:, dt * CH:(dt + 1) * CH],
                                    sb_delta[:, dt * CH:(dt + 1) * CH],
                                    sb_xic[:, dt * CH:(dt + 1) * CH],
                                    op=AluOpType.mult)
                v.drain()
                v.nop().then_inc(g.sVE.h, 1)
            g.sVE.n += 1
            u_done = g.sVE.n
            @blk.dsync
            def _(sync, c=c):
                sync.wait_ge(g.sVE.h, u_done)
                sync.wait_ge(g.sSE.h, bc_done)
                r = g.r_role
                with sync.If_eq(r, 0):
                    reg = g.du_sh[(0, c)]
                    sn = g.du_sn[(0, c)]
                    sync.dma_start(reg.ap()[:, 0:HW], sb_delta[:, 0:HW]).then_inc(g.sST.h, 16)
                    sync.dma_start(reg.ap()[:, HW:2 * HW], sb_delta[:, HW:2 * HW]).then_inc(g.sST.h, 16)
                    sync.dma_start(reg.ap()[:, DUW:DUW + HW], sb_u[:, 0:HW]).then_inc(g.sST.h, 16)
                    sync.dma_start(reg.ap()[:, DUW + HW:2 * DUW], sb_u[:, HW:2 * HW]).then_inc(g.sST.h, 16)
                    sync.dma_start(reg.ap()[0:32, 2 * DUW:2 * DUW + 512], sb_bc[:]).then_inc(g.sST.h, 16)
                    sync.dma_start(sn.ap(), g.c_nonrow[:]).then_inc(g.sST.h, 16)
                with sync.Else():
                    reg = g.du_sh[(1, c)]
                    sn = g.du_sn[(1, c)]
                    sync.dma_start(reg.ap()[:, 0:HW], sb_delta[:, HW:2 * HW]).then_inc(g.sST.h, 16)
                    sync.dma_start(reg.ap()[:, HW:2 * HW], sb_delta[:, 0:HW]).then_inc(g.sST.h, 16)
                    sync.dma_start(reg.ap()[:, DUW:DUW + HW], sb_u[:, HW:2 * HW]).then_inc(g.sST.h, 16)
                    sync.dma_start(reg.ap()[:, DUW + HW:2 * DUW], sb_u[:, 0:HW]).then_inc(g.sST.h, 16)
                    sync.dma_start(reg.ap()[0:32, 2 * DUW:2 * DUW + 512], sb_bc[:]).then_inc(g.sST.h, 16)
                    sync.dma_start(sn.ap(), g.c_nonrow[:]).then_inc(g.sST.h, 16)
                sync.end_ifs()
            g.sST.n += 96
            blk.flush_all()
        send_done = g.sST.n

    with ExitStack() as _es:
        sb_sdu = _es.enter_context(nc.sbuf_tensor("sb_sdu", [128, 2 * 2 * HW], BF16))
        sb_Bb = _es.enter_context(nc.sbuf_tensor("sb_Bb", [128, 2 * 4 * CH], BF16))
        sb_Cb = _es.enter_context(nc.sbuf_tensor("sb_Cb", [128, 2 * 4 * CH], BF16))
        sb_dA = _es.enter_context(nc.sbuf_tensor("sb_dA", [128, 2 * 8 * CH], BF16))
        sb_dBx = _es.enter_context(nc.sbuf_tensor("sb_dBx", [128, 8 * CH], BF16))
        sb_h = _es.enter_context(nc.sbuf_tensor("sb_h", [128, 8 * CH], BF16))
        sb_yacc = _es.enter_context(nc.sbuf_tensor("sb_yacc", [128, 2 * 8 * CH], BF16))
        sb_y16 = _es.enter_context(nc.sbuf_tensor("sb_y16", [128, DIT * CH], BF16))
        sb_gxi = _es.enter_context(nc.sbuf_tensor("sb_gxi", [128, DIT * CH], BF16))
        sb_gres = _es.enter_context(nc.sbuf_tensor("sb_gres", [128, DIT * CH], BF16))
        g.sb_wo = _es.enter_context(nc.sbuf_tensor("sb_wo3", [128, DIT * DM], BF16))
        g.sb_cxc = _es.enter_context(nc.sbuf_tensor("sb_cxc3", [128, 4 * DM], F32))
        g.ps_co = _es.enter_context(nc.psum_tensor("ps_co3", [128, 2048], F32))
        g.p3c_prev_resid = [0]
        g.p3c_prev_store = [0]

        @blk.gpsimd
        def _(gp):
            for i in range(DIT):
                g.sLG.inc(gp.dma_start(g.sb_wo[:, i * DM:(i + 1) * DM],
                                       g.din["woT"].ap()[i * 128:(i + 1) * 128, :]), 16)
        g.ms_wo_loaded = g.sLG.n
        first_scan_gate = [send_done]
        ysend_hist = []
        scan_ms = []
        sc_dones = []
        se_dones = []
        for sc in range(8):
            owner, cc = sc // 4, sc % 4
            buf = sc % 2
            emit_spin(g, blk, g.du_sn[(owner, cc)])
            @blk.dsync
            def _(sync, owner=owner, cc=cc, buf=buf, sc=sc):
                if sc < 2:
                    sync.wait_ge(g.sST.h, first_scan_gate[0])
                else:
                    sync.wait_ge(g.sVE.h, sc_dones[sc - 2])
                    sync.wait_ge(g.sSE.h, se_dones[sc - 2])
                reg = g.du_sh[(owner, cc)]
                r = g.r_role
                sdu_sem = g.sDU0 if buf == 0 else g.sDU1
                with sync.If_eq(r, 0):
                    sync.dma_start(sb_sdu[:, buf * 2 * HW: buf * 2 * HW + HW],
                                   reg.ap()[:, 0:HW]).then_inc(sdu_sem.h, 16)
                    sync.dma_start(sb_sdu[:, buf * 2 * HW + HW: buf * 2 * HW + 2 * HW],
                                   reg.ap()[:, DUW:DUW + HW]).then_inc(sdu_sem.h, 16)
                with sync.Else():
                    sync.dma_start(sb_sdu[:, buf * 2 * HW: buf * 2 * HW + HW],
                                   reg.ap()[:, HW:2 * HW]).then_inc(sdu_sem.h, 16)
                    sync.dma_start(sb_sdu[:, buf * 2 * HW + HW: buf * 2 * HW + 2 * HW],
                                   reg.ap()[:, DUW + HW:2 * DUW]).then_inc(sdu_sem.h, 16)
                sync.end_ifs()
            sdu_sem = g.sDU0 if buf == 0 else g.sDU1
            sdu_sem.n += 32
            sdu_l = sdu_sem.n
            grp_done = []
            grp_read_ms = {}
            @blk.sync
            def _(sync, owner=owner, cc=cc, sdu_sem=sdu_sem):
                reg = g.du_sh[(owner, cc)]
                for ng in range(2):
                    for n in range(ng * 4, ng * 4 + 4):
                        gslot = (ng % 2) * 4 + (n % 4)
                        sdu_sem.inc(sync.dma_start(
                            sb_Bb[:, gslot * CH:(gslot + 1) * CH],
                            reg.ap()[n:n + 1, 2 * DUW:2 * DUW + CH].partition_broadcast(128)), 16)
                        sdu_sem.inc(sync.dma_start(
                            sb_Cb[:, gslot * CH:(gslot + 1) * CH],
                            reg.ap()[16 + n:17 + n, 2 * DUW:2 * DUW + CH].partition_broadcast(128)), 16)
                    grp_done.append(sdu_sem.n)

            ysent_prev = ysend_hist[-2] if len(ysend_hist) >= 2 else 0
            for n in range(NST):
                ng = n // 4
                if ng >= 2 and n % 4 == 0:
                    gate_ms = grp_read_ms.get(ng - 2, 0)
                    @blk.sync
                    def _(sync, ng=ng, gate_ms=gate_ms, owner=owner, cc=cc, sdu_sem=sdu_sem):
                        sync.wait_ge(g.sVE.h, gate_ms)
                        reg = g.du_sh[(owner, cc)]
                        for nn in range(ng * 4, ng * 4 + 4):
                            gslot = (ng % 2) * 4 + (nn % 4)
                            sdu_sem.inc(sync.dma_start(
                                sb_Bb[:, gslot * CH:(gslot + 1) * CH],
                                reg.ap()[nn:nn + 1, 2 * DUW:2 * DUW + CH].partition_broadcast(128)), 16)
                            sdu_sem.inc(sync.dma_start(
                                sb_Cb[:, gslot * CH:(gslot + 1) * CH],
                                reg.ap()[16 + nn:17 + nn, 2 * DUW:2 * DUW + CH].partition_broadcast(128)), 16)
                        grp_done.append(sdu_sem.n)
                da_buf = n % 2
                @blk.scalar
                def _(s, n=n, buf=buf, da_buf=da_buf, sdu_sem=sdu_sem):
                    s.wait_ge(sdu_sem.h, sdu_l)
                    if len(scan_ms) >= 2:
                        s.wait_ge(g.sVE.h, scan_ms[-2])
                    for dt in range(8):
                        inst = s.activation(
                            sb_dA[:, da_buf * 8 * CH + dt * CH: da_buf * 8 * CH + (dt + 1) * CH],
                            sb_sdu[:, buf * 2 * HW + dt * CH: buf * 2 * HW + (dt + 1) * CH],
                            AF.Exp, bias=0.0,
                            scale=g.c_A[:, dt * NST + n: dt * NST + n + 1])
                    inst.then_inc(g.sSE.h, 1)
                g.sSE.n += 1
                da_done = g.sSE.n
                @blk.vector
                def _(v, n=n, buf=buf, da_buf=da_buf, ysp=ysent_prev, ng=ng,
                      bcast_ms=grp_done[min(ng, len(grp_done) - 1)],
                      sdu_sem2=(g.sDU0 if buf == 0 else g.sDU1)):
                    v.wait_ge(g.sSE.h, da_done)
                    v.wait_ge(sdu_sem2.h, bcast_ms)
                    if n == 0 and ysp:
                        v.wait_ge(g.sST.h, ysp)
                    gslot = (ng % 2) * 4 + (n % 4)
                    Brep = sb_Bb[:, gslot * CH:(gslot + 1) * CH].rearrange(
                        "p (o c) -> p o c", o=1).broadcast_to([128, 8, CH])
                    Crep = sb_Cb[:, gslot * CH:(gslot + 1) * CH].rearrange(
                        "p (o c) -> p o c", o=1).broadcast_to([128, 8, CH])
                    v.tensor_tensor(sb_dBx[:].rearrange("p (o c) -> p o c", o=8),
                                    sb_sdu[:, buf * 2 * HW + HW: buf * 2 * HW + 2 * HW].rearrange(
                                        "p (o c) -> p o c", o=8),
                                    Brep, op=AluOpType.mult)
                    v.drain()
                    for dt in range(8):
                        v.tensor_tensor_scan(
                            sb_h[:, dt * CH:(dt + 1) * CH],
                            sb_dA[:, da_buf * 8 * CH + dt * CH: da_buf * 8 * CH + (dt + 1) * CH],
                            sb_dBx[:, dt * CH:(dt + 1) * CH],
                            g.c_hprev[:, dt * NST + n: dt * NST + n + 1],
                            op0=AluOpType.mult, op1=AluOpType.add)
                    v.drain()
                    v.tensor_copy(
                        g.c_hprev[:].rearrange("p (d n2) -> p d n2", n2=NST)[:, 0:8, n:n + 1],
                        sb_h[:].rearrange("p (d c) -> p d c", c=CH)[:, :, CH - 1:CH])
                    v.drain()
                    ybuf = (sc % 2) * 8 * CH
                    dst = sb_yacc[:, ybuf:ybuf + 8 * CH] if n == 0 else sb_dBx[:]
                    v.tensor_tensor(dst.rearrange("p (o c) -> p o c", o=8),
                                    sb_h[:].rearrange("p (o c) -> p o c", o=8),
                                    Crep, op=AluOpType.mult)
                    v.drain()
                    if n > 0:
                        v.tensor_tensor(sb_yacc[:, ybuf:ybuf + 8 * CH],
                                        sb_yacc[:, ybuf:ybuf + 8 * CH],
                                        sb_dBx[:], op=AluOpType.add)
                        v.drain()
                    v.nop().then_inc(g.sVE.h, 1)
                g.sVE.n += 1
                scan_ms.append(g.sVE.n)
                grp_read_ms[ng] = g.sVE.n
                if n % 4 == 3:
                    blk.flush_all()
            sc_done = g.sVE.n
            sc_dones.append(g.sVE.n)
            se_dones.append(g.sSE.n)
            @blk.dsync
            def _(sync, sc=sc):
                sync.wait_ge(g.sVE.h, sc_done)
                ybuf = (sc % 2) * 8 * CH
                r = g.r_role
                with sync.If_eq(r, 0):
                    sync.dma_start(g.y_sh[(sc, 0)].ap(), sb_yacc[:, ybuf:ybuf + 8 * CH]).then_inc(g.sST.h, 16)
                    sync.dma_start(g.y_sn[(sc, 0)].ap(), g.c_nonrow[:]).then_inc(g.sST.h, 16)
                with sync.Else():
                    sync.dma_start(g.y_sh[(sc, 1)].ap(), sb_yacc[:, ybuf:ybuf + 8 * CH]).then_inc(g.sST.h, 16)
                    sync.dma_start(g.y_sn[(sc, 1)].ap(), g.c_nonrow[:]).then_inc(g.sST.h, 16)
                sync.end_ifs()
            g.sST.n += 32
            ysend_hist.append(g.sST.n)

            if sc >= 4:
                gi = sc - 4
                emit_gate(g, blk, gi, sb_y16, sb_gxi, sb_gres)
    g.ms_y3_stored = g.sST.n


def emit_gate(g, blk, gi, sb_y16, sb_gxi, sb_gres):
    nc = g.nc
    ve_prev = g.sVE.n
    @blk.sync
    def _(sync, vp=ve_prev):
        sync.wait_ge(g.sVE.h, vp)
    for part in range(2):
        @blk.dsync
        def _(sync, gi=gi, part=part):
            e = g.rv_nonce
            pr = g.r_probe
            r = g.r_role
            nread = [0]
            def spin_then_read(sync, reg_key):
                sync.sem_clear(g.sSP)
                sync.dma_start(g.c_probe[:], g.y_sn[reg_key].ap()).then_inc(g.sSP, 16)
                nread[0] = 1
                sync.wait_ge(g.sSP, 16)
                for ci_, row in enumerate(SPIN_CELLS):
                    K = 10 if ci_ == 0 else 1
                    sync.reg_load(pr, g.c_probe[row:row + 1, 0:1])
                    for _i in range(K):
                        with sync.If_ne(pr, e):
                            sync.nop(cycle_cnt=8000)
                            sync.dma_start(g.c_probe[:], g.y_sn[reg_key].ap()).then_inc(g.sSP, 16)
                            nread[0] += 1
                            sync.wait_ge(g.sSP, 16 * nread[0])
                            sync.reg_load(pr, g.c_probe[row:row + 1, 0:1])
                        sync.end_ifs()
                sync.dma_start(sb_y16[:, part * 8 * CH:(part + 1) * 8 * CH],
                               g.y_sh[reg_key].ap()).then_inc(g.sGT.h, 16)
            with sync.If_eq(r, 0):
                spin_then_read(sync, (gi, part))
            with sync.Else():
                spin_then_read(sync, (4 + gi, 1 - part))
            sync.end_ifs()
        g.sGT.n += 16
    y_l = g.sGT.n
    @blk.sync
    def _(sync, gi=gi):
        for dt in range(DIT):
            g.sGT.inc(sync.dma_start(
                sb_gxi[:, dt * CH:(dt + 1) * CH],
                g.xi_h.ap()[dt * 128:(dt + 1) * 128, gi * CH:(gi + 1) * CH]), 16)
            g.sGT.inc(sync.dma_start(
                sb_gres[:, dt * CH:(dt + 1) * CH],
                g.res_h.ap()[dt * 128:(dt + 1) * 128, gi * CH:(gi + 1) * CH]), 16)
    xi_l = g.sGT.n
    @blk.scalar
    def _(s):
        s.wait_ge(g.sGT.h, xi_l)
        s.activation(sb_gres[:], sb_gres[:], AF.Silu, bias=0.0, scale=1.0).then_inc(g.sSE.h, 1)
    g.sSE.n += 1
    sr_done = g.sSE.n
    @blk.vector
    def _(v, gi=gi):
        v.wait_ge(g.sSE.h, sr_done)
        v.wait_ge(g.sGT.h, y_l)
        v.tensor_tensor(sb_gxi[:].rearrange("p (d c) -> p d c", c=CH),
                        sb_gxi[:].rearrange("p (d c) -> p d c", c=CH),
                        g.c_D[:].rearrange("p (d o) -> p d o", o=1).broadcast_to([128, DIT, CH]),
                        op=AluOpType.mult)
        v.drain()
        v.tensor_tensor(sb_y16[:], sb_y16[:], sb_gxi[:], op=AluOpType.add)
        v.drain()
        v.tensor_tensor(sb_y16[:], sb_y16[:], sb_gres[:], op=AluOpType.mult)
        v.drain()
        v.nop().then_inc(g.sVE.h, 1)
    g.sVE.n += 1
    g3 = g.sVE.n
    @blk.sync
    def _(sync, gi=gi, ps0=g.p3c_prev_store[0]):
        if ps0:
            sync.wait_ge(g.sST.h, ps0)
        for r in range(4):
            g.sGT.inc(sync.dma_start(
                g.sb_cxc[:, r * DM:(r + 1) * DM],
                g.x2_h.ap()[gi * CH + 256 + r * 128: gi * CH + 256 + (r + 1) * 128, :]), 16)
    x2_l = g.sGT.n
    resid_ms = []
    for tt in range(4):
        @blk.tensor
        def _(t, tt=tt, gi=gi, pr3=g.p3c_prev_resid[0]):
            if tt == 0:
                t.wait_ge(g.sVE.h, g3)
                t.wait_ge(g.sGT.h, x2_l)
                t.wait_ge(g.sLG.h, g.ms_wo_loaded)
                if pr3:
                    t.wait_ge(g.sVE.h, pr3)
            if len(resid_ms) >= 2:
                t.wait_ge(g.sVE.h, resid_ms[-2])
            for fh in range(2):
                for kdt in range(DIT):
                    t.matmul(g.ps_co[:, ((tt % 2) * 2 + fh) * 512:((tt % 2) * 2 + fh) * 512 + 512],
                             sb_y16[:, kdt * CH + tt * 128: kdt * CH + tt * 128 + 128],
                             g.sb_wo[:, kdt * DM + fh * 512: kdt * DM + (fh + 1) * 512],
                             start=(kdt == 0), stop=False)
                inst = t.matmul(g.ps_co[:, ((tt % 2) * 2 + fh) * 512:((tt % 2) * 2 + fh) * 512 + 512],
                                g.c_ones[0:1, 0:128],
                                g.c_borow[0:1, fh * 512:(fh + 1) * 512],
                                start=False, stop=True)
            inst.then_inc(g.sTE.h, 1)
        g.sTE.n += 1
        mm3 = g.sTE.n
        @blk.vector
        def _(v, tt=tt, ps0=g.p3c_prev_store[0], mm3=mm3):
            v.wait_ge(g.sTE.h, mm3)
            if tt == 0 and ps0:
                v.wait_ge(g.sST.h, ps0)
            v.tensor_tensor(g.sb_cxc[:, tt * DM:(tt + 1) * DM],
                            g.sb_cxc[:, tt * DM:(tt + 1) * DM],
                            g.ps_co[:, (tt % 2) * 1024:(tt % 2) * 1024 + 1024],
                            op=AluOpType.add).then_inc(g.sVE.h, 1)
        g.sVE.n += 1
        resid_ms.append(g.sVE.n)
    g.p3c_prev_resid[0] = g.sVE.n
    done_v3 = g.sVE.n
    @blk.sync
    def _(sync, gi=gi, done_v3=done_v3):
        sync.wait_ge(g.sVE.h, done_v3)
        for r in range(4):
            g.sST.inc(sync.dma_start(
                g.x3_h.ap()[gi * CH + r * 128: gi * CH + (r + 1) * 128, :],
                g.sb_cxc[:, r * DM:(r + 1) * DM]), 16)
            if DEBUG:
                g.sST.inc(sync.dma_start(
                    g.dbg_x3.ap()[gi * CH + r * 128: gi * CH + (r + 1) * 128, :],
                    g.sb_cxc[:, r * DM:(r + 1) * DM]), 16)
    g.p3c_prev_store[0] = g.sST.n
    blk.flush_all()



def emit_p3c(g, blk):
    nc = g.nc
    din = g.din
    with ExitStack() as _es:
        sb_wo = _es.enter_context(nc.sbuf_tensor("sb_wo", [128, DIT * DM], BF16))
        sb_y3 = _es.enter_context(nc.sbuf_tensor("sb_y3", [128, DIT * CH], BF16))
        sb_xc = _es.enter_context(nc.sbuf_tensor("sb_cxc", [128, 4 * DM], F32))
        sb_o = _es.enter_context(nc.sbuf_tensor("sb_co", [128, 4 * DM], F32))
        ps_o = _es.enter_context(nc.psum_tensor("ps_co", [128, 2048], F32))

        @blk.gpsimd
        def _(gp):
            for i in range(DIT):
                g.sLG.inc(gp.dma_start(sb_wo[:, i * DM:(i + 1) * DM],
                                       din["woT"].ap()[i * 128:(i + 1) * 128, :]), 16)
        wo_l = g.sLG.n

        prev_resid = 0
        prev_store = g.sST.n
        resid_ms = []
        for c in range(NCH):
            te_prev3 = g.sTE.n
            @blk.sync
            def _(sync, c=c, pv=prev_resid, tp=te_prev3):
                if pv:
                    sync.wait_ge(g.sVE.h, pv)
                sync.wait_ge(g.sTE.h, tp)
                sync.wait_ge(g.sST.h, g.ms_y3_stored)
                for dt in range(DIT):
                    g.sLD.inc(sync.dma_start(
                        sb_y3[:, dt * CH:(dt + 1) * CH],
                        g.y3_h.ap()[dt * 128:(dt + 1) * 128, c * CH:(c + 1) * CH]), 16)
                for r in range(4):
                    g.sLD.inc(sync.dma_start(
                        sb_xc[:, r * DM:(r + 1) * DM],
                        g.x2_h.ap()[c * CH + 256 + r * 128: c * CH + 256 + (r + 1) * 128, :]), 16)
            yl = g.sLD.n
            for tt in range(4):
                @blk.tensor
                def _(t, tt=tt, pv=prev_resid):
                    if tt == 0:
                        t.wait_ge(g.sLD.h, yl)
                        t.wait_ge(g.sLG.h, wo_l)
                    if len(resid_ms) >= 2:
                        t.wait_ge(g.sVE.h, resid_ms[-2])
                    for fh in range(2):
                        for kdt in range(DIT):
                            t.matmul(ps_o[:, ((tt % 2) * 2 + fh) * 512:((tt % 2) * 2 + fh) * 512 + 512],
                                     sb_y3[:, kdt * CH + tt * 128: kdt * CH + tt * 128 + 128],
                                     sb_wo[:, kdt * DM + fh * 512: kdt * DM + (fh + 1) * 512],
                                     start=(kdt == 0), stop=False)
                        inst = t.matmul(ps_o[:, ((tt % 2) * 2 + fh) * 512:((tt % 2) * 2 + fh) * 512 + 512],
                                        g.c_ones[0:1, 0:128],
                                        g.c_borow[0:1, fh * 512:(fh + 1) * 512],
                                        start=False, stop=True)
                    inst.then_inc(g.sTE.h, 1)
                g.sTE.n += 1
                mm = g.sTE.n
                @blk.vector
                def _(v, tt=tt, c=c, ps0=prev_store):
                    v.wait_ge(g.sTE.h, mm)
                    if tt == 0:
                        v.wait_ge(g.sST.h, ps0)
                    v.tensor_tensor(sb_xc[:, tt * DM:(tt + 1) * DM],
                                    sb_xc[:, tt * DM:(tt + 1) * DM],
                                    ps_o[:, 0:1024],
                                    op=AluOpType.add).then_inc(g.sVE.h, 1)
                g.sVE.n += 1
                prev_resid = g.sVE.n
                resid_ms.append(g.sVE.n)
            done_v = g.sVE.n
            @blk.sync
            def _(sync, c=c):
                sync.wait_ge(g.sVE.h, done_v)
                for r in range(4):
                    g.sST.inc(sync.dma_start(
                        g.x3_h.ap()[c * CH + r * 128: c * CH + (r + 1) * 128, :],
                        sb_o[:, r * DM:(r + 1) * DM]), 16)
                    if DEBUG:
                        g.sST.inc(sync.dma_start(
                            g.dbg_x3.ap()[c * CH + r * 128: c * CH + (r + 1) * 128, :],
                            sb_o[:, r * DM:(r + 1) * DM]), 16)
            prev_store = g.sST.n
            blk.flush_all()


def emit(g):
    nc = g.nc
    with nc.Block() as rawblk:
        blk = RBlk(rawblk)
        emit_p0(g, blk)
        @blk.dsync
        def _(sync):
            sync.wait_ge(g.sLD.h, g.sLD.n)
            g.r_role = sync.to_reg(sync.value_load(g.c_rolei[0:1, 0:1]))
            g.rv_nonce = sync.value_load(g.c_noni[0:1, 0:1])
            g.r_probe = sync.alloc_register()
        barrier(g, blk)
        emit_p1(g, blk)
        barrier(g, blk)
        emit_ffn(g, blk, first=True)
        barrier(g, blk)
        emit_p3a(g, blk)
        barrier(g, blk)
        emit_p3b(g, blk)
        barrier(g, blk)
        emit_ffn(g, blk, first=False)
        blk.flush_all()
        @blk.sync
        def _(sync):
            sync.wait_ge(g.sST.h, g.sST.n)
        blk.flush_all()



def kernel(**inputs):
    ims = stage_inputs(inputs)
    nc, g = build_program()
    res = run_bass_kernel_spmd(nc, ims, core_ids=list(range(8)))
    outp = np.zeros((B, SEQ, DM), np.float32)
    for c in range(8):
        b_ = c // 2
        half = c % 2
        outp[b_, half * TOK:(half + 1) * TOK] = res.results[c]["out"]
    return outp


# version 22
# speedup vs baseline: 1.0093x; 1.0093x over previous
import time
from contextlib import ExitStack
import numpy as np
import ml_dtypes
import concourse.bass as bass
import concourse.mybir as mybir
from concourse.alu_op_type import AluOpType
from concourse.bass_utils import run_bass_kernel_spmd

F32 = mybir.dt.float32
BF16 = mybir.dt.bfloat16
I32 = mybir.dt.int32
AF = mybir.ActivationFunctionType
BFNP = ml_dtypes.bfloat16

B, SEQ, DM = 4, 4096, 1024
DI, NST, FF = 2048, 16, 4096
TOK = 2048
PRE = 256
CH = 512
NCH = 4
XROWS = 512 + TOK + 256
DT = DM // 128
DIT = DI // 128
FT = FF // 128
NEG = -960000.0
EPS = 1e-5
DEBUG = False
RACE_CHECK = False

CHUNKS = [(-256, 256)] + [(c * CH, CH) for c in range(NCH)]


def _f32(x):
    return np.ascontiguousarray(np.asarray(x), dtype=np.float32)


def _bf(x):
    return np.ascontiguousarray(np.asarray(x).astype(BFNP))


class Sem:
    def __init__(self, nc, name):
        self.h = nc.alloc_semaphore(name)
        self.n = 0

    def inc(self, inst, k):
        inst.then_inc(self.h, k)
        self.n += k


class G:
    pass


class _RInst:
    __slots__ = ("incs",)
    def __init__(self):
        self.incs = []
    def then_inc(self, sem, k):
        self.incs.append((sem, k))
        return self


class _REng:
    def __init__(self, rb, name):
        self._rb = rb
        self._name = name
    def __getattr__(self, meth):
        def call(*a, **kw):
            ri = _RInst()
            self._rb.q[self._name].append((meth, a, kw, ri))
            return ri
        return call


class RBlk:
    ENGINES = ("sync", "vector", "scalar", "tensor", "gpsimd")
    def __init__(self, blk):
        self._blk = blk
        self.q = {e: [] for e in self.ENGINES}
        for e in self.ENGINES:
            setattr(self, e, self._mk(e))
        self.dsync = self._mk_direct("sync")
        self.dgpsimd = self._mk_direct("gpsimd")
    def _mk(self, name):
        def deco(fn):
            fn(_REng(self, name))
            return fn
        return deco
    def _mk_direct(self, name):
        def deco(fn):
            self.flush(name)
            real = getattr(self._blk, name)
            @real
            def _(eng):
                fn(eng)
            return fn
        return deco
    def flush(self, engine=None):
        engines = [engine] if engine else list(self.ENGINES)
        for e in engines:
            items = self.q[e]
            if not items:
                continue
            self.q[e] = []
            real = getattr(self._blk, e)
            @real
            def _(eng, items=items):
                for meth, a, kw, ri in items:
                    inst = getattr(eng, meth)(*a, **kw)
                    for sem, k in ri.incs:
                        inst.then_inc(sem, k)
    def flush_all(self):
        self.flush()




def build_masks(half):
    def block_mask(gb):
        r = np.arange(256)[:, None]
        c = np.arange(768)[None, :]
        i = gb * 256 + r
        j = (gb - 1) * 256 + c
        rel = j - i
        valid = (rel >= -128) & (rel <= 128) & (j >= 0) & (j < SEQ)
        return np.where(valid, 0.0, NEG).astype(np.float32)

    def interior():
        r = np.arange(256)[:, None]
        c = np.arange(768)[None, :]
        rel = c - 256 - r
        return np.where((rel >= -128) & (rel <= 128), 0.0, NEG).astype(np.float32)

    g0 = half * 8
    blocks = [block_mask(g0 - 1) if g0 - 1 >= 0 else interior()]
    for gb in range(g0, g0 + 8):
        blocks.append(block_mask(gb))
    m = np.stack(blocks).reshape(9, 2, 128, 768)
    m = m.transpose(2, 0, 1, 3).reshape(128, 18 * 768)
    return _bf(m)


def tile_cols(vec, ntiles):
    return _f32(np.asarray(vec).reshape(ntiles, 128).T)


def stage_inputs(inputs):
    x = _f32(inputs["L0_attn_a_x"])
    winT = _f32(np.asarray(inputs["L0_attn_b_W_in"]).T)
    wxT = _f32(np.asarray(inputs["L0_attn_b_W_x"]).T)
    woT = _f32(np.asarray(inputs["L0_attn_b_W_out"]).T)
    wc = _f32(inputs["L0_attn_b_W_conv"][:, 0, :])
    alog = _f32(inputs["L0_attn_b_A_log"])
    dpar = _f32(inputs["L0_attn_b_D"])
    b_in = _f32(inputs["L0_attn_b_b_in"])
    bcv = _f32(inputs["L0_attn_b_b_conv"])
    b_x = _f32(inputs["L0_attn_b_b_x"])

    nonce_val = (int(time.time_ns()) % (2 ** 24)) | 1
    common = {
        "wq": _f32(inputs["L0_attn_a_Wq"]),
        "wk": _f32(inputs["L0_attn_a_Wk"]),
        "wv": _f32(inputs["L0_attn_a_Wv"]),
        "bq": tile_cols(inputs["L0_attn_a_bq"], DT),
        "bk": tile_cols(inputs["L0_attn_a_bk"], DT),
        "bv_row": _bf(np.asarray(inputs["L0_attn_a_bv"]).reshape(1, DM)),
        "w1T": _f32(np.asarray(inputs["L0_ffn_W1"]).T),
        "b1": tile_cols(inputs["L0_ffn_b1"], FT),
        "w2T": _f32(np.asarray(inputs["L0_ffn_W2"]).T),
        "b2_row": _bf(np.asarray(inputs["L0_ffn_b2"]).reshape(1, DM)),
        "ident": np.eye(128, dtype=BFNP),
        "nonce_i": np.array([[nonce_val]], np.int32),
        "nonce_row": np.full((128, 8), nonce_val, np.int32),
        "krow": np.tile(np.array([1e-5, 1.0 / 32.0, 1.0, 0.5, 2.0, 0.25, 0.0, 0.0],
                                 np.float32), (128, 1)),
    }
    ims = []
    for c in range(8):
        b_ = c // 2
        half = c % 2
        hs = half * TOK
        xp = np.zeros((XROWS, DM), np.float32)
        lo, hi = hs - 512, hs + TOK + 256
        s0, s1 = max(lo, 0), min(hi, SEQ)
        xp[s0 - lo: s1 - lo] = x[b_, s0:s1]
        perm = np.arange(DI) if half == 0 else np.concatenate(
            [np.arange(1024, 2048), np.arange(1024)])
        im = dict(common)
        im.update({
            "x_in": xp,
            "winT": np.ascontiguousarray(
                np.concatenate([winT[:, perm], winT[:, 2048 + perm]], axis=1)),
            "b_in_xi": tile_cols(b_in[perm], DIT),
            "b_in_res": tile_cols(b_in[2048 + perm], DIT),
            "wc": _f32(wc[perm].reshape(DIT, 128, 4).transpose(1, 0, 2).reshape(128, DIT * 4)),
            "bcv": tile_cols(bcv[perm], DIT),
            "wxT_d": np.ascontiguousarray(wxT[perm][:, :DI][:, perm]),
            "wxT_bc": np.ascontiguousarray(wxT[perm][:, DI:]),
            "b_x_d": tile_cols(b_x[:DI][perm], DIT),
            "b_x_bc": _f32(b_x[DI:].reshape(32, 1)),
            "woT": np.ascontiguousarray(woT[perm]),
            "bo_row": _bf(np.asarray(inputs["L0_attn_b_b_out"]).reshape(1, DM)),
            "alog": _f32(alog[perm].reshape(DIT, 128, NST).transpose(1, 0, 2).reshape(128, DIT * NST)),
            "dpar": tile_cols(dpar[perm], DIT),
            "masks": build_masks(half),
            "role": np.array([[half]], np.int32),
            "role_f": np.full((128, 1), float(half), np.float32),
        })
        ims.append(im)
    return ims



def build_program():
    nc = bass.Bass("TRN2", target_bir_lowering=False,
                   detect_race_conditions=RACE_CHECK)
    g = G()
    g.nc = nc
    din = {}

    def dram_in(name, shape, dt=F32):
        din[name] = nc.dram_tensor(name, shape, dt, kind="ExternalInput")

    dram_in("x_in", [XROWS, DM])
    for n in ("wq", "wk", "wv"):
        dram_in(n, [DM, DM])
    dram_in("bq", [128, DT]); dram_in("bk", [128, DT]); dram_in("bv_row", [1, DM], BF16)
    dram_in("w1T", [DM, FF]); dram_in("b1", [128, FT])
    dram_in("w2T", [FF, DM]); dram_in("b2_row", [1, DM], BF16)
    dram_in("winT", [DM, 2 * DI])
    dram_in("b_in_xi", [128, DIT]); dram_in("b_in_res", [128, DIT])
    dram_in("wc", [128, DIT * 4]); dram_in("bcv", [128, DIT])
    dram_in("wxT_d", [DI, DI]); dram_in("wxT_bc", [DI, 32])
    dram_in("b_x_d", [128, DIT]); dram_in("b_x_bc", [32, 1])
    dram_in("woT", [DI, DM]); dram_in("bo_row", [1, DM], BF16)
    dram_in("alog", [128, DIT * NST]); dram_in("dpar", [128, DIT])
    dram_in("masks", [128, 18 * 768], BF16)
    dram_in("ident", [128, 128], BF16)
    dram_in("role", [1, 1], I32); dram_in("role_f", [128, 1])
    dram_in("nonce_i", [1, 1], I32); dram_in("nonce_row", [128, 8], I32)
    dram_in("krow", [128, 8])
    g.din = din

    g.out = nc.dram_tensor("out", [TOK, DM], F32, kind="ExternalOutput")
    if DEBUG:
        g.dbg_x1 = nc.dram_tensor("dbg_x1", [PRE + TOK, DM], F32, kind="ExternalOutput")
        g.dbg_x2 = nc.dram_tensor("dbg_x2", [PRE + TOK, DM], F32, kind="ExternalOutput")
        g.dbg_x3 = nc.dram_tensor("dbg_x3", [TOK, DM], F32, kind="ExternalOutput")
        g.dbg_xi = nc.dram_tensor("dbg_xi", [DI, TOK], BF16, kind="ExternalOutput")
        g.dbg_y = nc.dram_tensor("dbg_y", [DI, TOK], BF16, kind="ExternalOutput")

    g.x1_h = nc.dram_tensor("x1_h", [PRE + TOK, DM], F32, kind="Internal")
    g.x2_h = nc.dram_tensor("x2_h", [PRE + TOK, DM], F32, kind="Internal")
    g.x3_h = nc.dram_tensor("x3_h", [TOK, DM], F32, kind="Internal")
    g.res_h = nc.dram_tensor("res_h", [DI, TOK], BF16, kind="Internal")
    g.w1t_h = nc.dram_tensor("w1t_h", [DM, FF], BF16, kind="Internal")
    g.w2t_h = nc.dram_tensor("w2t_h", [FF, DM], BF16, kind="Internal")
    g.xi_h = nc.dram_tensor("xi_h", [DI, TOK], BF16, kind="Internal")
    g.y3_h = nc.dram_tensor("y3_h", [DI, TOK], BF16, kind="Internal")

    DUW = DIT * CH
    g.du_sh, g.du_sn, g.y_sh, g.y_sn = {}, {}, {}, {}
    for r in range(2):
        for cc in range(NCH):
            g.du_sh[(r, cc)] = nc.dram_tensor(f"du_sh_{r}_{cc}", [128, 2 * DUW + 2 * CH],
                                              BF16, kind="Internal", addr_space="Shared")
            g.du_sn[(r, cc)] = nc.dram_tensor(f"du_sn_{r}_{cc}", [128, 8], I32,
                                              kind="Internal", addr_space="Shared")
    for gc in range(2 * NCH):
        for h in range(2):
            g.y_sh[(gc, h)] = nc.dram_tensor(f"y_sh_{gc}_{h}", [128, 8 * CH], BF16,
                                             kind="Internal", addr_space="Shared")
            g.y_sn[(gc, h)] = nc.dram_tensor(f"y_sn_{gc}_{h}", [128, 8], I32,
                                             kind="Internal", addr_space="Shared")

    for s in ("sLD", "sLG", "sLS", "sTE", "sVE", "sSE", "sST", "sBAR", "sGP",
              "sDU0", "sDU1", "sGT", "sTL", "sTL1", "sTS", "sTS1"):
        setattr(g, s, Sem(nc, s))
    g.sSP = nc.alloc_semaphore("sSP")

    A = nc.alloc_sbuf_tensor
    g.c_ident = A("c_ident", [128, 128], BF16)
    g.c_bq = A("c_bq", [128, DT], F32)
    g.c_bk = A("c_bk", [128, DT], F32)
    g.c_bvrow = A("c_bvrow", [1, DM], BF16)
    g.c_b1 = A("c_b1", [128, FT], F32)
    g.c_b2row = A("c_b2row", [1, DM], BF16)
    g.c_binxi = A("c_binxi", [128, DIT], F32)
    g.c_binres = A("c_binres", [128, DIT], F32)
    g.c_wc = A("c_wc", [128, DIT * 4], F32)
    g.c_bcv = A("c_bcv", [128, DIT], F32)
    g.c_bxd = A("c_bxd", [128, DIT], F32)
    g.c_bxbc = A("c_bxbc", [32, 1], F32)
    g.c_borow = A("c_borow", [1, DM], BF16)
    g.c_A = A("c_A", [128, DIT * NST], F32)
    g.c_D = A("c_D", [128, DIT], F32)
    g.c_rolef = A("c_rolef", [128, 1], F32)
    g.c_rolei = A("c_rolei", [1, 1], I32)
    g.c_noni = A("c_noni", [1, 1], I32)
    g.c_nonrow = A("c_nonrow", [128, 8], I32)
    g.c_ones = A("c_ones", [1, 1024], BF16)
    g.c_ones2 = A("c_ones2", [128, CH], BF16)
    g.c_onescol = A("c_onescol", [128, 1], F32)
    g.c_onesrow = A("c_onesrow", [1, 128], F32)
    g.c_probe = A("c_probe", [128, 8], I32)
    g.c_stats = A("c_stats", [128, 12], F32)
    g.c_mv = A("c_mv", [128, 2], F32)
    g.c_mvs = A("c_mvs", [128, 2 * 12], F32)
    g.c_sqs = A("c_sqs", [128, 12], F32)
    g.c_rsds = A("c_rsds", [128, 12], F32)
    g.ln_ev = [0]
    g.c_rstd = A("c_rstd", [128, 1], F32)
    g.c_sq = A("c_sq", [128, 1], F32)
    g.c_thr1 = A("c_thr1", [128, 2], F32)
    g.c_thr2 = A("c_thr2", [128, 2], F32)
    g.c_racc = A("c_racc", [128, 1], F32)
    g.c_hprev = A("c_hprev", [128, 8 * NST], F32)
    g.c_krow = A("c_krow", [128, 8], F32)
    for i, vv in enumerate([1e-5, 1.0 / 32.0, 1.0, 0.5, 2.0, 0.25]):
        nc.const_aps.register(F32, vv, g.c_krow[:, i:i + 1])

    emit(g)

    return nc, g


def barrier(g, blk):
    blk.flush_all()
    dma_waits = [(g.sLD, g.sLD.n), (g.sLG, g.sLG.n), (g.sLS, g.sLS.n),
                 (g.sST, g.sST.n), (g.sDU0, g.sDU0.n), (g.sDU1, g.sDU1.n),
                 (g.sGT, g.sGT.n), (g.sTL, g.sTL.n), (g.sTL1, g.sTL1.n),
                 (g.sTS, g.sTS.n), (g.sTS1, g.sTS1.n)]
    target = g.sBAR.n + 5
    for name in ("sync", "vector", "scalar", "tensor", "gpsimd"):
        @getattr(blk, name)
        def _(eng, _t=target):
            if name == "sync":
                for s_, n_ in dma_waits:
                    eng.wait_ge(s_.h, n_)
            eng.nop().then_inc(g.sBAR.h, 1)
            eng.wait_ge(g.sBAR.h, _t)
    g.sBAR.n = target



def ln_group(g, blk, tiles, xntm, ps_tr):
    n = len(tiles)
    for sl, (src_fn, _d) in enumerate(tiles):
        @blk.vector
        def _(v, src_fn=src_fn, sl=sl):
            src = src_fn()
            v.bn_stats(g.c_stats[:, 0:6], src[:, 0:512])
            v.bn_stats(g.c_stats[:, 6:12], src[:, 512:1024])
            v.drain()
            v.bn_aggr(g.c_mvs[:, 2 * sl:2 * sl + 2],
                      g.c_stats[:, 0:12]).then_inc(g.sVE.h, 1)
        g.sVE.n += 1
    m1 = g.sVE.n
    @blk.scalar
    def _(s):
        s.wait_ge(g.sVE.h, m1)
        s.activation(g.c_sqs[:, 0:n],
                     g.c_mvs[:, 1:2 * n].rearrange("p (a b) -> p a b", b=2)[:, :, 0:1]
                     if False else g.c_mvs[:].rearrange("p (a b) -> p a b", b=2)[:, 0:n, 1:2],
                     AF.Sqrt, bias=EPS, scale=1.0).then_inc(g.sSE.h, 1)
    g.sSE.n += 1
    m2 = g.sSE.n
    @blk.vector
    def _(v):
        v.wait_ge(g.sSE.h, m2)
        v.reciprocal(g.c_rsds[:, 0:n], g.c_sqs[:, 0:n]).then_inc(g.sVE.h, 1)
    g.sVE.n += 1
    rdone = g.sVE.n
    prev_tr = [0]
    for sl, (src_fn, dst_fn) in enumerate(tiles):
        @blk.vector
        def _(v, src_fn=src_fn, sl=sl, ptr=prev_tr[0]):
            src = src_fn()
            v.wait_ge(g.sVE.h, rdone)
            if ptr:
                v.wait_ge(g.sTE.h, ptr)
            v.drain()
            v.tensor_scalar(xntm[:], src, g.c_mvs[:, 2 * sl:2 * sl + 1],
                            g.c_rsds[:, sl:sl + 1],
                            op0=AluOpType.subtract, op1=AluOpType.mult).then_inc(g.sVE.h, 1)
        g.sVE.n += 1
        m3 = g.sVE.n
        @blk.tensor
        def _(t, prev_ev=g.ln_ev[0]):
            t.wait_ge(g.sVE.h, m3)
            if prev_ev:
                t.wait_ge(g.sSE.h, prev_ev)
            for dc in range(DT):
                inst = t.transpose(ps_tr[:, dc * 128:(dc + 1) * 128],
                                   xntm[:, dc * 128:(dc + 1) * 128], g.c_ident[:])
            inst.then_inc(g.sTE.h, 1)
        g.sTE.n += 1
        m4 = g.sTE.n
        prev_tr[0] = m4
        @blk.scalar
        def _(s, dst_fn=dst_fn):
            s.wait_ge(g.sTE.h, m4)
            for dc in range(DT):
                inst = s.activation(dst_fn(dc), ps_tr[:, dc * 128:(dc + 1) * 128],
                                    AF.Copy, bias=0.0, scale=1.0)
            inst.then_inc(g.sSE.h, 1)
        g.sSE.n += 1
        g.ln_ev[0] = g.sSE.n



def emit_p0(g, blk):
    din = g.din
    @blk.sync
    def _(sync):
        for name, dst in (
            ("ident", g.c_ident), ("bq", g.c_bq), ("bk", g.c_bk),
            ("bv_row", g.c_bvrow), ("b1", g.c_b1), ("b2_row", g.c_b2row),
            ("b_in_xi", g.c_binxi), ("b_in_res", g.c_binres),
            ("wc", g.c_wc), ("bcv", g.c_bcv),
            ("b_x_d", g.c_bxd), ("b_x_bc", g.c_bxbc), ("bo_row", g.c_borow),
            ("alog", g.c_A), ("dpar", g.c_D), ("role_f", g.c_rolef),
            ("role", g.c_rolei), ("nonce_i", g.c_noni), ("nonce_row", g.c_nonrow),
            ("krow", g.c_krow),
        ):
            g.sLD.inc(sync.dma_start(dst[:], din[name].ap()), 16)
    ld0 = g.sLD.n
    @blk.gpsimd
    def _(gp):
        gp.memset(g.c_ones[:], 1.0)
        gp.memset(g.c_onescol[:], 1.0)
        gp.memset(g.c_onesrow[:], 1.0)
        gp.memset(g.c_ones2[:], 1.0)
        gp.memset(g.c_hprev[:], 0.0).then_inc(g.sGP.h, 1)
    g.sGP.n += 1
    @blk.scalar
    def _(s):
        s.wait_ge(g.sLD.h, ld0)
        s.activation(g.c_A[:], g.c_A[:], AF.Exp, bias=0.0, scale=1.0).then_inc(g.sSE.h, 1)
    g.sSE.n += 1
    mA = g.sSE.n
    @blk.vector
    def _(v):
        v.wait_ge(g.sSE.h, mA)
        v.tensor_scalar(g.c_A[:], g.c_A[:], -1.0, None, op0=AluOpType.mult)
        v.drain()
        v.nop().then_inc(g.sVE.h, 1)
    g.sVE.n += 1



def emit_p1(g, blk):
    nc = g.nc
    din = g.din
    with ExitStack() as _es:
        sb_xwin = _es.enter_context(nc.sbuf_tensor("sb_xwin", [128, 8 * DM], F32))
        sb_xntm = _es.enter_context(nc.sbuf_tensor("sb_xntm", [128, DM], BF16))
        sb_xnt = _es.enter_context(nc.sbuf_tensor("sb_xnt", [128, DT * 1024], BF16))
        sb_wq = _es.enter_context(nc.sbuf_tensor("sb_wq", [128, DT * DM], BF16))
        sb_wk = _es.enter_context(nc.sbuf_tensor("sb_wk", [128, DT * DM], BF16))
        sb_wv = _es.enter_context(nc.sbuf_tensor("sb_wv", [128, DT * DM], BF16))
        sb_qT = _es.enter_context(nc.sbuf_tensor("sb_qT", [128, DT * CH], BF16))
        sb_kT = _es.enter_context(nc.sbuf_tensor("sb_kT", [128, DT * 1024], BF16))
        sb_v = _es.enter_context(nc.sbuf_tensor("sb_v", [128, 8 * DM], BF16))
        sb_mask = _es.enter_context(nc.sbuf_tensor("sb_mask", [128, 4 * 768], BF16))
        sb_exp = _es.enter_context(nc.sbuf_tensor("sb_exp", [128, 768], BF16))
        sb_rsum = _es.enter_context(nc.sbuf_tensor("sb_rsum", [128, 1], F32))
        sb_rrec = _es.enter_context(nc.sbuf_tensor("sb_rrec", [128, 1], F32))
        sb_wT = _es.enter_context(nc.sbuf_tensor("sb_wT", [128, 768], BF16))
        sb_x1c = _es.enter_context(nc.sbuf_tensor("sb_x1c", [128, 4 * DM], F32))
        ps_tr = _es.enter_context(nc.psum_tensor("ps_tr", [128, 1024], BF16))
        ps_a = _es.enter_context(nc.psum_tensor("ps_a", [128, 1024], F32))
        ps_b = _es.enter_context(nc.psum_tensor("ps_b", [128, 1024], F32))
        ps_c = _es.enter_context(nc.psum_tensor("ps_c", [128, 1024], F32))
        ps_wt = _es.enter_context(nc.psum_tensor("ps_wt", [128, 768], BF16))

        @blk.gpsimd
        def _(gp):
            for name, dst in (("wq", sb_wq), ("wk", sb_wk), ("wv", sb_wv)):
                for kd in range(DT):
                    g.sLG.inc(gp.dma_start(dst[:, kd * DM:(kd + 1) * DM],
                                           din[name].ap()[kd * 128:(kd + 1) * 128, :]), 16)
        w_loaded = g.sLG.n

        for ci, (t0, ntok) in enumerate(CHUNKS):
            winn = ntok + 512
            wtiles = winn // 128
            nqb = ntok // 256
            row0 = t0 - 256 + 512
            ve_prev = g.sVE.n
            te_prev = g.sTE.n
            st_prev = g.sST.n
            @blk.sync
            def _(sync, row0=row0, wtiles=wtiles, ci=ci, nqb=nqb):
                sync.wait_ge(g.sVE.h, ve_prev)
                sync.wait_ge(g.sTE.h, te_prev)
                for r in range(wtiles):
                    g.sLD.inc(sync.dma_start(
                        sb_xwin[:, r * DM:(r + 1) * DM],
                        din["x_in"].ap()[row0 + r * 128: row0 + (r + 1) * 128, :]), 16)
                qb0 = 0 if ci == 0 else 1 + (ci - 1) * 2
                for qb in range(nqb):
                    for qt in range(2):
                        sc = ((qb0 + qb) * 2 + qt) * 768
                        g.sLD.inc(sync.dma_start(
                            sb_mask[:, (qb * 2 + qt) * 768:(qb * 2 + qt + 1) * 768],
                            din["masks"].ap()[:, sc:sc + 768]), 16)
            xm_loaded = g.sLD.n

            @blk.vector
            def _(v):
                v.wait_ge(g.sLD.h, xm_loaded)
                v.nop().then_inc(g.sVE.h, 1)
            g.sVE.n += 1

            ln_group(g, blk,
                     [((lambda r=r: sb_xwin[:, r * DM:(r + 1) * DM]),
                       (lambda dc, r=r: sb_xnt[:, dc * 1024 + r * 128: dc * 1024 + r * 128 + 128]))
                      for r in range(wtiles)],
                     sb_xntm, ps_tr)
            ln_done = g.sSE.n

            q_evacs = []
            for m in range(DT):
                @blk.tensor
                def _(t, m=m, ntok=ntok):
                    if m == 0:
                        t.wait_ge(g.sSE.h, ln_done)
                        t.wait_ge(g.sLG.h, w_loaded)
                        t.wait_ge(g.sGP.h, 1)
                    if m >= 2:
                        t.wait_ge(g.sSE.h, q_evacs[m - 2])
                    p0 = (m % 2) * 512
                    for kd in range(DT):
                        inst = t.matmul(ps_a[:, p0:p0 + ntok],
                                 sb_wq[:, kd * DM + m * 128: kd * DM + m * 128 + 128],
                                 sb_xnt[:, kd * 1024 + 256: kd * 1024 + 256 + ntok],
                                 start=(kd == 0), stop=(kd == DT - 1))
                    inst.then_inc(g.sTE.h, 1)
                g.sTE.n += 1
                mm = g.sTE.n
                @blk.scalar
                def _(s, m=m, ntok=ntok):
                    s.wait_ge(g.sTE.h, mm)
                    p0 = (m % 2) * 512
                    s.activation(sb_qT[:, m * CH: m * CH + ntok], ps_a[:, p0:p0 + ntok],
                                 AF.Identity, bias=g.c_bq[:, m:m + 1], scale=1.0).then_inc(g.sSE.h, 1)
                g.sSE.n += 1
                q_evacs.append(g.sSE.n)

            k_evacs = []
            for m in range(DT):
                @blk.tensor
                def _(t, m=m, winn=winn):
                    if m == 0:
                        t.wait_ge(g.sSE.h, q_evacs[-1])
                    if m >= 2:
                        t.wait_ge(g.sSE.h, k_evacs[m - 2])
                    ps = ps_b if (m % 2 == 0) else ps_a
                    for f0 in range(0, winn, 512):
                        fw = min(512, winn - f0)
                        for kd in range(DT):
                            inst = t.matmul(ps[:, f0:f0 + fw],
                                     sb_wk[:, kd * DM + m * 128: kd * DM + m * 128 + 128],
                                     sb_xnt[:, kd * 1024 + f0: kd * 1024 + f0 + fw],
                                     start=(kd == 0), stop=(kd == DT - 1))
                    inst.then_inc(g.sTE.h, 1)
                g.sTE.n += 1
                mm = g.sTE.n
                @blk.scalar
                def _(s, m=m, winn=winn):
                    s.wait_ge(g.sTE.h, mm)
                    ps = ps_b if (m % 2 == 0) else ps_a
                    s.activation(sb_kT[:, m * 1024: m * 1024 + winn], ps[:, 0:winn],
                                 AF.Identity, bias=g.c_bk[:, m:m + 1], scale=1.0).then_inc(g.sSE.h, 1)
                g.sSE.n += 1
                k_evacs.append(g.sSE.n)

            v_evacs = []
            for r in range(wtiles):
                @blk.tensor
                def _(t, r=r):
                    if r == 0:
                        t.wait_ge(g.sSE.h, k_evacs[-1])
                    if r >= 2:
                        t.wait_ge(g.sSE.h, v_evacs[r - 2])
                    ps = ps_c if (r % 2 == 0) else ps_b
                    for hf in range(2):
                        for kd in range(DT):
                            t.matmul(ps[:, hf * 512:(hf + 1) * 512],
                                     sb_xnt[:, kd * 1024 + r * 128: kd * 1024 + r * 128 + 128],
                                     sb_wv[:, kd * DM + hf * 512: kd * DM + (hf + 1) * 512],
                                     start=(kd == 0), stop=False)
                        inst = t.matmul(ps[:, hf * 512:(hf + 1) * 512],
                                 g.c_ones[0:1, 0:128],
                                 g.c_bvrow[0:1, hf * 512:(hf + 1) * 512],
                                 start=False, stop=True)
                    inst.then_inc(g.sTE.h, 1)
                g.sTE.n += 1
                mm = g.sTE.n
                @blk.scalar
                def _(s, r=r):
                    s.wait_ge(g.sTE.h, mm)
                    ps = ps_c if (r % 2 == 0) else ps_b
                    s.activation(sb_v[:, r * DM:(r + 1) * DM], ps[:, 0:DM],
                                 AF.Copy, bias=0.0, scale=1.0).then_inc(g.sSE.h, 1)
                g.sSE.n += 1
                v_evacs.append(g.sSE.n)

            prev_exp = None
            prev_wts = None
            prev_wts_v = 0
            prev_x1 = None
            prev_wdone = [0]
            for qb in range(nqb):
                for qt in range(2):
                    @blk.tensor
                    def _(t, qb=qb, qt=qt, pe=prev_exp):
                        t.wait_ge(g.sSE.h, v_evacs[-1])
                        if pe is not None:
                            t.wait_ge(g.sSE.h, pe)
                        for f0, fw in ((0, 512), (512, 256)):
                            for kd in range(DT):
                                t.matmul(ps_a[:, f0:f0 + fw],
                                         sb_qT[:, kd * CH + qb * 256 + qt * 128:
                                               kd * CH + qb * 256 + qt * 128 + 128],
                                         sb_kT[:, kd * 1024 + qb * 256 + f0:
                                               kd * 1024 + qb * 256 + f0 + fw],
                                         start=(kd == 0), stop=False)
                            inst = t.matmul(ps_a[:, f0:f0 + fw], g.c_ident[:],
                                     sb_mask[:, (qb * 2 + qt) * 768 + f0:
                                             (qb * 2 + qt) * 768 + f0 + fw],
                                     start=False, stop=True)
                        inst.then_inc(g.sTE.h, 1)
                    g.sTE.n += 1
                    sc_done = g.sTE.n
                    pw2 = prev_wdone[0]
                    @blk.scalar
                    def _(s, pw2=pw2):
                        s.wait_ge(g.sTE.h, sc_done)
                        if pw2:
                            s.wait_ge(g.sVE.h, pw2)
                        s.activation(sb_exp[:], ps_a[:, 0:768], AF.Exp, bias=0.0,
                                     scale=1.0 / 32.0, accum_out=sb_rsum[:]).then_inc(g.sSE.h, 1)
                    g.sSE.n += 1
                    prev_exp = g.sSE.n
                    @blk.vector
                    def _(v):
                        v.wait_ge(g.sSE.h, prev_exp)
                        v.reciprocal(sb_rrec[:], sb_rsum[:])
                        v.drain()
                        v.tensor_scalar(sb_exp[:], sb_exp[:], sb_rrec[:], None,
                                        op0=AluOpType.mult).then_inc(g.sVE.h, 1)
                    g.sVE.n += 1
                    w_done = g.sVE.n
                    prev_wdone[0] = w_done
                    @blk.tensor
                    def _(t, pw=prev_wts_v):
                        t.wait_ge(g.sVE.h, w_done)
                        if pw:
                            t.wait_ge(g.sVE.h, pw)
                        for j in range(6):
                            inst = t.transpose(ps_wt[:, j * 128:(j + 1) * 128],
                                        sb_exp[:, j * 128:(j + 1) * 128], g.c_ident[:])
                        inst.then_inc(g.sTE.h, 1)
                    g.sTE.n += 1
                    wt_done = g.sTE.n
                    @blk.vector
                    def _(v):
                        v.wait_ge(g.sTE.h, wt_done)
                        v.tensor_copy(sb_wT[:], ps_wt[:, 0:768]).then_inc(g.sVE.h, 1)
                    g.sVE.n += 1
                    prev_wts_v = g.sVE.n
                    wts_done = g.sVE.n
                    @blk.tensor
                    def _(t, qb=qb, px=prev_x1):
                        t.wait_ge(g.sVE.h, wts_done)
                        if px is not None:
                            t.wait_ge(g.sVE.h, px)
                        for hf in range(2):
                            for j in range(6):
                                inst = t.matmul(ps_c[:, hf * 512:(hf + 1) * 512],
                                         sb_wT[:, j * 128:(j + 1) * 128],
                                         sb_v[:, (qb * 2 + j) * DM + hf * 512:
                                               (qb * 2 + j) * DM + (hf + 1) * 512],
                                         start=(j == 0), stop=(j == 5))
                        inst.then_inc(g.sTE.h, 1)
                    g.sTE.n += 1
                    av_done = g.sTE.n
                    @blk.vector
                    def _(v, qb=qb, qt=qt, st_prev=st_prev):
                        v.wait_ge(g.sTE.h, av_done)
                        if qb == 0 and qt == 0:
                            v.wait_ge(g.sST.h, st_prev)
                        rt = 2 + qb * 2 + qt
                        dstc = (qb * 2 + qt) * DM
                        v.tensor_tensor(sb_x1c[:, dstc:dstc + DM],
                                        sb_xwin[:, rt * DM:(rt + 1) * DM],
                                        ps_c[:, 0:DM], op=AluOpType.add).then_inc(g.sVE.h, 1)
                    g.sVE.n += 1
                    prev_x1 = g.sVE.n
            x1_done = g.sVE.n
            @blk.sync
            def _(sync, t0=t0, ntok=ntok):
                sync.wait_ge(g.sVE.h, x1_done)
                r0 = t0 + 256
                for rt in range(ntok // 128):
                    g.sST.inc(sync.dma_start(
                        g.x1_h.ap()[r0 + rt * 128: r0 + (rt + 1) * 128, :],
                        sb_x1c[:, rt * DM:(rt + 1) * DM]), 16)
                if DEBUG:
                    for rt in range(ntok // 128):
                        g.sST.inc(sync.dma_start(
                            g.dbg_x1.ap()[r0 + rt * 128: r0 + (rt + 1) * 128, :],
                            sb_x1c[:, rt * DM:(rt + 1) * DM]), 16)
            blk.flush_all()
    g.ms_x1_stored = g.sST.n



SPIN_CELLS = [0, 12, 28, 40, 64, 76, 88, 92]


def emit_spin(g, blk, sent_dram):
    @blk.dsync
    def _(sync):
        e = g.rv_nonce
        pr = g.r_probe
        sync.sem_clear(g.sSP)
        sync.dma_start(g.c_probe[:], sent_dram.ap()).then_inc(g.sSP, 16)
        sync.wait_ge(g.sSP, 16)
        nread = [1]
        for ci, row in enumerate(SPIN_CELLS):
            sync.reg_load(pr, g.c_probe[row:row + 1, 0:1])
            nout = 6 if ci == 0 else 1
            for o in range(nout):
                with sync.If_ne(pr, e):
                    for i in range(10 if ci == 0 else 1):
                        with sync.If_ne(pr, e):
                            sync.nop(cycle_cnt=4000)
                            sync.dma_start(g.c_probe[:], sent_dram.ap()).then_inc(g.sSP, 16)
                            nread[0] += 1
                            sync.wait_ge(g.sSP, 16 * nread[0])
                            sync.reg_load(pr, g.c_probe[row:row + 1, 0:1])
                        sync.end_ifs()
                sync.end_ifs()



def emit_ternarize(g, blk, w_dram, ntiles, width, thr_dst):
    uid = g.uid = getattr(g, "uid", 0) + 1
    """Pass A: thr = 0.5*max(mean|W|, 1e-8) into thr_dst ([128,2]: +thr, -thr).
    Streams [128, sw] subtiles double-buffered on parity DMA sems."""
    nc = g.nc
    ch2 = 2 if width > 2048 else 1
    sw = width // ch2
    nsub = ntiles * ch2
    with ExitStack() as _es:
        sb_w = _es.enter_context(nc.sbuf_tensor(f"sb_tw{uid}", [128, 2 * sw], F32))
        ps_t = _es.enter_context(nc.psum_tensor(f"ps_tern{uid}", [128, 2], F32))
        te_now = g.sTE.n
        ve_entry = g.sVE.n
        @blk.gpsimd
        def _(gp):
            gp.wait_ge(g.sTE.h, te_now)
            gp.memset(g.c_racc[:], 0.0).then_inc(g.sGP.h, 1)
        g.sGP.n += 1
        m0 = g.sGP.n
        vms = []
        for i in range(nsub):
            tl = g.sTL if i % 2 == 0 else g.sTL1
            @blk.sync
            def _(sync, i=i, tl=tl):
                if i >= 2:
                    sync.wait_ge(g.sVE.h, vms[-2])
                else:
                    sync.wait_ge(g.sVE.h, ve_entry)
                tl.inc(sync.dma_start(
                    sb_w[:, (i % 2) * sw:(i % 2 + 1) * sw],
                    w_dram.ap()[(i // ch2) * 128:(i // ch2 + 1) * 128,
                                (i % ch2) * sw:(i % ch2 + 1) * sw]), 16)
            ld = tl.n
            @blk.vector
            def _(v, i=i, tl=tl, ld=ld):
                v.wait_ge(tl.h, ld)
                if i == 0:
                    v.wait_ge(g.sGP.h, m0)
                src3 = sb_w[:, (i % 2) * sw:(i % 2 + 1) * sw].rearrange(
                    "p (a b) -> p a b", b=512)
                v.tensor_reduce(g.c_sqs[:, 0:sw // 512], src3,
                                axis=mybir.AxisListType.X,
                                op=AluOpType.add, apply_absolute_value=True)
                v.drain()
                v.tensor_reduce(g.c_sq[:], g.c_sqs[:, 0:sw // 512],
                                axis=mybir.AxisListType.X, op=AluOpType.add)
                v.drain()
                v.tensor_tensor(g.c_racc[:], g.c_racc[:], g.c_sq[:],
                                op=AluOpType.add).then_inc(g.sVE.h, 1)
            g.sVE.n += 1
            vms.append(g.sVE.n)
        @blk.tensor
        def _(t):
            t.wait_ge(g.sVE.h, vms[-1])
            t.matmul(ps_t[0:1, 0:1], g.c_onescol[:], g.c_racc[:],
                     start=True, stop=True).then_inc(g.sTE.h, 1)
        g.sTE.n += 1
        m1 = g.sTE.n
        @blk.scalar
        def _(s):
            s.wait_ge(g.sTE.h, m1)
            s.activation(g.c_sq[0:1, 0:1], ps_t[0:1, 0:1], AF.Copy, bias=0.0,
                         scale=float(1.0 / (ntiles * 128 * width))).then_inc(g.sSE.h, 1)
        g.sSE.n += 1
        m2 = g.sSE.n
        @blk.vector
        def _(v):
            v.wait_ge(g.sSE.h, m2)
            v.tensor_scalar(g.c_sq[0:1, 0:1], g.c_sq[0:1, 0:1], 1e-8, 0.5,
                            op0=AluOpType.max, op1=AluOpType.mult).then_inc(g.sVE.h, 1)
        g.sVE.n += 1
        m3 = g.sVE.n
        @blk.tensor
        def _(t):
            t.wait_ge(g.sVE.h, m3)
            t.matmul(ps_t[:, 1:2], g.c_onesrow[0:1, :], g.c_sq[0:1, 0:1],
                     start=True, stop=True).then_inc(g.sTE.h, 1)
        g.sTE.n += 1
        m4 = g.sTE.n
        @blk.scalar
        def _(s):
            s.wait_ge(g.sTE.h, m4)
            s.activation(thr_dst[:, 0:1], ps_t[:, 1:2], AF.Copy, bias=0.0, scale=1.0)
            s.activation(thr_dst[:, 1:2], ps_t[:, 1:2], AF.Copy, bias=0.0,
                         scale=-1.0).then_inc(g.sSE.h, 1)
        g.sSE.n += 1
        blk.flush_all()


def emit_tern_apply(g, blk, w_dram, ntiles, width, thr, dst_sb, dst_h):
    nc = g.nc
    uid = g.uid = getattr(g, "uid", 0) + 1
    ve_entry2 = g.sVE.n
    ch2 = 2 if width > 2048 else 1
    sw = width // ch2
    nsub = ntiles * ch2
    with ExitStack() as _es:
        sb_w = _es.enter_context(nc.sbuf_tensor(f"sb_tw2_{uid}", [128, 2 * sw], F32))
        sb_pp = _es.enter_context(nc.sbuf_tensor(f"sb_tpp{uid}", [128, 2 * sw], F32))
        thr_done = g.sSE.n
        vms = []
        for i in range(nsub):
            tl = g.sTL if i % 2 == 0 else g.sTL1
            @blk.sync
            def _(sync, i=i, tl=tl):
                if i >= 2:
                    sync.wait_ge(g.sVE.h, vms[-2])
                else:
                    sync.wait_ge(g.sVE.h, ve_entry2)
                tl.inc(sync.dma_start(
                    sb_w[:, (i % 2) * sw:(i % 2 + 1) * sw],
                    w_dram.ap()[(i // ch2) * 128:(i // ch2 + 1) * 128,
                                (i % ch2) * sw:(i % ch2 + 1) * sw]), 16)
            ld = tl.n
            @blk.vector
            def _(v, i=i, tl=tl, ld=ld):
                v.wait_ge(tl.h, ld)
                if i == 0:
                    v.wait_ge(g.sSE.h, thr_done)
                p = (i % 2)
                srcw = sb_w[:, p * sw:(p + 1) * sw]
                dcol = (i // ch2) * width + (i % ch2) * sw
                v.tensor_scalar(sb_pp[:, p * sw:(p + 1) * sw], srcw,
                                thr[:, 0:1], None, op0=AluOpType.is_ge)
                v.tensor_scalar(dst_sb[:, dcol:dcol + sw], srcw,
                                thr[:, 1:2], None, op0=AluOpType.is_le)
                v.drain()
                v.tensor_tensor(dst_sb[:, dcol:dcol + sw],
                                sb_pp[:, p * sw:(p + 1) * sw],
                                dst_sb[:, dcol:dcol + sw],
                                op=AluOpType.subtract).then_inc(g.sVE.h, 1)
            g.sVE.n += 1
            vms.append(g.sVE.n)
            vm = g.sVE.n
            ts_ = g.sTS if i % 2 == 0 else g.sTS1
            @blk.sync
            def _(sync, i=i, vm=vm, ts_=ts_):
                sync.wait_ge(g.sVE.h, vm)
                ts_.inc(sync.dma_start(
                    dst_h.ap()[(i // ch2) * 128:(i // ch2 + 1) * 128,
                               (i % ch2) * sw:(i % ch2 + 1) * sw],
                    dst_sb[:, (i // ch2) * width + (i % ch2) * sw:
                           (i // ch2) * width + (i % ch2) * sw + sw]), 16)



def emit_ffn(g, blk, first):
    nc = g.nc
    din = g.din
    src_h = g.x1_h if first else g.x3_h
    with ExitStack() as _es:
        sb_w1 = _es.enter_context(nc.sbuf_tensor(f"sb_w1t{first}", [128, DT * FF], BF16))
        sb_w2 = _es.enter_context(nc.sbuf_tensor(f"sb_w2t{first}", [128, FT * DM], BF16))

        if first:
            emit_ternarize(g, blk, din["w1T"], DT, FF, g.c_thr1)
            emit_ternarize(g, blk, din["w2T"], FT, DM, g.c_thr2)
            emit_tern_apply(g, blk, din["w1T"], DT, FF, g.c_thr1, sb_w1, g.w1t_h)
            emit_tern_apply(g, blk, din["w2T"], FT, DM, g.c_thr2, sb_w2, g.w2t_h)
        else:
            @blk.sync
            def _(sync):
                for i in range(DT):
                    g.sLD.inc(sync.dma_start(sb_w1[:, i * FF:(i + 1) * FF],
                                             g.w1t_h.ap()[i * 128:(i + 1) * 128, :]), 16)
                for i in range(FT):
                    g.sLD.inc(sync.dma_start(sb_w2[:, i * DM:(i + 1) * DM],
                                             g.w2t_h.ap()[i * 128:(i + 1) * 128, :]), 16)
        w_ready_v = g.sVE.n
        w_ready_l = g.sLD.n
        w_ready_s = g.sST.n
        sb_xc = _es.enter_context(nc.sbuf_tensor(f"sb_fxc{first}", [128, 4 * DM], F32))
        sb_xntm = _es.enter_context(nc.sbuf_tensor(f"sb_fxntm{first}", [128, DM], BF16))
        sb_xnt = _es.enter_context(nc.sbuf_tensor(f"sb_fxnt{first}", [128, DT * CH], BF16))
        sb_hT = _es.enter_context(nc.sbuf_tensor(f"sb_fhT{first}", [128, FT * 512], BF16))
        ps_tr = _es.enter_context(nc.psum_tensor(f"ps_ftr{first}", [128, 1024], BF16))
        ps_h = _es.enter_context(nc.psum_tensor(f"ps_fh{first}", [128, 1024], F32))
        ps_o = _es.enter_context(nc.psum_tensor(f"ps_fo{first}", [128, 2048], F32))

        chunk_list = CHUNKS if first else CHUNKS[1:]
        prev_resid = 0
        prev_store = g.sST.n
        prev_te = g.sTE.n
        for ci, (t0, ntok) in enumerate(chunk_list):
            rtiles = ntok // 128
            r0 = t0 + 256 if first else t0
            ve_prev = g.sVE.n
            te_prev = g.sTE.n
            @blk.sync
            def _(sync, r0=r0, rtiles=rtiles, pstore=prev_store):
                sync.wait_ge(g.sVE.h, ve_prev)
                sync.wait_ge(g.sTE.h, te_prev)
                if pstore:
                    sync.wait_ge(g.sST.h, pstore)
                if first:
                    sync.wait_ge(g.sST.h, g.ms_x1_stored)
                for r in range(rtiles):
                    g.sLD.inc(sync.dma_start(
                        sb_xc[:, r * DM:(r + 1) * DM],
                        src_h.ap()[r0 + r * 128: r0 + (r + 1) * 128, :]), 16)
            xl = g.sLD.n
            @blk.vector
            def _(v):
                v.wait_ge(g.sLD.h, xl)
                v.nop().then_inc(g.sVE.h, 1)
            g.sVE.n += 1
            ln_group(g, blk,
                     [((lambda r=r: sb_xc[:, r * DM:(r + 1) * DM]),
                       (lambda dc, r=r: sb_xnt[:, dc * CH + r * 128: dc * CH + r * 128 + 128]))
                      for r in range(rtiles)],
                     sb_xntm, ps_tr)
            ln_done = g.sSE.n

            h_evacs = []
            for mf in range(FT):
                @blk.tensor
                def _(t, mf=mf, ntok=ntok):
                    if mf == 0:
                        t.wait_ge(g.sSE.h, ln_done)
                        t.wait_ge(g.sVE.h, w_ready_v)
                        t.wait_ge(g.sLD.h, w_ready_l)
                    if mf >= 2:
                        t.wait_ge(g.sSE.h, h_evacs[mf - 2])
                    p0 = (mf % 2) * 512
                    for kd in range(DT):
                        inst = t.matmul(ps_h[:, p0:p0 + ntok],
                                        sb_w1[:, kd * FF + mf * 128: kd * FF + mf * 128 + 128],
                                        sb_xnt[:, kd * CH: kd * CH + ntok],
                                        start=(kd == 0), stop=(kd == DT - 1))
                    inst.then_inc(g.sTE.h, 1)
                g.sTE.n += 1
                mm = g.sTE.n
                @blk.scalar
                def _(s, mf=mf, ntok=ntok, mm=mm):
                    s.wait_ge(g.sTE.h, mm)
                    p0 = (mf % 2) * 512
                    s.activation(sb_hT[:, mf * 512: mf * 512 + ntok], ps_h[:, p0:p0 + ntok],
                                 AF.Gelu, bias=g.c_b1[:, mf:mf + 1],
                                 scale=1.0).then_inc(g.sSE.h, 1)
                g.sSE.n += 1
                h_evacs.append(g.sSE.n)
            for tt in range(ntok // 128):
                @blk.tensor
                def _(t, tt=tt, pr=prev_resid):
                    t.wait_ge(g.sSE.h, h_evacs[-1])
                    if pr:
                        t.wait_ge(g.sVE.h, pr)
                    for fh in range(2):
                        bk = ((tt % 2) * 2 + fh) * 512
                        for kf in range(FT):
                            t.matmul(ps_o[:, bk:bk + 512],
                                     sb_hT[:, kf * 512 + tt * 128: kf * 512 + tt * 128 + 128],
                                     sb_w2[:, kf * DM + fh * 512: kf * DM + (fh + 1) * 512],
                                     start=(kf == 0), stop=False)
                        inst = t.matmul(ps_o[:, bk:bk + 512],
                                        g.c_ones[0:1, 0:128],
                                        g.c_b2row[0:1, fh * 512:(fh + 1) * 512],
                                        start=False, stop=True)
                    inst.then_inc(g.sTE.h, 1)
                g.sTE.n += 1
                mm2 = g.sTE.n
                @blk.vector
                def _(v, tt=tt, ps0=prev_store, mm2=mm2):
                    v.wait_ge(g.sTE.h, mm2)
                    v.tensor_tensor(sb_xc[:, tt * DM:(tt + 1) * DM],
                                    sb_xc[:, tt * DM:(tt + 1) * DM],
                                    ps_o[:, (tt % 2) * 1024: (tt % 2) * 1024 + 1024],
                                    op=AluOpType.add).then_inc(g.sVE.h, 1)
                g.sVE.n += 1
                prev_resid = g.sVE.n
            done_v = g.sVE.n
            @blk.sync
            def _(sync, t0=t0, rtiles=rtiles, ci=ci):
                sync.wait_ge(g.sVE.h, done_v)
                for r in range(rtiles):
                    if first:
                        g.sST.inc(sync.dma_start(
                            g.x2_h.ap()[t0 + 256 + r * 128: t0 + 256 + (r + 1) * 128, :],
                            sb_xc[:, r * DM:(r + 1) * DM]), 16)
                        if DEBUG:
                            g.sST.inc(sync.dma_start(
                                g.dbg_x2.ap()[t0 + 256 + r * 128: t0 + 256 + (r + 1) * 128, :],
                                sb_xc[:, r * DM:(r + 1) * DM]), 16)
                    else:
                        g.sST.inc(sync.dma_start(
                            g.out.ap()[t0 + r * 128: t0 + (r + 1) * 128, :],
                            sb_xc[:, r * DM:(r + 1) * DM]), 16)
            prev_store = g.sST.n
            blk.flush_all()
            blk.flush_all()
    if first:
        g.ms_x2_stored = g.sST.n



def emit_p3a(g, blk):
    nc = g.nc
    din = g.din
    with ExitStack() as _es:
        sb_win = _es.enter_context(nc.sbuf_tensor("sb_win", [128, DT * 2 * DI], BF16))
        sb_xc = _es.enter_context(nc.sbuf_tensor("sb_3xc", [128, 4 * DM], F32))
        sb_xntm = _es.enter_context(nc.sbuf_tensor("sb_3xntm", [128, DM], BF16))
        sb_xnt = _es.enter_context(nc.sbuf_tensor("sb_3xnt", [128, DT * CH], BF16))
        sb_xrh = _es.enter_context(nc.sbuf_tensor("sb_xrh", [128, DIT * (CH + 3)], BF16))
        sb_res = _es.enter_context(nc.sbuf_tensor("sb_3res", [128, 2 * CH], BF16))
        sb_cacc = _es.enter_context(nc.sbuf_tensor("sb_cacc", [128, 2 * CH], F32))
        sb_xi = _es.enter_context(nc.sbuf_tensor("sb_3xi", [128, DIT * CH], BF16))
        ps_tr = _es.enter_context(nc.psum_tensor("ps_3tr", [128, 1024], BF16))
        ps_w = _es.enter_context(nc.psum_tensor("ps_3w", [128, 1024], F32))

        @blk.gpsimd
        def _(gp):
            for i in range(DT):
                g.sLG.inc(gp.dma_start(sb_win[:, i * 2 * DI:(i + 1) * 2 * DI],
                                       din["winT"].ap()[i * 128:(i + 1) * 128, :]), 16)
        win_l = g.sLG.n

        prev_ve = 0
        prev_chunk_st = [0, 0]
        for ci, (t0, ntok) in enumerate(CHUNKS):
            prev_chunk_st[1] = g.sST.n
            rtiles = ntok // 128
            r0 = t0 + 256
            ve_prev = g.sVE.n
            te_prev = g.sTE.n
            @blk.sync
            def _(sync, r0=r0, rtiles=rtiles):
                sync.wait_ge(g.sVE.h, ve_prev)
                sync.wait_ge(g.sTE.h, te_prev)
                sync.wait_ge(g.sST.h, g.ms_x2_stored)
                for r in range(rtiles):
                    g.sLD.inc(sync.dma_start(
                        sb_xc[:, r * DM:(r + 1) * DM],
                        g.x2_h.ap()[r0 + r * 128: r0 + (r + 1) * 128, :]), 16)
            xl = g.sLD.n
            @blk.vector
            def _(v):
                v.wait_ge(g.sLD.h, xl)
                v.nop().then_inc(g.sVE.h, 1)
            g.sVE.n += 1
            rlist = range(rtiles - 1, rtiles) if ci == 0 else range(rtiles)
            ln_group(g, blk,
                     [((lambda r=r: sb_xc[:, r * DM:(r + 1) * DM]),
                       (lambda dc, r=r: sb_xnt[:, dc * CH + r * 128: dc * CH + r * 128 + 128]))
                      for r in rlist],
                     sb_xntm, ps_tr)
            ln_done = g.sSE.n

            if ci == 0:
                evs = []
                for mf in range(DIT):
                    @blk.tensor
                    def _(t, mf=mf):
                        if mf == 0:
                            t.wait_ge(g.sSE.h, ln_done)
                            t.wait_ge(g.sLG.h, win_l)
                        if mf >= 2:
                            t.wait_ge(g.sSE.h, evs[mf - 2])
                        p0 = (mf % 2) * 512
                        for kd in range(DT):
                            inst = t.matmul(ps_w[:, p0:p0 + 128],
                                            sb_win[:, kd * 2 * DI + mf * 128: kd * 2 * DI + mf * 128 + 128],
                                            sb_xnt[:, kd * CH + 128: kd * CH + 256],
                                            start=(kd == 0), stop=(kd == DT - 1))
                        inst.then_inc(g.sTE.h, 1)
                    g.sTE.n += 1
                    mm = g.sTE.n
                    @blk.scalar
                    def _(s, mf=mf):
                        s.wait_ge(g.sTE.h, mm)
                        p0 = (mf % 2) * 512
                        s.activation(sb_xrh[:, mf * (CH + 3): mf * (CH + 3) + 3],
                                     ps_w[:, p0 + 125: p0 + 128], AF.Identity,
                                     bias=g.c_binxi[:, mf:mf + 1], scale=1.0).then_inc(g.sSE.h, 1)
                    g.sSE.n += 1
                    evs.append(g.sSE.n)
                halo_se = g.sSE.n
                @blk.vector
                def _(v):
                    v.wait_ge(g.sSE.h, halo_se)
                    for mf in range(DIT):
                        v.tensor_scalar(sb_xrh[:, mf * (CH + 3): mf * (CH + 3) + 3],
                                        sb_xrh[:, mf * (CH + 3): mf * (CH + 3) + 3],
                                        g.c_rolef[:, 0:1], None, op0=AluOpType.mult)
                    v.nop().then_inc(g.sVE.h, 1)
                g.sVE.n += 1
                prev_ve = g.sVE.n
                continue

            c = ci - 1
            evs = []
            res_stores = []
            prev_chunk_st[0] = prev_chunk_st[1]
            for mf in range(2 * DIT):
                @blk.tensor
                def _(t, mf=mf, pv=prev_ve):
                    if mf == 0:
                        t.wait_ge(g.sSE.h, ln_done)
                        t.wait_ge(g.sLG.h, win_l)
                        t.wait_ge(g.sVE.h, pv)
                    if mf >= 2:
                        t.wait_ge(g.sSE.h, evs[mf - 2])
                    p0 = (mf % 2) * 512
                    for kd in range(DT):
                        inst = t.matmul(ps_w[:, p0:p0 + CH],
                                        sb_win[:, kd * 2 * DI + mf * 128: kd * 2 * DI + mf * 128 + 128],
                                        sb_xnt[:, kd * CH: kd * CH + CH],
                                        start=(kd == 0), stop=(kd == DT - 1))
                    inst.then_inc(g.sTE.h, 1)
                g.sTE.n += 1
                mm = g.sTE.n
                if mf < DIT:
                    @blk.scalar
                    def _(s, mf=mf):
                        s.wait_ge(g.sTE.h, mm)
                        p0 = (mf % 2) * 512
                        s.activation(sb_xrh[:, mf * (CH + 3) + 3: mf * (CH + 3) + 3 + CH],
                                     ps_w[:, p0:p0 + CH], AF.Identity,
                                     bias=g.c_binxi[:, mf:mf + 1], scale=1.0).then_inc(g.sSE.h, 1)
                    g.sSE.n += 1
                else:
                    rb = (mf - DIT) % 2
                    @blk.scalar
                    def _(s, mf=mf, c=c, rb=rb):
                        s.wait_ge(g.sTE.h, mm)
                        if res_stores:
                            s.wait_ge(g.sST.h, res_stores[-1])
                        elif prev_chunk_st[0]:
                            s.wait_ge(g.sST.h, prev_chunk_st[0])
                        p0 = (mf % 2) * 512
                        s.activation(sb_res[:, rb * CH:(rb + 1) * CH], ps_w[:, p0:p0 + CH],
                                     AF.Identity,
                                     bias=g.c_binres[:, mf - DIT:mf - DIT + 1],
                                     scale=1.0).then_inc(g.sSE.h, 1)
                    g.sSE.n += 1
                    se_res = g.sSE.n
                    @blk.sync
                    def _(sync, mf=mf, c=c, se_res=se_res, rb=rb):
                        sync.wait_ge(g.sSE.h, se_res)
                        g.sST.inc(sync.dma_start(
                            g.res_h.ap()[(mf - DIT) * 128:(mf - DIT + 1) * 128,
                                         c * CH:(c + 1) * CH],
                            sb_res[:, rb * CH:(rb + 1) * CH]), 16)
                    res_stores.append(g.sST.n)
                evs.append(g.sSE.n)
            xr_done = g.sSE.n

            cv_dones = []
            for dt in range(DIT):
                @blk.vector
                def _(v, dt=dt):
                    if dt == 0:
                        v.wait_ge(g.sSE.h, xr_done)
                    else:
                        v.wait_ge(g.sSE.h, cv_dones[-1])
                    base = dt * (CH + 3)
                    v.tensor_scalar(sb_cacc[:, 0:CH], sb_xrh[:, base: base + CH],
                                    g.c_wc[:, dt * 4: dt * 4 + 1], None,
                                    op0=AluOpType.mult).then_inc(g.sVE.h, 1)
                    for k in range(1, 4):
                        v.wait_ge(g.sVE.h, g.sVE.n + k)
                        src_half = ((k - 1) % 2) * CH
                        dst_half = (k % 2) * CH
                        v.scalar_tensor_tensor(sb_cacc[:, dst_half:dst_half + CH],
                                               sb_xrh[:, base + k: base + k + CH],
                                               g.c_wc[:, dt * 4 + k: dt * 4 + k + 1],
                                               sb_cacc[:, src_half:src_half + CH],
                                               op0=AluOpType.mult,
                                               op1=AluOpType.add).then_inc(g.sVE.h, 1)
                g.sVE.n += 4
                mcv = g.sVE.n
                @blk.scalar
                def _(s, dt=dt):
                    s.wait_ge(g.sVE.h, mcv)
                    s.activation(sb_xi[:, dt * CH:(dt + 1) * CH], sb_cacc[:, CH:2 * CH],
                                 AF.Silu,
                                 bias=g.c_bcv[:, dt:dt + 1], scale=1.0).then_inc(g.sSE.h, 1)
                g.sSE.n += 1
                cv_dones.append(g.sSE.n)
            @blk.vector
            def _(v):
                v.wait_ge(g.sSE.h, cv_dones[-1])
                for dt in range(DIT):
                    base = dt * (CH + 3)
                    v.tensor_copy(g.c_stats[:, 0:3], sb_xrh[:, base + CH: base + CH + 3])
                    v.drain()
                    v.tensor_copy(sb_xrh[:, base: base + 3], g.c_stats[:, 0:3])
                    v.drain()
                v.nop().then_inc(g.sVE.h, 1)
            g.sVE.n += 1
            prev_ve = g.sVE.n
            @blk.sync
            def _(sync, c=c):
                sync.wait_ge(g.sSE.h, cv_dones[-1])
                for dt in range(DIT):
                    g.sST.inc(sync.dma_start(
                        g.xi_h.ap()[dt * 128:(dt + 1) * 128, c * CH:(c + 1) * CH],
                        sb_xi[:, dt * CH:(dt + 1) * CH]), 16)
                if DEBUG:
                    for dt in range(DIT):
                        g.sST.inc(sync.dma_start(
                            g.dbg_xi.ap()[dt * 128:(dt + 1) * 128, c * CH:(c + 1) * CH],
                            sb_xi[:, dt * CH:(dt + 1) * CH]), 16)
            blk.flush_all()
    g.ms_xi_stored = g.sST.n





def emit_p3b(g, blk):
    nc = g.nc
    din = g.din
    DUW = DIT * CH
    HW = 8 * CH

    with ExitStack() as _es:
        sb_wx = _es.enter_context(nc.sbuf_tensor("sb_wx", [128, DIT * DI], BF16))
        sb_wxbc = _es.enter_context(nc.sbuf_tensor("sb_wxbc", [128, DIT * 32], BF16))
        sb_xic = _es.enter_context(nc.sbuf_tensor("sb_xic", [128, DIT * CH], BF16))
        sb_delta = _es.enter_context(nc.sbuf_tensor("sb_delta", [128, DIT * CH], BF16))
        sb_u = _es.enter_context(nc.sbuf_tensor("sb_u", [128, DIT * CH], BF16))
        sb_t1 = _es.enter_context(nc.sbuf_tensor("sb_t1", [128, DIT * CH], BF16))
        sb_bc = _es.enter_context(nc.sbuf_tensor("sb_bc", [32, 512], BF16))
        ps_p = _es.enter_context(nc.psum_tensor("ps_p", [128, 1024], F32))
        ps_bc = _es.enter_context(nc.psum_tensor("ps_bc", [128, 512], F32))

        @blk.gpsimd
        def _(gp):
            for i in range(DIT):
                g.sLG.inc(gp.dma_start(sb_wx[:, i * DI:(i + 1) * DI],
                                       din["wxT_d"].ap()[i * 128:(i + 1) * 128, :]), 16)
                g.sLG.inc(gp.dma_start(sb_wxbc[:, i * 32:(i + 1) * 32],
                                       din["wxT_bc"].ap()[i * 128:(i + 1) * 128, :]), 16)
        wx_l = g.sLG.n

        for c in range(NCH):
            ve_prev = g.sVE.n
            se_prev = g.sSE.n
            te_prev2 = g.sTE.n
            @blk.sync
            def _(sync, c=c):
                sync.wait_ge(g.sVE.h, ve_prev)
                sync.wait_ge(g.sSE.h, se_prev)
                sync.wait_ge(g.sTE.h, te_prev2)
                sync.wait_ge(g.sST.h, g.ms_xi_stored)
                for dt in range(DIT):
                    g.sLD.inc(sync.dma_start(
                        sb_xic[:, dt * CH:(dt + 1) * CH],
                        g.xi_h.ap()[dt * 128:(dt + 1) * 128, c * CH:(c + 1) * CH]), 16)
            xl = g.sLD.n
            evs = []
            for mt in range(DIT):
                @blk.tensor
                def _(t, mt=mt):
                    if mt == 0:
                        t.wait_ge(g.sLD.h, xl)
                        t.wait_ge(g.sLG.h, wx_l)
                    if mt >= 2:
                        t.wait_ge(g.sSE.h, evs[mt - 2])
                    p0 = (mt % 2) * 512
                    for kdt in range(DIT):
                        inst = t.matmul(ps_p[:, p0:p0 + CH],
                                        sb_wx[:, kdt * DI + mt * 128: kdt * DI + mt * 128 + 128],
                                        sb_xic[:, kdt * CH:(kdt + 1) * CH],
                                        start=(kdt == 0), stop=(kdt == DIT - 1))
                    inst.then_inc(g.sTE.h, 1)
                g.sTE.n += 1
                mm = g.sTE.n
                @blk.scalar
                def _(s, mt=mt):
                    s.wait_ge(g.sTE.h, mm)
                    p0 = (mt % 2) * 512
                    s.activation(sb_t1[:, mt * CH:(mt + 1) * CH], ps_p[:, p0:p0 + CH],
                                 AF.Exp,
                                 bias=g.c_bxd[:, mt:mt + 1], scale=1.0).then_inc(g.sSE.h, 1)
                g.sSE.n += 1
                evs.append(g.sSE.n)
            @blk.scalar
            def _(s):
                s.activation(sb_delta[:], sb_t1[:], AF.Ln, bias=1.0,
                             scale=1.0).then_inc(g.sSE.h, 1)
            g.sSE.n += 1
            evs.append(g.sSE.n)
            @blk.tensor
            def _(t):
                t.wait_ge(g.sSE.h, evs[-1])
                for kdt in range(DIT):
                    inst = t.matmul(ps_bc[0:32, 0:CH],
                                    sb_wxbc[:, kdt * 32:(kdt + 1) * 32],
                                    sb_xic[:, kdt * CH:(kdt + 1) * CH],
                                    start=(kdt == 0), stop=(kdt == DIT - 1))
                inst.then_inc(g.sTE.h, 1)
            g.sTE.n += 1
            mmbc = g.sTE.n
            @blk.scalar
            def _(s):
                s.wait_ge(g.sTE.h, mmbc)
                s.activation(sb_bc[:], ps_bc[0:32, 0:CH], AF.Identity,
                             bias=g.c_bxbc[:, 0:1], scale=1.0).then_inc(g.sSE.h, 1)
            g.sSE.n += 1
            bc_done = g.sSE.n
            @blk.vector
            def _(v):
                v.wait_ge(g.sSE.h, evs[-1])
                for dt in range(DIT):
                    v.tensor_tensor(sb_u[:, dt * CH:(dt + 1) * CH],
                                    sb_delta[:, dt * CH:(dt + 1) * CH],
                                    sb_xic[:, dt * CH:(dt + 1) * CH],
                                    op=AluOpType.mult)
                v.drain()
                v.nop().then_inc(g.sVE.h, 1)
            g.sVE.n += 1
            u_done = g.sVE.n
            @blk.dsync
            def _(sync, c=c):
                sync.wait_ge(g.sVE.h, u_done)
                sync.wait_ge(g.sSE.h, bc_done)
                r = g.r_role
                with sync.If_eq(r, 0):
                    reg = g.du_sh[(0, c)]
                    sn = g.du_sn[(0, c)]
                    sync.dma_start(reg.ap()[:, 0:HW], sb_delta[:, 0:HW]).then_inc(g.sST.h, 16)
                    sync.dma_start(reg.ap()[:, HW:2 * HW], sb_delta[:, HW:2 * HW]).then_inc(g.sST.h, 16)
                    sync.dma_start(reg.ap()[:, DUW:DUW + HW], sb_u[:, 0:HW]).then_inc(g.sST.h, 16)
                    sync.dma_start(reg.ap()[:, DUW + HW:2 * DUW], sb_u[:, HW:2 * HW]).then_inc(g.sST.h, 16)
                    sync.dma_start(reg.ap()[0:32, 2 * DUW:2 * DUW + 512], sb_bc[:]).then_inc(g.sST.h, 16)
                    sync.dma_start(sn.ap(), g.c_nonrow[:]).then_inc(g.sST.h, 16)
                with sync.Else():
                    reg = g.du_sh[(1, c)]
                    sn = g.du_sn[(1, c)]
                    sync.dma_start(reg.ap()[:, 0:HW], sb_delta[:, HW:2 * HW]).then_inc(g.sST.h, 16)
                    sync.dma_start(reg.ap()[:, HW:2 * HW], sb_delta[:, 0:HW]).then_inc(g.sST.h, 16)
                    sync.dma_start(reg.ap()[:, DUW:DUW + HW], sb_u[:, HW:2 * HW]).then_inc(g.sST.h, 16)
                    sync.dma_start(reg.ap()[:, DUW + HW:2 * DUW], sb_u[:, 0:HW]).then_inc(g.sST.h, 16)
                    sync.dma_start(reg.ap()[0:32, 2 * DUW:2 * DUW + 512], sb_bc[:]).then_inc(g.sST.h, 16)
                    sync.dma_start(sn.ap(), g.c_nonrow[:]).then_inc(g.sST.h, 16)
                sync.end_ifs()
            g.sST.n += 96
            blk.flush_all()
        send_done = g.sST.n

    with ExitStack() as _es:
        sb_sdu = _es.enter_context(nc.sbuf_tensor("sb_sdu", [128, 2 * 2 * HW], BF16))
        sb_Bb = _es.enter_context(nc.sbuf_tensor("sb_Bb", [128, 2 * 4 * CH], BF16))
        sb_Cb = _es.enter_context(nc.sbuf_tensor("sb_Cb", [128, 2 * 4 * CH], BF16))
        sb_dA = _es.enter_context(nc.sbuf_tensor("sb_dA", [128, 2 * 8 * CH], BF16))
        sb_dBx = _es.enter_context(nc.sbuf_tensor("sb_dBx", [128, 8 * CH], BF16))
        sb_h = _es.enter_context(nc.sbuf_tensor("sb_h", [128, 8 * CH], BF16))
        sb_yacc = _es.enter_context(nc.sbuf_tensor("sb_yacc", [128, 2 * 8 * CH], BF16))
        sb_y16 = _es.enter_context(nc.sbuf_tensor("sb_y16", [128, DIT * CH], BF16))
        sb_gxi = _es.enter_context(nc.sbuf_tensor("sb_gxi", [128, DIT * CH], BF16))
        sb_gres = _es.enter_context(nc.sbuf_tensor("sb_gres", [128, DIT * CH], BF16))
        g.sb_wo = _es.enter_context(nc.sbuf_tensor("sb_wo3", [128, DIT * DM], BF16))
        g.sb_cxc = _es.enter_context(nc.sbuf_tensor("sb_cxc3", [128, 4 * DM], F32))
        g.ps_co = _es.enter_context(nc.psum_tensor("ps_co3", [128, 2048], F32))
        g.p3c_prev_resid = [0]
        g.p3c_prev_store = [0]

        @blk.gpsimd
        def _(gp):
            for i in range(DIT):
                g.sLG.inc(gp.dma_start(g.sb_wo[:, i * DM:(i + 1) * DM],
                                       g.din["woT"].ap()[i * 128:(i + 1) * 128, :]), 16)
        g.ms_wo_loaded = g.sLG.n
        first_scan_gate = [send_done]
        ysend_hist = []
        scan_ms = []
        sc_dones = []
        se_dones = []
        for sc in range(8):
            owner, cc = sc // 4, sc % 4
            buf = sc % 2
            emit_spin(g, blk, g.du_sn[(owner, cc)])
            @blk.dsync
            def _(sync, owner=owner, cc=cc, buf=buf, sc=sc):
                if sc < 2:
                    sync.wait_ge(g.sST.h, first_scan_gate[0])
                else:
                    sync.wait_ge(g.sVE.h, sc_dones[sc - 2])
                    sync.wait_ge(g.sSE.h, se_dones[sc - 2])
                reg = g.du_sh[(owner, cc)]
                r = g.r_role
                sdu_sem = g.sDU0 if buf == 0 else g.sDU1
                with sync.If_eq(r, 0):
                    sync.dma_start(sb_sdu[:, buf * 2 * HW: buf * 2 * HW + HW],
                                   reg.ap()[:, 0:HW]).then_inc(sdu_sem.h, 16)
                    sync.dma_start(sb_sdu[:, buf * 2 * HW + HW: buf * 2 * HW + 2 * HW],
                                   reg.ap()[:, DUW:DUW + HW]).then_inc(sdu_sem.h, 16)
                with sync.Else():
                    sync.dma_start(sb_sdu[:, buf * 2 * HW: buf * 2 * HW + HW],
                                   reg.ap()[:, HW:2 * HW]).then_inc(sdu_sem.h, 16)
                    sync.dma_start(sb_sdu[:, buf * 2 * HW + HW: buf * 2 * HW + 2 * HW],
                                   reg.ap()[:, DUW + HW:2 * DUW]).then_inc(sdu_sem.h, 16)
                sync.end_ifs()
            sdu_sem = g.sDU0 if buf == 0 else g.sDU1
            sdu_sem.n += 32
            sdu_l = sdu_sem.n
            grp_done = []
            grp_read_ms = {}
            @blk.sync
            def _(sync, owner=owner, cc=cc, sdu_sem=sdu_sem):
                reg = g.du_sh[(owner, cc)]
                for ng in range(2):
                    for n in range(ng * 4, ng * 4 + 4):
                        gslot = (ng % 2) * 4 + (n % 4)
                        sdu_sem.inc(sync.dma_start(
                            sb_Bb[:, gslot * CH:(gslot + 1) * CH],
                            reg.ap()[n:n + 1, 2 * DUW:2 * DUW + CH].partition_broadcast(128)), 16)
                        sdu_sem.inc(sync.dma_start(
                            sb_Cb[:, gslot * CH:(gslot + 1) * CH],
                            reg.ap()[16 + n:17 + n, 2 * DUW:2 * DUW + CH].partition_broadcast(128)), 16)
                    grp_done.append(sdu_sem.n)

            ysent_prev = ysend_hist[-1] if ysend_hist else 0
            for n in range(NST):
                ng = n // 4
                if ng >= 2 and n % 4 == 0:
                    gate_ms = grp_read_ms.get(ng - 2, 0)
                    @blk.sync
                    def _(sync, ng=ng, gate_ms=gate_ms, owner=owner, cc=cc, sdu_sem=sdu_sem):
                        sync.wait_ge(g.sVE.h, gate_ms)
                        reg = g.du_sh[(owner, cc)]
                        for nn in range(ng * 4, ng * 4 + 4):
                            gslot = (ng % 2) * 4 + (nn % 4)
                            sdu_sem.inc(sync.dma_start(
                                sb_Bb[:, gslot * CH:(gslot + 1) * CH],
                                reg.ap()[nn:nn + 1, 2 * DUW:2 * DUW + CH].partition_broadcast(128)), 16)
                            sdu_sem.inc(sync.dma_start(
                                sb_Cb[:, gslot * CH:(gslot + 1) * CH],
                                reg.ap()[16 + nn:17 + nn, 2 * DUW:2 * DUW + CH].partition_broadcast(128)), 16)
                        grp_done.append(sdu_sem.n)
                da_buf = n % 2
                @blk.scalar
                def _(s, n=n, buf=buf, da_buf=da_buf, sdu_sem=sdu_sem):
                    s.wait_ge(sdu_sem.h, sdu_l)
                    if len(scan_ms) >= 2:
                        s.wait_ge(g.sVE.h, scan_ms[-2])
                    for dt in range(8):
                        inst = s.activation(
                            sb_dA[:, da_buf * 8 * CH + dt * CH: da_buf * 8 * CH + (dt + 1) * CH],
                            sb_sdu[:, buf * 2 * HW + dt * CH: buf * 2 * HW + (dt + 1) * CH],
                            AF.Exp, bias=0.0,
                            scale=g.c_A[:, dt * NST + n: dt * NST + n + 1])
                    inst.then_inc(g.sSE.h, 1)
                g.sSE.n += 1
                da_done = g.sSE.n
                @blk.vector
                def _(v, n=n, buf=buf, da_buf=da_buf, ysp=ysent_prev, ng=ng,
                      bcast_ms=grp_done[min(ng, len(grp_done) - 1)],
                      sdu_sem2=(g.sDU0 if buf == 0 else g.sDU1)):
                    v.wait_ge(g.sSE.h, da_done)
                    v.wait_ge(sdu_sem2.h, bcast_ms)
                    if n == 0 and ysp:
                        v.wait_ge(g.sST.h, ysp)
                    gslot = (ng % 2) * 4 + (n % 4)
                    Brep = sb_Bb[:, gslot * CH:(gslot + 1) * CH].rearrange(
                        "p (o c) -> p o c", o=1).broadcast_to([128, 8, CH])
                    Crep = sb_Cb[:, gslot * CH:(gslot + 1) * CH].rearrange(
                        "p (o c) -> p o c", o=1).broadcast_to([128, 8, CH])
                    v.tensor_tensor(sb_dBx[:].rearrange("p (o c) -> p o c", o=8),
                                    sb_sdu[:, buf * 2 * HW + HW: buf * 2 * HW + 2 * HW].rearrange(
                                        "p (o c) -> p o c", o=8),
                                    Brep, op=AluOpType.mult)
                    v.drain()
                    for dt in range(8):
                        v.tensor_tensor_scan(
                            sb_h[:, dt * CH:(dt + 1) * CH],
                            sb_dA[:, da_buf * 8 * CH + dt * CH: da_buf * 8 * CH + (dt + 1) * CH],
                            sb_dBx[:, dt * CH:(dt + 1) * CH],
                            g.c_hprev[:, dt * NST + n: dt * NST + n + 1],
                            op0=AluOpType.mult, op1=AluOpType.add)
                    v.drain()
                    v.tensor_copy(
                        g.c_hprev[:].rearrange("p (d n2) -> p d n2", n2=NST)[:, 0:8, n:n + 1],
                        sb_h[:].rearrange("p (d c) -> p d c", c=CH)[:, :, CH - 1:CH])
                    v.drain()
                    ybuf = (sc % 2) * 8 * CH
                    dst = sb_yacc[:, ybuf:ybuf + 8 * CH] if n == 0 else sb_dBx[:]
                    v.tensor_tensor(dst.rearrange("p (o c) -> p o c", o=8),
                                    sb_h[:].rearrange("p (o c) -> p o c", o=8),
                                    Crep, op=AluOpType.mult)
                    v.drain()
                    if n > 0:
                        v.tensor_tensor(sb_yacc[:, ybuf:ybuf + 8 * CH],
                                        sb_yacc[:, ybuf:ybuf + 8 * CH],
                                        sb_dBx[:], op=AluOpType.add)
                        v.drain()
                    v.nop().then_inc(g.sVE.h, 1)
                g.sVE.n += 1
                scan_ms.append(g.sVE.n)
                grp_read_ms[ng] = g.sVE.n
                if n % 4 == 3:
                    blk.flush_all()
            sc_done = g.sVE.n
            sc_dones.append(g.sVE.n)
            se_dones.append(g.sSE.n)
            @blk.dsync
            def _(sync, sc=sc):
                sync.wait_ge(g.sVE.h, sc_done)
                ybuf = (sc % 2) * 8 * CH
                r = g.r_role
                with sync.If_eq(r, 0):
                    sync.dma_start(g.y_sh[(sc, 0)].ap(), sb_yacc[:, ybuf:ybuf + 8 * CH]).then_inc(g.sST.h, 16)
                    sync.dma_start(g.y_sn[(sc, 0)].ap(), g.c_nonrow[:]).then_inc(g.sST.h, 16)
                with sync.Else():
                    sync.dma_start(g.y_sh[(sc, 1)].ap(), sb_yacc[:, ybuf:ybuf + 8 * CH]).then_inc(g.sST.h, 16)
                    sync.dma_start(g.y_sn[(sc, 1)].ap(), g.c_nonrow[:]).then_inc(g.sST.h, 16)
                sync.end_ifs()
            g.sST.n += 32
            ysend_hist.append(g.sST.n)

            if sc >= 4:
                gi = sc - 4
                emit_gate(g, blk, gi, sb_y16, sb_gxi, sb_gres)
    g.ms_y3_stored = g.sST.n


def emit_gate(g, blk, gi, sb_y16, sb_gxi, sb_gres):
    nc = g.nc
    ve_prev = g.sVE.n
    @blk.sync
    def _(sync, vp=ve_prev):
        sync.wait_ge(g.sVE.h, vp)
    for part in range(2):
        @blk.dsync
        def _(sync, gi=gi, part=part):
            e = g.rv_nonce
            pr = g.r_probe
            r = g.r_role
            nread = [0]
            def spin_then_read(sync, reg_key):
                sync.sem_clear(g.sSP)
                sync.dma_start(g.c_probe[:], g.y_sn[reg_key].ap()).then_inc(g.sSP, 16)
                nread[0] = 1
                sync.wait_ge(g.sSP, 16)
                for ci_, row in enumerate(SPIN_CELLS):
                    sync.reg_load(pr, g.c_probe[row:row + 1, 0:1])
                    nout = 6 if ci_ == 0 else 1
                    for _o in range(nout):
                        with sync.If_ne(pr, e):
                            for _i in range(10 if ci_ == 0 else 1):
                                with sync.If_ne(pr, e):
                                    sync.nop(cycle_cnt=4000)
                                    sync.dma_start(g.c_probe[:], g.y_sn[reg_key].ap()).then_inc(g.sSP, 16)
                                    nread[0] += 1
                                    sync.wait_ge(g.sSP, 16 * nread[0])
                                    sync.reg_load(pr, g.c_probe[row:row + 1, 0:1])
                                sync.end_ifs()
                        sync.end_ifs()
                sync.dma_start(sb_y16[:, part * 8 * CH:(part + 1) * 8 * CH],
                               g.y_sh[reg_key].ap()).then_inc(g.sGT.h, 16)
            with sync.If_eq(r, 0):
                spin_then_read(sync, (gi, part))
            with sync.Else():
                spin_then_read(sync, (4 + gi, 1 - part))
            sync.end_ifs()
        g.sGT.n += 16
    y_l = g.sGT.n
    @blk.sync
    def _(sync, gi=gi):
        for dt in range(DIT):
            g.sGT.inc(sync.dma_start(
                sb_gxi[:, dt * CH:(dt + 1) * CH],
                g.xi_h.ap()[dt * 128:(dt + 1) * 128, gi * CH:(gi + 1) * CH]), 16)
            g.sGT.inc(sync.dma_start(
                sb_gres[:, dt * CH:(dt + 1) * CH],
                g.res_h.ap()[dt * 128:(dt + 1) * 128, gi * CH:(gi + 1) * CH]), 16)
    xi_l = g.sGT.n
    @blk.scalar
    def _(s):
        s.wait_ge(g.sGT.h, xi_l)
        s.activation(sb_gres[:], sb_gres[:], AF.Silu, bias=0.0, scale=1.0).then_inc(g.sSE.h, 1)
    g.sSE.n += 1
    sr_done = g.sSE.n
    @blk.vector
    def _(v, gi=gi):
        v.wait_ge(g.sSE.h, sr_done)
        v.wait_ge(g.sGT.h, y_l)
        v.tensor_tensor(sb_gxi[:].rearrange("p (d c) -> p d c", c=CH),
                        sb_gxi[:].rearrange("p (d c) -> p d c", c=CH),
                        g.c_D[:].rearrange("p (d o) -> p d o", o=1).broadcast_to([128, DIT, CH]),
                        op=AluOpType.mult)
        v.drain()
        v.tensor_tensor(sb_y16[:], sb_y16[:], sb_gxi[:], op=AluOpType.add)
        v.drain()
        v.tensor_tensor(sb_y16[:], sb_y16[:], sb_gres[:], op=AluOpType.mult)
        v.drain()
        v.nop().then_inc(g.sVE.h, 1)
    g.sVE.n += 1
    g3 = g.sVE.n
    @blk.sync
    def _(sync, gi=gi, ps0=g.p3c_prev_store[0]):
        if ps0:
            sync.wait_ge(g.sST.h, ps0)
        for r in range(4):
            g.sGT.inc(sync.dma_start(
                g.sb_cxc[:, r * DM:(r + 1) * DM],
                g.x2_h.ap()[gi * CH + 256 + r * 128: gi * CH + 256 + (r + 1) * 128, :]), 16)
    x2_l = g.sGT.n
    resid_ms = []
    for tt in range(4):
        @blk.tensor
        def _(t, tt=tt, gi=gi, pr3=g.p3c_prev_resid[0]):
            if tt == 0:
                t.wait_ge(g.sVE.h, g3)
                t.wait_ge(g.sGT.h, x2_l)
                t.wait_ge(g.sLG.h, g.ms_wo_loaded)
                if pr3:
                    t.wait_ge(g.sVE.h, pr3)
            if len(resid_ms) >= 2:
                t.wait_ge(g.sVE.h, resid_ms[-2])
            for fh in range(2):
                for kdt in range(DIT):
                    t.matmul(g.ps_co[:, ((tt % 2) * 2 + fh) * 512:((tt % 2) * 2 + fh) * 512 + 512],
                             sb_y16[:, kdt * CH + tt * 128: kdt * CH + tt * 128 + 128],
                             g.sb_wo[:, kdt * DM + fh * 512: kdt * DM + (fh + 1) * 512],
                             start=(kdt == 0), stop=False)
                inst = t.matmul(g.ps_co[:, ((tt % 2) * 2 + fh) * 512:((tt % 2) * 2 + fh) * 512 + 512],
                                g.c_ones[0:1, 0:128],
                                g.c_borow[0:1, fh * 512:(fh + 1) * 512],
                                start=False, stop=True)
            inst.then_inc(g.sTE.h, 1)
        g.sTE.n += 1
        mm3 = g.sTE.n
        @blk.vector
        def _(v, tt=tt, ps0=g.p3c_prev_store[0], mm3=mm3):
            v.wait_ge(g.sTE.h, mm3)
            if tt == 0 and ps0:
                v.wait_ge(g.sST.h, ps0)
            v.tensor_tensor(g.sb_cxc[:, tt * DM:(tt + 1) * DM],
                            g.sb_cxc[:, tt * DM:(tt + 1) * DM],
                            g.ps_co[:, (tt % 2) * 1024:(tt % 2) * 1024 + 1024],
                            op=AluOpType.add).then_inc(g.sVE.h, 1)
        g.sVE.n += 1
        resid_ms.append(g.sVE.n)
    g.p3c_prev_resid[0] = g.sVE.n
    done_v3 = g.sVE.n
    @blk.sync
    def _(sync, gi=gi, done_v3=done_v3):
        sync.wait_ge(g.sVE.h, done_v3)
        for r in range(4):
            g.sST.inc(sync.dma_start(
                g.x3_h.ap()[gi * CH + r * 128: gi * CH + (r + 1) * 128, :],
                g.sb_cxc[:, r * DM:(r + 1) * DM]), 16)
            if DEBUG:
                g.sST.inc(sync.dma_start(
                    g.dbg_x3.ap()[gi * CH + r * 128: gi * CH + (r + 1) * 128, :],
                    g.sb_cxc[:, r * DM:(r + 1) * DM]), 16)
    g.p3c_prev_store[0] = g.sST.n
    blk.flush_all()



def emit_p3c(g, blk):
    nc = g.nc
    din = g.din
    with ExitStack() as _es:
        sb_wo = _es.enter_context(nc.sbuf_tensor("sb_wo", [128, DIT * DM], BF16))
        sb_y3 = _es.enter_context(nc.sbuf_tensor("sb_y3", [128, DIT * CH], BF16))
        sb_xc = _es.enter_context(nc.sbuf_tensor("sb_cxc", [128, 4 * DM], F32))
        sb_o = _es.enter_context(nc.sbuf_tensor("sb_co", [128, 4 * DM], F32))
        ps_o = _es.enter_context(nc.psum_tensor("ps_co", [128, 2048], F32))

        @blk.gpsimd
        def _(gp):
            for i in range(DIT):
                g.sLG.inc(gp.dma_start(sb_wo[:, i * DM:(i + 1) * DM],
                                       din["woT"].ap()[i * 128:(i + 1) * 128, :]), 16)
        wo_l = g.sLG.n

        prev_resid = 0
        prev_store = g.sST.n
        resid_ms = []
        for c in range(NCH):
            te_prev3 = g.sTE.n
            @blk.sync
            def _(sync, c=c, pv=prev_resid, tp=te_prev3):
                if pv:
                    sync.wait_ge(g.sVE.h, pv)
                sync.wait_ge(g.sTE.h, tp)
                sync.wait_ge(g.sST.h, g.ms_y3_stored)
                for dt in range(DIT):
                    g.sLD.inc(sync.dma_start(
                        sb_y3[:, dt * CH:(dt + 1) * CH],
                        g.y3_h.ap()[dt * 128:(dt + 1) * 128, c * CH:(c + 1) * CH]), 16)
                for r in range(4):
                    g.sLD.inc(sync.dma_start(
                        sb_xc[:, r * DM:(r + 1) * DM],
                        g.x2_h.ap()[c * CH + 256 + r * 128: c * CH + 256 + (r + 1) * 128, :]), 16)
            yl = g.sLD.n
            for tt in range(4):
                @blk.tensor
                def _(t, tt=tt, pv=prev_resid):
                    if tt == 0:
                        t.wait_ge(g.sLD.h, yl)
                        t.wait_ge(g.sLG.h, wo_l)
                    if len(resid_ms) >= 2:
                        t.wait_ge(g.sVE.h, resid_ms[-2])
                    for fh in range(2):
                        for kdt in range(DIT):
                            t.matmul(ps_o[:, ((tt % 2) * 2 + fh) * 512:((tt % 2) * 2 + fh) * 512 + 512],
                                     sb_y3[:, kdt * CH + tt * 128: kdt * CH + tt * 128 + 128],
                                     sb_wo[:, kdt * DM + fh * 512: kdt * DM + (fh + 1) * 512],
                                     start=(kdt == 0), stop=False)
                        inst = t.matmul(ps_o[:, ((tt % 2) * 2 + fh) * 512:((tt % 2) * 2 + fh) * 512 + 512],
                                        g.c_ones[0:1, 0:128],
                                        g.c_borow[0:1, fh * 512:(fh + 1) * 512],
                                        start=False, stop=True)
                    inst.then_inc(g.sTE.h, 1)
                g.sTE.n += 1
                mm = g.sTE.n
                @blk.vector
                def _(v, tt=tt, c=c, ps0=prev_store):
                    v.wait_ge(g.sTE.h, mm)
                    if tt == 0:
                        v.wait_ge(g.sST.h, ps0)
                    v.tensor_tensor(sb_xc[:, tt * DM:(tt + 1) * DM],
                                    sb_xc[:, tt * DM:(tt + 1) * DM],
                                    ps_o[:, 0:1024],
                                    op=AluOpType.add).then_inc(g.sVE.h, 1)
                g.sVE.n += 1
                prev_resid = g.sVE.n
                resid_ms.append(g.sVE.n)
            done_v = g.sVE.n
            @blk.sync
            def _(sync, c=c):
                sync.wait_ge(g.sVE.h, done_v)
                for r in range(4):
                    g.sST.inc(sync.dma_start(
                        g.x3_h.ap()[c * CH + r * 128: c * CH + (r + 1) * 128, :],
                        sb_xc[:, r * DM:(r + 1) * DM]), 16)
                    if DEBUG:
                        g.sST.inc(sync.dma_start(
                            g.dbg_x3.ap()[c * CH + r * 128: c * CH + (r + 1) * 128, :],
                            sb_xc[:, r * DM:(r + 1) * DM]), 16)
            prev_store = g.sST.n
            blk.flush_all()


def emit(g):
    nc = g.nc
    with nc.Block() as rawblk:
        blk = RBlk(rawblk)
        emit_p0(g, blk)
        @blk.dsync
        def _(sync):
            sync.wait_ge(g.sLD.h, g.sLD.n)
            g.r_role = sync.to_reg(sync.value_load(g.c_rolei[0:1, 0:1]))
            g.rv_nonce = sync.value_load(g.c_noni[0:1, 0:1])
            g.r_probe = sync.alloc_register()
        barrier(g, blk)
        emit_p1(g, blk)
        barrier(g, blk)
        emit_ffn(g, blk, first=True)
        barrier(g, blk)
        emit_p3a(g, blk)
        barrier(g, blk)
        emit_p3b(g, blk)
        barrier(g, blk)
        emit_ffn(g, blk, first=False)
        blk.flush_all()
        @blk.sync
        def _(sync):
            sync.wait_ge(g.sST.h, g.sST.n)
        blk.flush_all()



def kernel(**inputs):
    ims = stage_inputs(inputs)
    nc, g = build_program()
    res = run_bass_kernel_spmd(nc, ims, core_ids=list(range(8)))
    outp = np.zeros((B, SEQ, DM), np.float32)
    for c in range(8):
        b_ = c // 2
        half = c % 2
        outp[b_, half * TOK:(half + 1) * TOK] = res.results[c]["out"]
    return outp


# version 24
# speedup vs baseline: 1.0266x; 1.0171x over previous
import time
from contextlib import ExitStack
import numpy as np
import ml_dtypes
import concourse.bass as bass
import concourse.mybir as mybir
from concourse.alu_op_type import AluOpType
from concourse.bass_utils import run_bass_kernel_spmd

F32 = mybir.dt.float32
BF16 = mybir.dt.bfloat16
I32 = mybir.dt.int32
AF = mybir.ActivationFunctionType
BFNP = ml_dtypes.bfloat16

B, SEQ, DM = 4, 4096, 1024
DI, NST, FF = 2048, 16, 4096
TOK = 2048
PRE = 256
CH = 512
NCH = 4
XROWS = 512 + TOK + 256
DT = DM // 128
DIT = DI // 128
FT = FF // 128
NEG = -960000.0
EPS = 1e-5
DEBUG = False
RACE_CHECK = False

CHUNKS = [(-256, 256)] + [(c * CH, CH) for c in range(NCH)]


def _f32(x):
    return np.ascontiguousarray(np.asarray(x), dtype=np.float32)


def _bf(x):
    return np.ascontiguousarray(np.asarray(x).astype(BFNP))


class Sem:
    def __init__(self, nc, name):
        self.h = nc.alloc_semaphore(name)
        self.n = 0

    def inc(self, inst, k):
        inst.then_inc(self.h, k)
        self.n += k


class G:
    pass


class _RInst:
    __slots__ = ("incs",)
    def __init__(self):
        self.incs = []
    def then_inc(self, sem, k):
        self.incs.append((sem, k))
        return self


class _REng:
    def __init__(self, rb, name):
        self._rb = rb
        self._name = name
    def __getattr__(self, meth):
        def call(*a, **kw):
            ri = _RInst()
            self._rb.q[self._name].append((meth, a, kw, ri))
            return ri
        return call


class RBlk:
    ENGINES = ("sync", "vector", "scalar", "tensor", "gpsimd")
    def __init__(self, blk):
        self._blk = blk
        self.q = {e: [] for e in self.ENGINES}
        for e in self.ENGINES:
            setattr(self, e, self._mk(e))
        self.dsync = self._mk_direct("sync")
        self.dgpsimd = self._mk_direct("gpsimd")
    def _mk(self, name):
        def deco(fn):
            fn(_REng(self, name))
            return fn
        return deco
    def _mk_direct(self, name):
        def deco(fn):
            self.flush(name)
            real = getattr(self._blk, name)
            @real
            def _(eng):
                fn(eng)
            return fn
        return deco
    def flush(self, engine=None):
        engines = [engine] if engine else list(self.ENGINES)
        for e in engines:
            items = self.q[e]
            if not items:
                continue
            self.q[e] = []
            real = getattr(self._blk, e)
            @real
            def _(eng, items=items):
                for meth, a, kw, ri in items:
                    inst = getattr(eng, meth)(*a, **kw)
                    for sem, k in ri.incs:
                        inst.then_inc(sem, k)
    def flush_all(self):
        self.flush()




def build_masks(half):
    def block_mask(gb):
        r = np.arange(256)[:, None]
        c = np.arange(768)[None, :]
        i = gb * 256 + r
        j = (gb - 1) * 256 + c
        rel = j - i
        valid = (rel >= -128) & (rel <= 128) & (j >= 0) & (j < SEQ)
        return np.where(valid, 0.0, NEG).astype(np.float32)

    def interior():
        r = np.arange(256)[:, None]
        c = np.arange(768)[None, :]
        rel = c - 256 - r
        return np.where((rel >= -128) & (rel <= 128), 0.0, NEG).astype(np.float32)

    g0 = half * 8
    blocks = [block_mask(g0 - 1) if g0 - 1 >= 0 else interior()]
    for gb in range(g0, g0 + 8):
        blocks.append(block_mask(gb))
    m = np.stack(blocks).reshape(9, 2, 128, 768)
    m = m.transpose(2, 0, 1, 3).reshape(128, 18 * 768)
    return _bf(m)


def tile_cols(vec, ntiles):
    return _f32(np.asarray(vec).reshape(ntiles, 128).T)


def stage_inputs(inputs):
    x = _f32(inputs["L0_attn_a_x"])
    winT = _f32(np.asarray(inputs["L0_attn_b_W_in"]).T)
    wxT = _f32(np.asarray(inputs["L0_attn_b_W_x"]).T)
    woT = _f32(np.asarray(inputs["L0_attn_b_W_out"]).T)
    wc = _f32(inputs["L0_attn_b_W_conv"][:, 0, :])
    alog = _f32(inputs["L0_attn_b_A_log"])
    dpar = _f32(inputs["L0_attn_b_D"])
    b_in = _f32(inputs["L0_attn_b_b_in"])
    bcv = _f32(inputs["L0_attn_b_b_conv"])
    b_x = _f32(inputs["L0_attn_b_b_x"])

    nonce_val = (int(time.time_ns()) % (2 ** 24)) | 1
    common = {
        "wq": _f32(inputs["L0_attn_a_Wq"]),
        "wk": _f32(inputs["L0_attn_a_Wk"]),
        "wv": _f32(inputs["L0_attn_a_Wv"]),
        "bq": tile_cols(inputs["L0_attn_a_bq"], DT),
        "bk": tile_cols(inputs["L0_attn_a_bk"], DT),
        "bv_row": _bf(np.asarray(inputs["L0_attn_a_bv"]).reshape(1, DM)),
        "w1T": _f32(np.asarray(inputs["L0_ffn_W1"]).T),
        "b1": tile_cols(inputs["L0_ffn_b1"], FT),
        "w2T": _f32(np.asarray(inputs["L0_ffn_W2"]).T),
        "b2_row": _bf(np.asarray(inputs["L0_ffn_b2"]).reshape(1, DM)),
        "ident": np.eye(128, dtype=BFNP),
        "nonce_i": np.array([[nonce_val]], np.int32),
        "nonce_row": np.full((128, 8), nonce_val, np.int32),
        "krow": np.tile(np.array([1e-5, 1.0 / 32.0, 1.0, 0.5, 2.0, 0.25, 0.0, 0.0],
                                 np.float32), (128, 1)),
    }
    ims = []
    for c in range(8):
        b_ = c // 2
        half = c % 2
        hs = half * TOK
        xp = np.zeros((XROWS, DM), np.float32)
        lo, hi = hs - 512, hs + TOK + 256
        s0, s1 = max(lo, 0), min(hi, SEQ)
        xp[s0 - lo: s1 - lo] = x[b_, s0:s1]
        perm = np.arange(DI) if half == 0 else np.concatenate(
            [np.arange(1024, 2048), np.arange(1024)])
        im = dict(common)
        im.update({
            "x_in": xp,
            "winT": np.ascontiguousarray(
                np.concatenate([winT[:, perm], winT[:, 2048 + perm]], axis=1)),
            "b_in_xi": tile_cols(b_in[perm], DIT),
            "b_in_res": tile_cols(b_in[2048 + perm], DIT),
            "wc": _f32(wc[perm].reshape(DIT, 128, 4).transpose(1, 0, 2).reshape(128, DIT * 4)),
            "bcv": tile_cols(bcv[perm], DIT),
            "wxT_d": np.ascontiguousarray(wxT[perm][:, :DI][:, perm]),
            "wxT_bc": np.ascontiguousarray(wxT[perm][:, DI:]),
            "b_x_d": tile_cols(b_x[:DI][perm], DIT),
            "b_x_bc": _f32(b_x[DI:].reshape(32, 1)),
            "woT": np.ascontiguousarray(woT[perm]),
            "bo_row": _bf(np.asarray(inputs["L0_attn_b_b_out"]).reshape(1, DM)),
            "alog": _f32(alog[perm].reshape(DIT, 128, NST).transpose(1, 0, 2).reshape(128, DIT * NST)),
            "dpar": tile_cols(dpar[perm], DIT),
            "masks": build_masks(half),
            "role": np.array([[half]], np.int32),
            "role_f": np.full((128, 1), float(half), np.float32),
        })
        ims.append(im)
    return ims



def build_program():
    nc = bass.Bass("TRN2", target_bir_lowering=False,
                   detect_race_conditions=RACE_CHECK)
    g = G()
    g.nc = nc
    din = {}

    def dram_in(name, shape, dt=F32):
        din[name] = nc.dram_tensor(name, shape, dt, kind="ExternalInput")

    dram_in("x_in", [XROWS, DM])
    for n in ("wq", "wk", "wv"):
        dram_in(n, [DM, DM])
    dram_in("bq", [128, DT]); dram_in("bk", [128, DT]); dram_in("bv_row", [1, DM], BF16)
    dram_in("w1T", [DM, FF]); dram_in("b1", [128, FT])
    dram_in("w2T", [FF, DM]); dram_in("b2_row", [1, DM], BF16)
    dram_in("winT", [DM, 2 * DI])
    dram_in("b_in_xi", [128, DIT]); dram_in("b_in_res", [128, DIT])
    dram_in("wc", [128, DIT * 4]); dram_in("bcv", [128, DIT])
    dram_in("wxT_d", [DI, DI]); dram_in("wxT_bc", [DI, 32])
    dram_in("b_x_d", [128, DIT]); dram_in("b_x_bc", [32, 1])
    dram_in("woT", [DI, DM]); dram_in("bo_row", [1, DM], BF16)
    dram_in("alog", [128, DIT * NST]); dram_in("dpar", [128, DIT])
    dram_in("masks", [128, 18 * 768], BF16)
    dram_in("ident", [128, 128], BF16)
    dram_in("role", [1, 1], I32); dram_in("role_f", [128, 1])
    dram_in("nonce_i", [1, 1], I32); dram_in("nonce_row", [128, 8], I32)
    dram_in("krow", [128, 8])
    g.din = din

    g.out = nc.dram_tensor("out", [TOK, DM], F32, kind="ExternalOutput")
    if DEBUG:
        g.dbg_x1 = nc.dram_tensor("dbg_x1", [PRE + TOK, DM], F32, kind="ExternalOutput")
        g.dbg_x2 = nc.dram_tensor("dbg_x2", [PRE + TOK, DM], F32, kind="ExternalOutput")
        g.dbg_x3 = nc.dram_tensor("dbg_x3", [TOK, DM], F32, kind="ExternalOutput")
        g.dbg_xi = nc.dram_tensor("dbg_xi", [DI, TOK], BF16, kind="ExternalOutput")
        g.dbg_y = nc.dram_tensor("dbg_y", [DI, TOK], BF16, kind="ExternalOutput")

    g.x1_h = nc.dram_tensor("x1_h", [PRE + TOK, DM], F32, kind="Internal")
    g.x2_h = nc.dram_tensor("x2_h", [PRE + TOK, DM], F32, kind="Internal")
    g.x3_h = nc.dram_tensor("x3_h", [TOK, DM], F32, kind="Internal")
    g.res_h = nc.dram_tensor("res_h", [DI, TOK], BF16, kind="Internal")
    g.w1t_h = nc.dram_tensor("w1t_h", [DM, FF], BF16, kind="Internal")
    g.w2t_h = nc.dram_tensor("w2t_h", [FF, DM], BF16, kind="Internal")
    g.xi_h = nc.dram_tensor("xi_h", [DI, TOK], BF16, kind="Internal")
    g.y3_h = nc.dram_tensor("y3_h", [DI, TOK], BF16, kind="Internal")

    DUW = DIT * CH
    g.du_sh, g.du_sn, g.y_sh, g.y_sn = {}, {}, {}, {}
    for r in range(2):
        for cc in range(NCH):
            g.du_sh[(r, cc)] = nc.dram_tensor(f"du_sh_{r}_{cc}", [128, 2 * DUW + 2 * CH],
                                              BF16, kind="Internal", addr_space="Shared")
            g.du_sn[(r, cc)] = nc.dram_tensor(f"du_sn_{r}_{cc}", [128, 8], I32,
                                              kind="Internal", addr_space="Shared")
    for gc in range(2 * NCH):
        for h in range(2):
            g.y_sh[(gc, h)] = nc.dram_tensor(f"y_sh_{gc}_{h}", [128, 8 * CH], BF16,
                                             kind="Internal", addr_space="Shared")
            g.y_sn[(gc, h)] = nc.dram_tensor(f"y_sn_{gc}_{h}", [128, 8], I32,
                                             kind="Internal", addr_space="Shared")

    for s in ("sLD", "sLG", "sLS", "sTE", "sVE", "sSE", "sST", "sBAR", "sGP",
              "sDU0", "sDU1", "sGT", "sTL", "sTL1", "sTS", "sTS1"):
        setattr(g, s, Sem(nc, s))
    g.sSP = nc.alloc_semaphore("sSP")

    A = nc.alloc_sbuf_tensor
    g.c_ident = A("c_ident", [128, 128], BF16)
    g.c_bq = A("c_bq", [128, DT], F32)
    g.c_bk = A("c_bk", [128, DT], F32)
    g.c_bvrow = A("c_bvrow", [1, DM], BF16)
    g.c_b1 = A("c_b1", [128, FT], F32)
    g.c_b2row = A("c_b2row", [1, DM], BF16)
    g.c_binxi = A("c_binxi", [128, DIT], F32)
    g.c_binres = A("c_binres", [128, DIT], F32)
    g.c_wc = A("c_wc", [128, DIT * 4], F32)
    g.c_bcv = A("c_bcv", [128, DIT], F32)
    g.c_bxd = A("c_bxd", [128, DIT], F32)
    g.c_bxbc = A("c_bxbc", [32, 1], F32)
    g.c_borow = A("c_borow", [1, DM], BF16)
    g.c_A = A("c_A", [128, DIT * NST], F32)
    g.c_D = A("c_D", [128, DIT], F32)
    g.c_rolef = A("c_rolef", [128, 1], F32)
    g.c_rolei = A("c_rolei", [1, 1], I32)
    g.c_noni = A("c_noni", [1, 1], I32)
    g.c_nonrow = A("c_nonrow", [128, 8], I32)
    g.c_ones = A("c_ones", [1, 1024], BF16)
    g.c_ones2 = A("c_ones2", [128, CH], BF16)
    g.c_onescol = A("c_onescol", [128, 1], F32)
    g.c_onesrow = A("c_onesrow", [1, 128], F32)
    g.c_probe = A("c_probe", [128, 8], I32)
    g.c_stats = A("c_stats", [128, 12], F32)
    g.c_mv = A("c_mv", [128, 2], F32)
    g.c_mvs = A("c_mvs", [128, 2 * 12], F32)
    g.c_sqs = A("c_sqs", [128, 12], F32)
    g.c_rsds = A("c_rsds", [128, 12], F32)
    g.ln_ev = [0]
    g.c_rstd = A("c_rstd", [128, 1], F32)
    g.c_sq = A("c_sq", [128, 1], F32)
    g.c_thr1 = A("c_thr1", [128, 2], F32)
    g.c_thr2 = A("c_thr2", [128, 2], F32)
    g.c_racc = A("c_racc", [128, 1], F32)
    g.c_hprev = A("c_hprev", [128, 8 * NST], F32)
    g.c_krow = A("c_krow", [128, 8], F32)
    for i, vv in enumerate([1e-5, 1.0 / 32.0, 1.0, 0.5, 2.0, 0.25]):
        nc.const_aps.register(F32, vv, g.c_krow[:, i:i + 1])

    emit(g)

    return nc, g


def barrier(g, blk):
    blk.flush_all()
    dma_waits = [(g.sLD, g.sLD.n), (g.sLG, g.sLG.n), (g.sLS, g.sLS.n),
                 (g.sST, g.sST.n), (g.sDU0, g.sDU0.n), (g.sDU1, g.sDU1.n),
                 (g.sGT, g.sGT.n), (g.sTL, g.sTL.n), (g.sTL1, g.sTL1.n),
                 (g.sTS, g.sTS.n), (g.sTS1, g.sTS1.n)]
    target = g.sBAR.n + 5
    for name in ("sync", "vector", "scalar", "tensor", "gpsimd"):
        @getattr(blk, name)
        def _(eng, _t=target):
            if name == "sync":
                for s_, n_ in dma_waits:
                    eng.wait_ge(s_.h, n_)
            eng.nop().then_inc(g.sBAR.h, 1)
            eng.wait_ge(g.sBAR.h, _t)
    g.sBAR.n = target



def ln_group(g, blk, tiles, xntm, ps_tr):
    n = len(tiles)
    for sl, (src_fn, _d) in enumerate(tiles):
        @blk.vector
        def _(v, src_fn=src_fn, sl=sl):
            src = src_fn()
            v.bn_stats(g.c_stats[:, 0:6], src[:, 0:512])
            v.bn_stats(g.c_stats[:, 6:12], src[:, 512:1024])
            v.drain()
            v.bn_aggr(g.c_mvs[:, 2 * sl:2 * sl + 2],
                      g.c_stats[:, 0:12]).then_inc(g.sVE.h, 1)
        g.sVE.n += 1
    m1 = g.sVE.n
    @blk.scalar
    def _(s):
        s.wait_ge(g.sVE.h, m1)
        s.activation(g.c_sqs[:, 0:n],
                     g.c_mvs[:, 1:2 * n].rearrange("p (a b) -> p a b", b=2)[:, :, 0:1]
                     if False else g.c_mvs[:].rearrange("p (a b) -> p a b", b=2)[:, 0:n, 1:2],
                     AF.Sqrt, bias=EPS, scale=1.0).then_inc(g.sSE.h, 1)
    g.sSE.n += 1
    m2 = g.sSE.n
    @blk.vector
    def _(v):
        v.wait_ge(g.sSE.h, m2)
        v.reciprocal(g.c_rsds[:, 0:n], g.c_sqs[:, 0:n]).then_inc(g.sVE.h, 1)
    g.sVE.n += 1
    rdone = g.sVE.n
    prev_tr = [0]
    for sl, (src_fn, dst_fn) in enumerate(tiles):
        @blk.vector
        def _(v, src_fn=src_fn, sl=sl, ptr=prev_tr[0]):
            src = src_fn()
            v.wait_ge(g.sVE.h, rdone)
            if ptr:
                v.wait_ge(g.sTE.h, ptr)
            v.drain()
            v.tensor_scalar(xntm[:], src, g.c_mvs[:, 2 * sl:2 * sl + 1],
                            g.c_rsds[:, sl:sl + 1],
                            op0=AluOpType.subtract, op1=AluOpType.mult).then_inc(g.sVE.h, 1)
        g.sVE.n += 1
        m3 = g.sVE.n
        @blk.tensor
        def _(t, prev_ev=g.ln_ev[0]):
            t.wait_ge(g.sVE.h, m3)
            if prev_ev:
                t.wait_ge(g.sSE.h, prev_ev)
            for dc in range(DT):
                inst = t.transpose(ps_tr[:, dc * 128:(dc + 1) * 128],
                                   xntm[:, dc * 128:(dc + 1) * 128], g.c_ident[:])
            inst.then_inc(g.sTE.h, 1)
        g.sTE.n += 1
        m4 = g.sTE.n
        prev_tr[0] = m4
        @blk.scalar
        def _(s, dst_fn=dst_fn):
            s.wait_ge(g.sTE.h, m4)
            for dc in range(DT):
                inst = s.activation(dst_fn(dc), ps_tr[:, dc * 128:(dc + 1) * 128],
                                    AF.Copy, bias=0.0, scale=1.0)
            inst.then_inc(g.sSE.h, 1)
        g.sSE.n += 1
        g.ln_ev[0] = g.sSE.n



def emit_p0(g, blk):
    din = g.din
    @blk.sync
    def _(sync):
        for name, dst in (
            ("ident", g.c_ident), ("bq", g.c_bq), ("bk", g.c_bk),
            ("bv_row", g.c_bvrow), ("b1", g.c_b1), ("b2_row", g.c_b2row),
            ("b_in_xi", g.c_binxi), ("b_in_res", g.c_binres),
            ("wc", g.c_wc), ("bcv", g.c_bcv),
            ("b_x_d", g.c_bxd), ("b_x_bc", g.c_bxbc), ("bo_row", g.c_borow),
            ("alog", g.c_A), ("dpar", g.c_D), ("role_f", g.c_rolef),
            ("role", g.c_rolei), ("nonce_i", g.c_noni), ("nonce_row", g.c_nonrow),
            ("krow", g.c_krow),
        ):
            g.sLD.inc(sync.dma_start(dst[:], din[name].ap()), 16)
    ld0 = g.sLD.n
    @blk.gpsimd
    def _(gp):
        gp.memset(g.c_ones[:], 1.0)
        gp.memset(g.c_onescol[:], 1.0)
        gp.memset(g.c_onesrow[:], 1.0)
        gp.memset(g.c_ones2[:], 1.0)
        gp.memset(g.c_hprev[:], 0.0).then_inc(g.sGP.h, 1)
    g.sGP.n += 1
    @blk.scalar
    def _(s):
        s.wait_ge(g.sLD.h, ld0)
        s.activation(g.c_A[:], g.c_A[:], AF.Exp, bias=0.0, scale=1.0).then_inc(g.sSE.h, 1)
    g.sSE.n += 1
    mA = g.sSE.n
    @blk.vector
    def _(v):
        v.wait_ge(g.sSE.h, mA)
        v.tensor_scalar(g.c_A[:], g.c_A[:], -1.0, None, op0=AluOpType.mult)
        v.drain()
        v.nop().then_inc(g.sVE.h, 1)
    g.sVE.n += 1



def emit_p1(g, blk):
    nc = g.nc
    din = g.din
    with ExitStack() as _es:
        sb_xwin = _es.enter_context(nc.sbuf_tensor("sb_xwin", [128, 8 * DM], F32))
        sb_xntm = _es.enter_context(nc.sbuf_tensor("sb_xntm", [128, DM], BF16))
        sb_xnt = _es.enter_context(nc.sbuf_tensor("sb_xnt", [128, DT * 1024], BF16))
        sb_wq = _es.enter_context(nc.sbuf_tensor("sb_wq", [128, DT * DM], BF16))
        sb_wk = _es.enter_context(nc.sbuf_tensor("sb_wk", [128, DT * DM], BF16))
        sb_wv = _es.enter_context(nc.sbuf_tensor("sb_wv", [128, DT * DM], BF16))
        sb_qT = _es.enter_context(nc.sbuf_tensor("sb_qT", [128, DT * CH], BF16))
        sb_kT = _es.enter_context(nc.sbuf_tensor("sb_kT", [128, DT * 1024], BF16))
        sb_v = _es.enter_context(nc.sbuf_tensor("sb_v", [128, 8 * DM], BF16))
        sb_mask = _es.enter_context(nc.sbuf_tensor("sb_mask", [128, 4 * 768], BF16))
        sb_exp = _es.enter_context(nc.sbuf_tensor("sb_exp", [128, 768], BF16))
        sb_rsum = _es.enter_context(nc.sbuf_tensor("sb_rsum", [128, 1], F32))
        sb_rrec = _es.enter_context(nc.sbuf_tensor("sb_rrec", [128, 1], F32))
        sb_wT = _es.enter_context(nc.sbuf_tensor("sb_wT", [128, 768], BF16))
        sb_x1c = _es.enter_context(nc.sbuf_tensor("sb_x1c", [128, 4 * DM], F32))
        ps_tr = _es.enter_context(nc.psum_tensor("ps_tr", [128, 1024], BF16))
        ps_a = _es.enter_context(nc.psum_tensor("ps_a", [128, 1024], F32))
        ps_b = _es.enter_context(nc.psum_tensor("ps_b", [128, 1024], F32))
        ps_c = _es.enter_context(nc.psum_tensor("ps_c", [128, 1024], F32))
        ps_wt = _es.enter_context(nc.psum_tensor("ps_wt", [128, 768], BF16))

        @blk.gpsimd
        def _(gp):
            for name, dst in (("wq", sb_wq), ("wk", sb_wk), ("wv", sb_wv)):
                for kd in range(DT):
                    g.sLG.inc(gp.dma_start(dst[:, kd * DM:(kd + 1) * DM],
                                           din[name].ap()[kd * 128:(kd + 1) * 128, :]), 16)
        w_loaded = g.sLG.n

        for ci, (t0, ntok) in enumerate(CHUNKS):
            winn = ntok + 512
            wtiles = winn // 128
            nqb = ntok // 256
            row0 = t0 - 256 + 512
            ve_prev = g.sVE.n
            te_prev = g.sTE.n
            st_prev = g.sST.n
            @blk.sync
            def _(sync, row0=row0, wtiles=wtiles, ci=ci, nqb=nqb):
                sync.wait_ge(g.sVE.h, ve_prev)
                sync.wait_ge(g.sTE.h, te_prev)
                for r in range(wtiles):
                    g.sLD.inc(sync.dma_start(
                        sb_xwin[:, r * DM:(r + 1) * DM],
                        din["x_in"].ap()[row0 + r * 128: row0 + (r + 1) * 128, :]), 16)
                qb0 = 0 if ci == 0 else 1 + (ci - 1) * 2
                for qb in range(nqb):
                    for qt in range(2):
                        sc = ((qb0 + qb) * 2 + qt) * 768
                        g.sLD.inc(sync.dma_start(
                            sb_mask[:, (qb * 2 + qt) * 768:(qb * 2 + qt + 1) * 768],
                            din["masks"].ap()[:, sc:sc + 768]), 16)
            xm_loaded = g.sLD.n

            @blk.vector
            def _(v):
                v.wait_ge(g.sLD.h, xm_loaded)
                v.nop().then_inc(g.sVE.h, 1)
            g.sVE.n += 1

            ln_group(g, blk,
                     [((lambda r=r: sb_xwin[:, r * DM:(r + 1) * DM]),
                       (lambda dc, r=r: sb_xnt[:, dc * 1024 + r * 128: dc * 1024 + r * 128 + 128]))
                      for r in range(wtiles)],
                     sb_xntm, ps_tr)
            ln_done = g.sSE.n

            q_evacs = []
            for m in range(DT):
                @blk.tensor
                def _(t, m=m, ntok=ntok):
                    if m == 0:
                        t.wait_ge(g.sSE.h, ln_done)
                        t.wait_ge(g.sLG.h, w_loaded)
                        t.wait_ge(g.sGP.h, 1)
                    if m >= 2:
                        t.wait_ge(g.sSE.h, q_evacs[m - 2])
                    p0 = (m % 2) * 512
                    for kd in range(DT):
                        inst = t.matmul(ps_a[:, p0:p0 + ntok],
                                 sb_wq[:, kd * DM + m * 128: kd * DM + m * 128 + 128],
                                 sb_xnt[:, kd * 1024 + 256: kd * 1024 + 256 + ntok],
                                 start=(kd == 0), stop=(kd == DT - 1))
                    inst.then_inc(g.sTE.h, 1)
                g.sTE.n += 1
                mm = g.sTE.n
                @blk.scalar
                def _(s, m=m, ntok=ntok):
                    s.wait_ge(g.sTE.h, mm)
                    p0 = (m % 2) * 512
                    s.activation(sb_qT[:, m * CH: m * CH + ntok], ps_a[:, p0:p0 + ntok],
                                 AF.Identity, bias=g.c_bq[:, m:m + 1], scale=1.0).then_inc(g.sSE.h, 1)
                g.sSE.n += 1
                q_evacs.append(g.sSE.n)

            k_evacs = []
            for m in range(DT):
                @blk.tensor
                def _(t, m=m, winn=winn):
                    if m == 0:
                        t.wait_ge(g.sSE.h, q_evacs[-1])
                    if m >= 2:
                        t.wait_ge(g.sSE.h, k_evacs[m - 2])
                    ps = ps_b if (m % 2 == 0) else ps_a
                    for f0 in range(0, winn, 512):
                        fw = min(512, winn - f0)
                        for kd in range(DT):
                            inst = t.matmul(ps[:, f0:f0 + fw],
                                     sb_wk[:, kd * DM + m * 128: kd * DM + m * 128 + 128],
                                     sb_xnt[:, kd * 1024 + f0: kd * 1024 + f0 + fw],
                                     start=(kd == 0), stop=(kd == DT - 1))
                    inst.then_inc(g.sTE.h, 1)
                g.sTE.n += 1
                mm = g.sTE.n
                @blk.scalar
                def _(s, m=m, winn=winn):
                    s.wait_ge(g.sTE.h, mm)
                    ps = ps_b if (m % 2 == 0) else ps_a
                    s.activation(sb_kT[:, m * 1024: m * 1024 + winn], ps[:, 0:winn],
                                 AF.Identity, bias=g.c_bk[:, m:m + 1], scale=1.0).then_inc(g.sSE.h, 1)
                g.sSE.n += 1
                k_evacs.append(g.sSE.n)

            v_evacs = []
            for r in range(wtiles):
                @blk.tensor
                def _(t, r=r):
                    if r == 0:
                        t.wait_ge(g.sSE.h, k_evacs[-1])
                    if r >= 2:
                        t.wait_ge(g.sSE.h, v_evacs[r - 2])
                    ps = ps_c if (r % 2 == 0) else ps_b
                    for hf in range(2):
                        for kd in range(DT):
                            t.matmul(ps[:, hf * 512:(hf + 1) * 512],
                                     sb_xnt[:, kd * 1024 + r * 128: kd * 1024 + r * 128 + 128],
                                     sb_wv[:, kd * DM + hf * 512: kd * DM + (hf + 1) * 512],
                                     start=(kd == 0), stop=False)
                        inst = t.matmul(ps[:, hf * 512:(hf + 1) * 512],
                                 g.c_ones[0:1, 0:128],
                                 g.c_bvrow[0:1, hf * 512:(hf + 1) * 512],
                                 start=False, stop=True)
                    inst.then_inc(g.sTE.h, 1)
                g.sTE.n += 1
                mm = g.sTE.n
                @blk.scalar
                def _(s, r=r):
                    s.wait_ge(g.sTE.h, mm)
                    ps = ps_c if (r % 2 == 0) else ps_b
                    s.activation(sb_v[:, r * DM:(r + 1) * DM], ps[:, 0:DM],
                                 AF.Copy, bias=0.0, scale=1.0).then_inc(g.sSE.h, 1)
                g.sSE.n += 1
                v_evacs.append(g.sSE.n)

            prev_exp = None
            prev_wts = None
            prev_wts_v = 0
            prev_x1 = None
            prev_wdone = [0]
            for qb in range(nqb):
                for qt in range(2):
                    @blk.tensor
                    def _(t, qb=qb, qt=qt, pe=prev_exp):
                        t.wait_ge(g.sSE.h, v_evacs[-1])
                        if pe is not None:
                            t.wait_ge(g.sSE.h, pe)
                        for f0, fw in ((0, 512), (512, 256)):
                            for kd in range(DT):
                                t.matmul(ps_a[:, f0:f0 + fw],
                                         sb_qT[:, kd * CH + qb * 256 + qt * 128:
                                               kd * CH + qb * 256 + qt * 128 + 128],
                                         sb_kT[:, kd * 1024 + qb * 256 + f0:
                                               kd * 1024 + qb * 256 + f0 + fw],
                                         start=(kd == 0), stop=False)
                            inst = t.matmul(ps_a[:, f0:f0 + fw], g.c_ident[:],
                                     sb_mask[:, (qb * 2 + qt) * 768 + f0:
                                             (qb * 2 + qt) * 768 + f0 + fw],
                                     start=False, stop=True)
                        inst.then_inc(g.sTE.h, 1)
                    g.sTE.n += 1
                    sc_done = g.sTE.n
                    pw2 = prev_wdone[0]
                    @blk.scalar
                    def _(s, pw2=pw2):
                        s.wait_ge(g.sTE.h, sc_done)
                        if pw2:
                            s.wait_ge(g.sVE.h, pw2)
                        s.activation(sb_exp[:], ps_a[:, 0:768], AF.Exp, bias=0.0,
                                     scale=1.0 / 32.0, accum_out=sb_rsum[:]).then_inc(g.sSE.h, 1)
                    g.sSE.n += 1
                    prev_exp = g.sSE.n
                    @blk.vector
                    def _(v):
                        v.wait_ge(g.sSE.h, prev_exp)
                        v.reciprocal(sb_rrec[:], sb_rsum[:])
                        v.drain()
                        v.tensor_scalar(sb_exp[:], sb_exp[:], sb_rrec[:], None,
                                        op0=AluOpType.mult).then_inc(g.sVE.h, 1)
                    g.sVE.n += 1
                    w_done = g.sVE.n
                    prev_wdone[0] = w_done
                    @blk.tensor
                    def _(t, pw=prev_wts_v):
                        t.wait_ge(g.sVE.h, w_done)
                        if pw:
                            t.wait_ge(g.sVE.h, pw)
                        for j in range(6):
                            inst = t.transpose(ps_wt[:, j * 128:(j + 1) * 128],
                                        sb_exp[:, j * 128:(j + 1) * 128], g.c_ident[:])
                        inst.then_inc(g.sTE.h, 1)
                    g.sTE.n += 1
                    wt_done = g.sTE.n
                    @blk.vector
                    def _(v):
                        v.wait_ge(g.sTE.h, wt_done)
                        v.tensor_copy(sb_wT[:], ps_wt[:, 0:768]).then_inc(g.sVE.h, 1)
                    g.sVE.n += 1
                    prev_wts_v = g.sVE.n
                    wts_done = g.sVE.n
                    @blk.tensor
                    def _(t, qb=qb, px=prev_x1):
                        t.wait_ge(g.sVE.h, wts_done)
                        if px is not None:
                            t.wait_ge(g.sVE.h, px)
                        for hf in range(2):
                            for j in range(6):
                                inst = t.matmul(ps_c[:, hf * 512:(hf + 1) * 512],
                                         sb_wT[:, j * 128:(j + 1) * 128],
                                         sb_v[:, (qb * 2 + j) * DM + hf * 512:
                                               (qb * 2 + j) * DM + (hf + 1) * 512],
                                         start=(j == 0), stop=(j == 5))
                        inst.then_inc(g.sTE.h, 1)
                    g.sTE.n += 1
                    av_done = g.sTE.n
                    @blk.vector
                    def _(v, qb=qb, qt=qt, st_prev=st_prev):
                        v.wait_ge(g.sTE.h, av_done)
                        if qb == 0 and qt == 0:
                            v.wait_ge(g.sST.h, st_prev)
                        rt = 2 + qb * 2 + qt
                        dstc = (qb * 2 + qt) * DM
                        v.tensor_tensor(sb_x1c[:, dstc:dstc + DM],
                                        sb_xwin[:, rt * DM:(rt + 1) * DM],
                                        ps_c[:, 0:DM], op=AluOpType.add).then_inc(g.sVE.h, 1)
                    g.sVE.n += 1
                    prev_x1 = g.sVE.n
            x1_done = g.sVE.n
            @blk.sync
            def _(sync, t0=t0, ntok=ntok):
                sync.wait_ge(g.sVE.h, x1_done)
                r0 = t0 + 256
                for rt in range(ntok // 128):
                    g.sST.inc(sync.dma_start(
                        g.x1_h.ap()[r0 + rt * 128: r0 + (rt + 1) * 128, :],
                        sb_x1c[:, rt * DM:(rt + 1) * DM]), 16)
                if DEBUG:
                    for rt in range(ntok // 128):
                        g.sST.inc(sync.dma_start(
                            g.dbg_x1.ap()[r0 + rt * 128: r0 + (rt + 1) * 128, :],
                            sb_x1c[:, rt * DM:(rt + 1) * DM]), 16)
            blk.flush_all()
    g.ms_x1_stored = g.sST.n



SPIN_CELLS = [0, 12, 28, 40, 64, 76, 88, 92]


def emit_spin(g, blk, sent_dram):
    @blk.dsync
    def _(sync):
        e = g.rv_nonce
        pr = g.r_probe
        sync.sem_clear(g.sSP)
        sync.dma_start(g.c_probe[:], sent_dram.ap()).then_inc(g.sSP, 16)
        sync.wait_ge(g.sSP, 16)
        nread = [1]
        for ci, row in enumerate(SPIN_CELLS):
            sync.reg_load(pr, g.c_probe[row:row + 1, 0:1])
            nout = 6 if ci == 0 else 1
            for o in range(nout):
                with sync.If_ne(pr, e):
                    for i in range(10 if ci == 0 else 1):
                        with sync.If_ne(pr, e):
                            sync.nop(cycle_cnt=8000)
                            sync.dma_start(g.c_probe[:], sent_dram.ap()).then_inc(g.sSP, 16)
                            nread[0] += 1
                            sync.wait_ge(g.sSP, 16 * nread[0])
                            sync.reg_load(pr, g.c_probe[row:row + 1, 0:1])
                        sync.end_ifs()
                sync.end_ifs()



def emit_ternarize(g, blk, w_dram, ntiles, width, thr_dst):
    uid = g.uid = getattr(g, "uid", 0) + 1
    """Pass A: thr = 0.5*max(mean|W|, 1e-8) into thr_dst ([128,2]: +thr, -thr).
    Streams [128, sw] subtiles double-buffered on parity DMA sems."""
    nc = g.nc
    ch2 = 2 if width > 2048 else 1
    sw = width // ch2
    nsub = ntiles * ch2
    with ExitStack() as _es:
        sb_w = _es.enter_context(nc.sbuf_tensor(f"sb_tw{uid}", [128, 2 * sw], F32))
        ps_t = _es.enter_context(nc.psum_tensor(f"ps_tern{uid}", [128, 2], F32))
        te_now = g.sTE.n
        ve_entry = g.sVE.n
        @blk.gpsimd
        def _(gp):
            gp.wait_ge(g.sTE.h, te_now)
            gp.memset(g.c_racc[:], 0.0).then_inc(g.sGP.h, 1)
        g.sGP.n += 1
        m0 = g.sGP.n
        vms = []
        for i in range(nsub):
            tl = g.sTL if i % 2 == 0 else g.sTL1
            @blk.sync
            def _(sync, i=i, tl=tl):
                if i >= 2:
                    sync.wait_ge(g.sVE.h, vms[-2])
                else:
                    sync.wait_ge(g.sVE.h, ve_entry)
                tl.inc(sync.dma_start(
                    sb_w[:, (i % 2) * sw:(i % 2 + 1) * sw],
                    w_dram.ap()[(i // ch2) * 128:(i // ch2 + 1) * 128,
                                (i % ch2) * sw:(i % ch2 + 1) * sw]), 16)
            ld = tl.n
            @blk.vector
            def _(v, i=i, tl=tl, ld=ld):
                v.wait_ge(tl.h, ld)
                if i == 0:
                    v.wait_ge(g.sGP.h, m0)
                src3 = sb_w[:, (i % 2) * sw:(i % 2 + 1) * sw].rearrange(
                    "p (a b) -> p a b", b=512)
                v.tensor_reduce(g.c_sqs[:, 0:sw // 512], src3,
                                axis=mybir.AxisListType.X,
                                op=AluOpType.add, apply_absolute_value=True)
                v.drain()
                v.tensor_reduce(g.c_sq[:], g.c_sqs[:, 0:sw // 512],
                                axis=mybir.AxisListType.X, op=AluOpType.add)
                v.drain()
                v.tensor_tensor(g.c_racc[:], g.c_racc[:], g.c_sq[:],
                                op=AluOpType.add).then_inc(g.sVE.h, 1)
            g.sVE.n += 1
            vms.append(g.sVE.n)
        @blk.tensor
        def _(t):
            t.wait_ge(g.sVE.h, vms[-1])
            t.matmul(ps_t[0:1, 0:1], g.c_onescol[:], g.c_racc[:],
                     start=True, stop=True).then_inc(g.sTE.h, 1)
        g.sTE.n += 1
        m1 = g.sTE.n
        @blk.scalar
        def _(s):
            s.wait_ge(g.sTE.h, m1)
            s.activation(g.c_sq[0:1, 0:1], ps_t[0:1, 0:1], AF.Copy, bias=0.0,
                         scale=float(1.0 / (ntiles * 128 * width))).then_inc(g.sSE.h, 1)
        g.sSE.n += 1
        m2 = g.sSE.n
        @blk.vector
        def _(v):
            v.wait_ge(g.sSE.h, m2)
            v.tensor_scalar(g.c_sq[0:1, 0:1], g.c_sq[0:1, 0:1], 1e-8, 0.5,
                            op0=AluOpType.max, op1=AluOpType.mult).then_inc(g.sVE.h, 1)
        g.sVE.n += 1
        m3 = g.sVE.n
        @blk.tensor
        def _(t):
            t.wait_ge(g.sVE.h, m3)
            t.matmul(ps_t[:, 1:2], g.c_onesrow[0:1, :], g.c_sq[0:1, 0:1],
                     start=True, stop=True).then_inc(g.sTE.h, 1)
        g.sTE.n += 1
        m4 = g.sTE.n
        @blk.scalar
        def _(s):
            s.wait_ge(g.sTE.h, m4)
            s.activation(thr_dst[:, 0:1], ps_t[:, 1:2], AF.Copy, bias=0.0, scale=1.0)
            s.activation(thr_dst[:, 1:2], ps_t[:, 1:2], AF.Copy, bias=0.0,
                         scale=-1.0).then_inc(g.sSE.h, 1)
        g.sSE.n += 1
        blk.flush_all()


def emit_tern_apply(g, blk, w_dram, ntiles, width, thr, dst_sb, dst_h):
    nc = g.nc
    uid = g.uid = getattr(g, "uid", 0) + 1
    ve_entry2 = g.sVE.n
    ch2 = 2 if width > 2048 else 1
    sw = width // ch2
    nsub = ntiles * ch2
    with ExitStack() as _es:
        sb_w = _es.enter_context(nc.sbuf_tensor(f"sb_tw2_{uid}", [128, 2 * sw], F32))
        sb_pp = _es.enter_context(nc.sbuf_tensor(f"sb_tpp{uid}", [128, 2 * sw], F32))
        thr_done = g.sSE.n
        vms = []
        for i in range(nsub):
            tl = g.sTL if i % 2 == 0 else g.sTL1
            @blk.sync
            def _(sync, i=i, tl=tl):
                if i >= 2:
                    sync.wait_ge(g.sVE.h, vms[-2])
                else:
                    sync.wait_ge(g.sVE.h, ve_entry2)
                tl.inc(sync.dma_start(
                    sb_w[:, (i % 2) * sw:(i % 2 + 1) * sw],
                    w_dram.ap()[(i // ch2) * 128:(i // ch2 + 1) * 128,
                                (i % ch2) * sw:(i % ch2 + 1) * sw]), 16)
            ld = tl.n
            @blk.vector
            def _(v, i=i, tl=tl, ld=ld):
                v.wait_ge(tl.h, ld)
                if i == 0:
                    v.wait_ge(g.sSE.h, thr_done)
                p = (i % 2)
                srcw = sb_w[:, p * sw:(p + 1) * sw]
                dcol = (i // ch2) * width + (i % ch2) * sw
                v.tensor_scalar(sb_pp[:, p * sw:(p + 1) * sw], srcw,
                                thr[:, 0:1], None, op0=AluOpType.is_ge)
                v.tensor_scalar(dst_sb[:, dcol:dcol + sw], srcw,
                                thr[:, 1:2], None, op0=AluOpType.is_le)
                v.drain()
                v.tensor_tensor(dst_sb[:, dcol:dcol + sw],
                                sb_pp[:, p * sw:(p + 1) * sw],
                                dst_sb[:, dcol:dcol + sw],
                                op=AluOpType.subtract).then_inc(g.sVE.h, 1)
            g.sVE.n += 1
            vms.append(g.sVE.n)
            vm = g.sVE.n
            ts_ = g.sTS if i % 2 == 0 else g.sTS1
            @blk.sync
            def _(sync, i=i, vm=vm, ts_=ts_):
                sync.wait_ge(g.sVE.h, vm)
                ts_.inc(sync.dma_start(
                    dst_h.ap()[(i // ch2) * 128:(i // ch2 + 1) * 128,
                               (i % ch2) * sw:(i % ch2 + 1) * sw],
                    dst_sb[:, (i // ch2) * width + (i % ch2) * sw:
                           (i // ch2) * width + (i % ch2) * sw + sw]), 16)



def emit_ffn(g, blk, first):
    nc = g.nc
    din = g.din
    src_h = g.x1_h if first else g.x3_h
    with ExitStack() as _es:
        sb_w1 = _es.enter_context(nc.sbuf_tensor(f"sb_w1t{first}", [128, DT * FF], BF16))
        sb_w2 = _es.enter_context(nc.sbuf_tensor(f"sb_w2t{first}", [128, FT * DM], BF16))

        if first:
            emit_ternarize(g, blk, din["w1T"], DT, FF, g.c_thr1)
            emit_ternarize(g, blk, din["w2T"], FT, DM, g.c_thr2)
            emit_tern_apply(g, blk, din["w1T"], DT, FF, g.c_thr1, sb_w1, g.w1t_h)
            emit_tern_apply(g, blk, din["w2T"], FT, DM, g.c_thr2, sb_w2, g.w2t_h)
        else:
            @blk.sync
            def _(sync):
                for i in range(DT):
                    g.sLD.inc(sync.dma_start(sb_w1[:, i * FF:(i + 1) * FF],
                                             g.w1t_h.ap()[i * 128:(i + 1) * 128, :]), 16)
                for i in range(FT):
                    g.sLD.inc(sync.dma_start(sb_w2[:, i * DM:(i + 1) * DM],
                                             g.w2t_h.ap()[i * 128:(i + 1) * 128, :]), 16)
        w_ready_v = g.sVE.n
        w_ready_l = g.sLD.n
        w_ready_s = g.sST.n
        sb_xc = _es.enter_context(nc.sbuf_tensor(f"sb_fxc{first}", [128, 4 * DM], F32))
        sb_xntm = _es.enter_context(nc.sbuf_tensor(f"sb_fxntm{first}", [128, DM], BF16))
        sb_xnt = _es.enter_context(nc.sbuf_tensor(f"sb_fxnt{first}", [128, DT * CH], BF16))
        sb_hT = _es.enter_context(nc.sbuf_tensor(f"sb_fhT{first}", [128, FT * 512], BF16))
        ps_tr = _es.enter_context(nc.psum_tensor(f"ps_ftr{first}", [128, 1024], BF16))
        ps_h = _es.enter_context(nc.psum_tensor(f"ps_fh{first}", [128, 1024], F32))
        ps_o = _es.enter_context(nc.psum_tensor(f"ps_fo{first}", [128, 2048], F32))

        chunk_list = CHUNKS if first else CHUNKS[1:]
        prev_resid = 0
        prev_store = g.sST.n
        prev_te = g.sTE.n
        for ci, (t0, ntok) in enumerate(chunk_list):
            rtiles = ntok // 128
            r0 = t0 + 256 if first else t0
            ve_prev = g.sVE.n
            te_prev = g.sTE.n
            @blk.sync
            def _(sync, r0=r0, rtiles=rtiles, pstore=prev_store):
                sync.wait_ge(g.sVE.h, ve_prev)
                sync.wait_ge(g.sTE.h, te_prev)
                if pstore:
                    sync.wait_ge(g.sST.h, pstore)
                if first:
                    sync.wait_ge(g.sST.h, g.ms_x1_stored)
                for r in range(rtiles):
                    g.sLD.inc(sync.dma_start(
                        sb_xc[:, r * DM:(r + 1) * DM],
                        src_h.ap()[r0 + r * 128: r0 + (r + 1) * 128, :]), 16)
            xl = g.sLD.n
            @blk.vector
            def _(v):
                v.wait_ge(g.sLD.h, xl)
                v.nop().then_inc(g.sVE.h, 1)
            g.sVE.n += 1
            ln_group(g, blk,
                     [((lambda r=r: sb_xc[:, r * DM:(r + 1) * DM]),
                       (lambda dc, r=r: sb_xnt[:, dc * CH + r * 128: dc * CH + r * 128 + 128]))
                      for r in range(rtiles)],
                     sb_xntm, ps_tr)
            ln_done = g.sSE.n

            h_evacs = []
            for mf in range(FT):
                @blk.tensor
                def _(t, mf=mf, ntok=ntok):
                    if mf == 0:
                        t.wait_ge(g.sSE.h, ln_done)
                        t.wait_ge(g.sVE.h, w_ready_v)
                        t.wait_ge(g.sLD.h, w_ready_l)
                    if mf >= 2:
                        t.wait_ge(g.sSE.h, h_evacs[mf - 2])
                    p0 = (mf % 2) * 512
                    for kd in range(DT):
                        inst = t.matmul(ps_h[:, p0:p0 + ntok],
                                        sb_w1[:, kd * FF + mf * 128: kd * FF + mf * 128 + 128],
                                        sb_xnt[:, kd * CH: kd * CH + ntok],
                                        start=(kd == 0), stop=(kd == DT - 1))
                    inst.then_inc(g.sTE.h, 1)
                g.sTE.n += 1
                mm = g.sTE.n
                @blk.scalar
                def _(s, mf=mf, ntok=ntok, mm=mm):
                    s.wait_ge(g.sTE.h, mm)
                    p0 = (mf % 2) * 512
                    s.activation(sb_hT[:, mf * 512: mf * 512 + ntok], ps_h[:, p0:p0 + ntok],
                                 AF.Gelu, bias=g.c_b1[:, mf:mf + 1],
                                 scale=1.0).then_inc(g.sSE.h, 1)
                g.sSE.n += 1
                h_evacs.append(g.sSE.n)
            for tt in range(ntok // 128):
                @blk.tensor
                def _(t, tt=tt, pr=prev_resid):
                    t.wait_ge(g.sSE.h, h_evacs[-1])
                    if pr:
                        t.wait_ge(g.sVE.h, pr)
                    for fh in range(2):
                        bk = ((tt % 2) * 2 + fh) * 512
                        for kf in range(FT):
                            t.matmul(ps_o[:, bk:bk + 512],
                                     sb_hT[:, kf * 512 + tt * 128: kf * 512 + tt * 128 + 128],
                                     sb_w2[:, kf * DM + fh * 512: kf * DM + (fh + 1) * 512],
                                     start=(kf == 0), stop=False)
                        inst = t.matmul(ps_o[:, bk:bk + 512],
                                        g.c_ones[0:1, 0:128],
                                        g.c_b2row[0:1, fh * 512:(fh + 1) * 512],
                                        start=False, stop=True)
                    inst.then_inc(g.sTE.h, 1)
                g.sTE.n += 1
                mm2 = g.sTE.n
                @blk.vector
                def _(v, tt=tt, ps0=prev_store, mm2=mm2):
                    v.wait_ge(g.sTE.h, mm2)
                    v.tensor_tensor(sb_xc[:, tt * DM:(tt + 1) * DM],
                                    sb_xc[:, tt * DM:(tt + 1) * DM],
                                    ps_o[:, (tt % 2) * 1024: (tt % 2) * 1024 + 1024],
                                    op=AluOpType.add).then_inc(g.sVE.h, 1)
                g.sVE.n += 1
                prev_resid = g.sVE.n
            done_v = g.sVE.n
            @blk.sync
            def _(sync, t0=t0, rtiles=rtiles, ci=ci):
                sync.wait_ge(g.sVE.h, done_v)
                for r in range(rtiles):
                    if first:
                        g.sST.inc(sync.dma_start(
                            g.x2_h.ap()[t0 + 256 + r * 128: t0 + 256 + (r + 1) * 128, :],
                            sb_xc[:, r * DM:(r + 1) * DM]), 16)
                        if DEBUG:
                            g.sST.inc(sync.dma_start(
                                g.dbg_x2.ap()[t0 + 256 + r * 128: t0 + 256 + (r + 1) * 128, :],
                                sb_xc[:, r * DM:(r + 1) * DM]), 16)
                    else:
                        g.sST.inc(sync.dma_start(
                            g.out.ap()[t0 + r * 128: t0 + (r + 1) * 128, :],
                            sb_xc[:, r * DM:(r + 1) * DM]), 16)
            prev_store = g.sST.n
            blk.flush_all()
            blk.flush_all()
    if first:
        g.ms_x2_stored = g.sST.n



def emit_p3a(g, blk):
    nc = g.nc
    din = g.din
    with ExitStack() as _es:
        sb_win = _es.enter_context(nc.sbuf_tensor("sb_win", [128, DT * 2 * DI], BF16))
        sb_xc = _es.enter_context(nc.sbuf_tensor("sb_3xc", [128, 4 * DM], F32))
        sb_xntm = _es.enter_context(nc.sbuf_tensor("sb_3xntm", [128, DM], BF16))
        sb_xnt = _es.enter_context(nc.sbuf_tensor("sb_3xnt", [128, DT * CH], BF16))
        sb_xrh = _es.enter_context(nc.sbuf_tensor("sb_xrh", [128, DIT * (CH + 3)], BF16))
        sb_res = _es.enter_context(nc.sbuf_tensor("sb_3res", [128, 2 * CH], BF16))
        sb_cacc = _es.enter_context(nc.sbuf_tensor("sb_cacc", [128, 2 * CH], F32))
        sb_xi = _es.enter_context(nc.sbuf_tensor("sb_3xi", [128, DIT * CH], BF16))
        ps_tr = _es.enter_context(nc.psum_tensor("ps_3tr", [128, 1024], BF16))
        ps_w = _es.enter_context(nc.psum_tensor("ps_3w", [128, 1024], F32))

        @blk.gpsimd
        def _(gp):
            for i in range(DT):
                g.sLG.inc(gp.dma_start(sb_win[:, i * 2 * DI:(i + 1) * 2 * DI],
                                       din["winT"].ap()[i * 128:(i + 1) * 128, :]), 16)
        win_l = g.sLG.n

        prev_ve = 0
        prev_chunk_st = [0, 0]
        for ci, (t0, ntok) in enumerate(CHUNKS):
            prev_chunk_st[1] = g.sST.n
            rtiles = ntok // 128
            r0 = t0 + 256
            ve_prev = g.sVE.n
            te_prev = g.sTE.n
            @blk.sync
            def _(sync, r0=r0, rtiles=rtiles):
                sync.wait_ge(g.sVE.h, ve_prev)
                sync.wait_ge(g.sTE.h, te_prev)
                sync.wait_ge(g.sST.h, g.ms_x2_stored)
                for r in range(rtiles):
                    g.sLD.inc(sync.dma_start(
                        sb_xc[:, r * DM:(r + 1) * DM],
                        g.x2_h.ap()[r0 + r * 128: r0 + (r + 1) * 128, :]), 16)
            xl = g.sLD.n
            @blk.vector
            def _(v):
                v.wait_ge(g.sLD.h, xl)
                v.nop().then_inc(g.sVE.h, 1)
            g.sVE.n += 1
            rlist = range(rtiles - 1, rtiles) if ci == 0 else range(rtiles)
            ln_group(g, blk,
                     [((lambda r=r: sb_xc[:, r * DM:(r + 1) * DM]),
                       (lambda dc, r=r: sb_xnt[:, dc * CH + r * 128: dc * CH + r * 128 + 128]))
                      for r in rlist],
                     sb_xntm, ps_tr)
            ln_done = g.sSE.n

            if ci == 0:
                evs = []
                for mf in range(DIT):
                    @blk.tensor
                    def _(t, mf=mf):
                        if mf == 0:
                            t.wait_ge(g.sSE.h, ln_done)
                            t.wait_ge(g.sLG.h, win_l)
                        if mf >= 2:
                            t.wait_ge(g.sSE.h, evs[mf - 2])
                        p0 = (mf % 2) * 512
                        for kd in range(DT):
                            inst = t.matmul(ps_w[:, p0:p0 + 128],
                                            sb_win[:, kd * 2 * DI + mf * 128: kd * 2 * DI + mf * 128 + 128],
                                            sb_xnt[:, kd * CH + 128: kd * CH + 256],
                                            start=(kd == 0), stop=(kd == DT - 1))
                        inst.then_inc(g.sTE.h, 1)
                    g.sTE.n += 1
                    mm = g.sTE.n
                    @blk.scalar
                    def _(s, mf=mf):
                        s.wait_ge(g.sTE.h, mm)
                        p0 = (mf % 2) * 512
                        s.activation(sb_xrh[:, mf * (CH + 3): mf * (CH + 3) + 3],
                                     ps_w[:, p0 + 125: p0 + 128], AF.Identity,
                                     bias=g.c_binxi[:, mf:mf + 1], scale=1.0).then_inc(g.sSE.h, 1)
                    g.sSE.n += 1
                    evs.append(g.sSE.n)
                halo_se = g.sSE.n
                @blk.vector
                def _(v):
                    v.wait_ge(g.sSE.h, halo_se)
                    for mf in range(DIT):
                        v.tensor_scalar(sb_xrh[:, mf * (CH + 3): mf * (CH + 3) + 3],
                                        sb_xrh[:, mf * (CH + 3): mf * (CH + 3) + 3],
                                        g.c_rolef[:, 0:1], None, op0=AluOpType.mult)
                    v.nop().then_inc(g.sVE.h, 1)
                g.sVE.n += 1
                prev_ve = g.sVE.n
                continue

            c = ci - 1
            evs = []
            res_stores = []
            prev_chunk_st[0] = prev_chunk_st[1]
            for mf in range(2 * DIT):
                @blk.tensor
                def _(t, mf=mf, pv=prev_ve):
                    if mf == 0:
                        t.wait_ge(g.sSE.h, ln_done)
                        t.wait_ge(g.sLG.h, win_l)
                        t.wait_ge(g.sVE.h, pv)
                    if mf >= 2:
                        t.wait_ge(g.sSE.h, evs[mf - 2])
                    p0 = (mf % 2) * 512
                    for kd in range(DT):
                        inst = t.matmul(ps_w[:, p0:p0 + CH],
                                        sb_win[:, kd * 2 * DI + mf * 128: kd * 2 * DI + mf * 128 + 128],
                                        sb_xnt[:, kd * CH: kd * CH + CH],
                                        start=(kd == 0), stop=(kd == DT - 1))
                    inst.then_inc(g.sTE.h, 1)
                g.sTE.n += 1
                mm = g.sTE.n
                if mf < DIT:
                    @blk.scalar
                    def _(s, mf=mf):
                        s.wait_ge(g.sTE.h, mm)
                        p0 = (mf % 2) * 512
                        s.activation(sb_xrh[:, mf * (CH + 3) + 3: mf * (CH + 3) + 3 + CH],
                                     ps_w[:, p0:p0 + CH], AF.Identity,
                                     bias=g.c_binxi[:, mf:mf + 1], scale=1.0).then_inc(g.sSE.h, 1)
                    g.sSE.n += 1
                else:
                    rb = (mf - DIT) % 2
                    @blk.scalar
                    def _(s, mf=mf, c=c, rb=rb):
                        s.wait_ge(g.sTE.h, mm)
                        if res_stores:
                            s.wait_ge(g.sST.h, res_stores[-1])
                        elif prev_chunk_st[0]:
                            s.wait_ge(g.sST.h, prev_chunk_st[0])
                        p0 = (mf % 2) * 512
                        s.activation(sb_res[:, rb * CH:(rb + 1) * CH], ps_w[:, p0:p0 + CH],
                                     AF.Identity,
                                     bias=g.c_binres[:, mf - DIT:mf - DIT + 1],
                                     scale=1.0).then_inc(g.sSE.h, 1)
                    g.sSE.n += 1
                    se_res = g.sSE.n
                    @blk.sync
                    def _(sync, mf=mf, c=c, se_res=se_res, rb=rb):
                        sync.wait_ge(g.sSE.h, se_res)
                        g.sST.inc(sync.dma_start(
                            g.res_h.ap()[(mf - DIT) * 128:(mf - DIT + 1) * 128,
                                         c * CH:(c + 1) * CH],
                            sb_res[:, rb * CH:(rb + 1) * CH]), 16)
                    res_stores.append(g.sST.n)
                evs.append(g.sSE.n)
            xr_done = g.sSE.n

            cv_dones = []
            for dt in range(DIT):
                @blk.vector
                def _(v, dt=dt):
                    if dt == 0:
                        v.wait_ge(g.sSE.h, xr_done)
                    else:
                        v.wait_ge(g.sSE.h, cv_dones[-1])
                    base = dt * (CH + 3)
                    v.tensor_scalar(sb_cacc[:, 0:CH], sb_xrh[:, base: base + CH],
                                    g.c_wc[:, dt * 4: dt * 4 + 1], None,
                                    op0=AluOpType.mult).then_inc(g.sVE.h, 1)
                    for k in range(1, 4):
                        v.wait_ge(g.sVE.h, g.sVE.n + k)
                        src_half = ((k - 1) % 2) * CH
                        dst_half = (k % 2) * CH
                        v.scalar_tensor_tensor(sb_cacc[:, dst_half:dst_half + CH],
                                               sb_xrh[:, base + k: base + k + CH],
                                               g.c_wc[:, dt * 4 + k: dt * 4 + k + 1],
                                               sb_cacc[:, src_half:src_half + CH],
                                               op0=AluOpType.mult,
                                               op1=AluOpType.add).then_inc(g.sVE.h, 1)
                g.sVE.n += 4
                mcv = g.sVE.n
                @blk.scalar
                def _(s, dt=dt):
                    s.wait_ge(g.sVE.h, mcv)
                    s.activation(sb_xi[:, dt * CH:(dt + 1) * CH], sb_cacc[:, CH:2 * CH],
                                 AF.Silu,
                                 bias=g.c_bcv[:, dt:dt + 1], scale=1.0).then_inc(g.sSE.h, 1)
                g.sSE.n += 1
                cv_dones.append(g.sSE.n)
            @blk.vector
            def _(v):
                v.wait_ge(g.sSE.h, cv_dones[-1])
                for dt in range(DIT):
                    base = dt * (CH + 3)
                    v.tensor_copy(g.c_stats[:, 0:3], sb_xrh[:, base + CH: base + CH + 3])
                    v.drain()
                    v.tensor_copy(sb_xrh[:, base: base + 3], g.c_stats[:, 0:3])
                    v.drain()
                v.nop().then_inc(g.sVE.h, 1)
            g.sVE.n += 1
            prev_ve = g.sVE.n
            @blk.sync
            def _(sync, c=c):
                sync.wait_ge(g.sSE.h, cv_dones[-1])
                for dt in range(DIT):
                    g.sST.inc(sync.dma_start(
                        g.xi_h.ap()[dt * 128:(dt + 1) * 128, c * CH:(c + 1) * CH],
                        sb_xi[:, dt * CH:(dt + 1) * CH]), 16)
                if DEBUG:
                    for dt in range(DIT):
                        g.sST.inc(sync.dma_start(
                            g.dbg_xi.ap()[dt * 128:(dt + 1) * 128, c * CH:(c + 1) * CH],
                            sb_xi[:, dt * CH:(dt + 1) * CH]), 16)
            blk.flush_all()
    g.ms_xi_stored = g.sST.n





def emit_p3b(g, blk):
    nc = g.nc
    din = g.din
    DUW = DIT * CH
    HW = 8 * CH

    with ExitStack() as _es:
        sb_wx = _es.enter_context(nc.sbuf_tensor("sb_wx", [128, DIT * DI], BF16))
        sb_wxbc = _es.enter_context(nc.sbuf_tensor("sb_wxbc", [128, DIT * 32], BF16))
        sb_xic = _es.enter_context(nc.sbuf_tensor("sb_xic", [128, DIT * CH], BF16))
        sb_delta = _es.enter_context(nc.sbuf_tensor("sb_delta", [128, DIT * CH], BF16))
        sb_u = _es.enter_context(nc.sbuf_tensor("sb_u", [128, DIT * CH], BF16))
        sb_t1 = _es.enter_context(nc.sbuf_tensor("sb_t1", [128, DIT * CH], BF16))
        sb_bc = _es.enter_context(nc.sbuf_tensor("sb_bc", [32, 512], BF16))
        ps_p = _es.enter_context(nc.psum_tensor("ps_p", [128, 1024], F32))
        ps_bc = _es.enter_context(nc.psum_tensor("ps_bc", [128, 512], F32))

        @blk.gpsimd
        def _(gp):
            for i in range(DIT):
                g.sLG.inc(gp.dma_start(sb_wx[:, i * DI:(i + 1) * DI],
                                       din["wxT_d"].ap()[i * 128:(i + 1) * 128, :]), 16)
                g.sLG.inc(gp.dma_start(sb_wxbc[:, i * 32:(i + 1) * 32],
                                       din["wxT_bc"].ap()[i * 128:(i + 1) * 128, :]), 16)
        wx_l = g.sLG.n

        for c in range(NCH):
            ve_prev = g.sVE.n
            se_prev = g.sSE.n
            te_prev2 = g.sTE.n
            @blk.sync
            def _(sync, c=c):
                sync.wait_ge(g.sVE.h, ve_prev)
                sync.wait_ge(g.sSE.h, se_prev)
                sync.wait_ge(g.sTE.h, te_prev2)
                sync.wait_ge(g.sST.h, g.ms_xi_stored)
                for dt in range(DIT):
                    g.sLD.inc(sync.dma_start(
                        sb_xic[:, dt * CH:(dt + 1) * CH],
                        g.xi_h.ap()[dt * 128:(dt + 1) * 128, c * CH:(c + 1) * CH]), 16)
            xl = g.sLD.n
            evs = []
            for mt in range(DIT):
                @blk.tensor
                def _(t, mt=mt):
                    if mt == 0:
                        t.wait_ge(g.sLD.h, xl)
                        t.wait_ge(g.sLG.h, wx_l)
                    if mt >= 2:
                        t.wait_ge(g.sSE.h, evs[mt - 2])
                    p0 = (mt % 2) * 512
                    for kdt in range(DIT):
                        inst = t.matmul(ps_p[:, p0:p0 + CH],
                                        sb_wx[:, kdt * DI + mt * 128: kdt * DI + mt * 128 + 128],
                                        sb_xic[:, kdt * CH:(kdt + 1) * CH],
                                        start=(kdt == 0), stop=(kdt == DIT - 1))
                    inst.then_inc(g.sTE.h, 1)
                g.sTE.n += 1
                mm = g.sTE.n
                @blk.scalar
                def _(s, mt=mt):
                    s.wait_ge(g.sTE.h, mm)
                    p0 = (mt % 2) * 512
                    s.activation(sb_t1[:, mt * CH:(mt + 1) * CH], ps_p[:, p0:p0 + CH],
                                 AF.Exp,
                                 bias=g.c_bxd[:, mt:mt + 1], scale=1.0).then_inc(g.sSE.h, 1)
                g.sSE.n += 1
                evs.append(g.sSE.n)
            @blk.scalar
            def _(s):
                s.activation(sb_delta[:], sb_t1[:], AF.Ln, bias=1.0,
                             scale=1.0).then_inc(g.sSE.h, 1)
            g.sSE.n += 1
            evs.append(g.sSE.n)
            @blk.tensor
            def _(t):
                t.wait_ge(g.sSE.h, evs[-1])
                for kdt in range(DIT):
                    inst = t.matmul(ps_bc[0:32, 0:CH],
                                    sb_wxbc[:, kdt * 32:(kdt + 1) * 32],
                                    sb_xic[:, kdt * CH:(kdt + 1) * CH],
                                    start=(kdt == 0), stop=(kdt == DIT - 1))
                inst.then_inc(g.sTE.h, 1)
            g.sTE.n += 1
            mmbc = g.sTE.n
            @blk.scalar
            def _(s):
                s.wait_ge(g.sTE.h, mmbc)
                s.activation(sb_bc[:], ps_bc[0:32, 0:CH], AF.Identity,
                             bias=g.c_bxbc[:, 0:1], scale=1.0).then_inc(g.sSE.h, 1)
            g.sSE.n += 1
            bc_done = g.sSE.n
            @blk.vector
            def _(v):
                v.wait_ge(g.sSE.h, evs[-1])
                for dt in range(DIT):
                    v.tensor_tensor(sb_u[:, dt * CH:(dt + 1) * CH],
                                    sb_delta[:, dt * CH:(dt + 1) * CH],
                                    sb_xic[:, dt * CH:(dt + 1) * CH],
                                    op=AluOpType.mult)
                v.drain()
                v.nop().then_inc(g.sVE.h, 1)
            g.sVE.n += 1
            u_done = g.sVE.n
            @blk.dsync
            def _(sync, c=c):
                sync.wait_ge(g.sVE.h, u_done)
                sync.wait_ge(g.sSE.h, bc_done)
                r = g.r_role
                with sync.If_eq(r, 0):
                    reg = g.du_sh[(0, c)]
                    sn = g.du_sn[(0, c)]
                    sync.dma_start(reg.ap()[:, 0:HW], sb_delta[:, 0:HW]).then_inc(g.sST.h, 16)
                    sync.dma_start(reg.ap()[:, HW:2 * HW], sb_delta[:, HW:2 * HW]).then_inc(g.sST.h, 16)
                    sync.dma_start(reg.ap()[:, DUW:DUW + HW], sb_u[:, 0:HW]).then_inc(g.sST.h, 16)
                    sync.dma_start(reg.ap()[:, DUW + HW:2 * DUW], sb_u[:, HW:2 * HW]).then_inc(g.sST.h, 16)
                    sync.dma_start(reg.ap()[0:32, 2 * DUW:2 * DUW + 512], sb_bc[:]).then_inc(g.sST.h, 16)
                    sync.dma_start(sn.ap(), g.c_nonrow[:]).then_inc(g.sST.h, 16)
                with sync.Else():
                    reg = g.du_sh[(1, c)]
                    sn = g.du_sn[(1, c)]
                    sync.dma_start(reg.ap()[:, 0:HW], sb_delta[:, HW:2 * HW]).then_inc(g.sST.h, 16)
                    sync.dma_start(reg.ap()[:, HW:2 * HW], sb_delta[:, 0:HW]).then_inc(g.sST.h, 16)
                    sync.dma_start(reg.ap()[:, DUW:DUW + HW], sb_u[:, HW:2 * HW]).then_inc(g.sST.h, 16)
                    sync.dma_start(reg.ap()[:, DUW + HW:2 * DUW], sb_u[:, 0:HW]).then_inc(g.sST.h, 16)
                    sync.dma_start(reg.ap()[0:32, 2 * DUW:2 * DUW + 512], sb_bc[:]).then_inc(g.sST.h, 16)
                    sync.dma_start(sn.ap(), g.c_nonrow[:]).then_inc(g.sST.h, 16)
                sync.end_ifs()
            g.sST.n += 96
            blk.flush_all()
        send_done = g.sST.n

    with ExitStack() as _es:
        sb_sdu = _es.enter_context(nc.sbuf_tensor("sb_sdu", [128, 2 * 2 * HW], BF16))
        sb_Bb = _es.enter_context(nc.sbuf_tensor("sb_Bb", [128, 2 * 4 * CH], BF16))
        sb_Cb = _es.enter_context(nc.sbuf_tensor("sb_Cb", [128, 2 * 4 * CH], BF16))
        sb_dA = _es.enter_context(nc.sbuf_tensor("sb_dA", [128, 2 * 8 * CH], BF16))
        sb_dBx = _es.enter_context(nc.sbuf_tensor("sb_dBx", [128, 8 * CH], BF16))
        sb_h = _es.enter_context(nc.sbuf_tensor("sb_h", [128, 8 * CH], BF16))
        sb_yacc = _es.enter_context(nc.sbuf_tensor("sb_yacc", [128, 2 * 8 * CH], BF16))
        sb_y16 = _es.enter_context(nc.sbuf_tensor("sb_y16", [128, DIT * CH], BF16))
        sb_gxi = _es.enter_context(nc.sbuf_tensor("sb_gxi", [128, DIT * CH], BF16))
        sb_gres = _es.enter_context(nc.sbuf_tensor("sb_gres", [128, DIT * CH], BF16))
        g.sb_wo = _es.enter_context(nc.sbuf_tensor("sb_wo3", [128, DIT * DM], BF16))
        g.sb_cxc = _es.enter_context(nc.sbuf_tensor("sb_cxc3", [128, 4 * DM], F32))
        g.ps_co = _es.enter_context(nc.psum_tensor("ps_co3", [128, 2048], F32))
        g.p3c_prev_resid = [0]
        g.p3c_prev_store = [0]

        @blk.gpsimd
        def _(gp):
            for i in range(DIT):
                g.sLG.inc(gp.dma_start(g.sb_wo[:, i * DM:(i + 1) * DM],
                                       g.din["woT"].ap()[i * 128:(i + 1) * 128, :]), 16)
        g.ms_wo_loaded = g.sLG.n
        first_scan_gate = [send_done]
        ysend_hist = []
        scan_ms = []
        sc_dones = []
        se_dones = []
        for sc in range(8):
            owner, cc = sc // 4, sc % 4
            buf = sc % 2
            emit_spin(g, blk, g.du_sn[(owner, cc)])
            @blk.dsync
            def _(sync, owner=owner, cc=cc, buf=buf, sc=sc):
                if sc < 2:
                    sync.wait_ge(g.sST.h, first_scan_gate[0])
                else:
                    sync.wait_ge(g.sVE.h, sc_dones[sc - 2])
                    sync.wait_ge(g.sSE.h, se_dones[sc - 2])
                reg = g.du_sh[(owner, cc)]
                r = g.r_role
                sdu_sem = g.sDU0 if buf == 0 else g.sDU1
                with sync.If_eq(r, 0):
                    sync.dma_start(sb_sdu[:, buf * 2 * HW: buf * 2 * HW + HW],
                                   reg.ap()[:, 0:HW]).then_inc(sdu_sem.h, 16)
                    sync.dma_start(sb_sdu[:, buf * 2 * HW + HW: buf * 2 * HW + 2 * HW],
                                   reg.ap()[:, DUW:DUW + HW]).then_inc(sdu_sem.h, 16)
                with sync.Else():
                    sync.dma_start(sb_sdu[:, buf * 2 * HW: buf * 2 * HW + HW],
                                   reg.ap()[:, HW:2 * HW]).then_inc(sdu_sem.h, 16)
                    sync.dma_start(sb_sdu[:, buf * 2 * HW + HW: buf * 2 * HW + 2 * HW],
                                   reg.ap()[:, DUW + HW:2 * DUW]).then_inc(sdu_sem.h, 16)
                sync.end_ifs()
            sdu_sem = g.sDU0 if buf == 0 else g.sDU1
            sdu_sem.n += 32
            sdu_l = sdu_sem.n
            grp_done = []
            grp_read_ms = {}
            @blk.sync
            def _(sync, owner=owner, cc=cc, sdu_sem=sdu_sem):
                reg = g.du_sh[(owner, cc)]
                for ng in range(2):
                    for n in range(ng * 4, ng * 4 + 4):
                        gslot = (ng % 2) * 4 + (n % 4)
                        sdu_sem.inc(sync.dma_start(
                            sb_Bb[:, gslot * CH:(gslot + 1) * CH],
                            reg.ap()[n:n + 1, 2 * DUW:2 * DUW + CH].partition_broadcast(128)), 16)
                        sdu_sem.inc(sync.dma_start(
                            sb_Cb[:, gslot * CH:(gslot + 1) * CH],
                            reg.ap()[16 + n:17 + n, 2 * DUW:2 * DUW + CH].partition_broadcast(128)), 16)
                    grp_done.append(sdu_sem.n)

            ysent_prev = ysend_hist[-1] if ysend_hist else 0
            for n in range(NST):
                ng = n // 4
                if ng >= 2 and n % 4 == 0:
                    gate_ms = grp_read_ms.get(ng - 2, 0)
                    @blk.sync
                    def _(sync, ng=ng, gate_ms=gate_ms, owner=owner, cc=cc, sdu_sem=sdu_sem):
                        sync.wait_ge(g.sVE.h, gate_ms)
                        reg = g.du_sh[(owner, cc)]
                        for nn in range(ng * 4, ng * 4 + 4):
                            gslot = (ng % 2) * 4 + (nn % 4)
                            sdu_sem.inc(sync.dma_start(
                                sb_Bb[:, gslot * CH:(gslot + 1) * CH],
                                reg.ap()[nn:nn + 1, 2 * DUW:2 * DUW + CH].partition_broadcast(128)), 16)
                            sdu_sem.inc(sync.dma_start(
                                sb_Cb[:, gslot * CH:(gslot + 1) * CH],
                                reg.ap()[16 + nn:17 + nn, 2 * DUW:2 * DUW + CH].partition_broadcast(128)), 16)
                        grp_done.append(sdu_sem.n)
                da_buf = n % 2
                @blk.scalar
                def _(s, n=n, buf=buf, da_buf=da_buf, sdu_sem=sdu_sem):
                    s.wait_ge(sdu_sem.h, sdu_l)
                    if len(scan_ms) >= 2:
                        s.wait_ge(g.sVE.h, scan_ms[-2])
                    for dt in range(8):
                        inst = s.activation(
                            sb_dA[:, da_buf * 8 * CH + dt * CH: da_buf * 8 * CH + (dt + 1) * CH],
                            sb_sdu[:, buf * 2 * HW + dt * CH: buf * 2 * HW + (dt + 1) * CH],
                            AF.Exp, bias=0.0,
                            scale=g.c_A[:, dt * NST + n: dt * NST + n + 1])
                    inst.then_inc(g.sSE.h, 1)
                g.sSE.n += 1
                da_done = g.sSE.n
                @blk.vector
                def _(v, n=n, buf=buf, da_buf=da_buf, ysp=ysent_prev, ng=ng,
                      bcast_ms=grp_done[min(ng, len(grp_done) - 1)],
                      sdu_sem2=(g.sDU0 if buf == 0 else g.sDU1)):
                    v.wait_ge(g.sSE.h, da_done)
                    v.wait_ge(sdu_sem2.h, bcast_ms)
                    if n == 0 and ysp:
                        v.wait_ge(g.sST.h, ysp)
                    gslot = (ng % 2) * 4 + (n % 4)
                    Brep = sb_Bb[:, gslot * CH:(gslot + 1) * CH].rearrange(
                        "p (o c) -> p o c", o=1).broadcast_to([128, 8, CH])
                    Crep = sb_Cb[:, gslot * CH:(gslot + 1) * CH].rearrange(
                        "p (o c) -> p o c", o=1).broadcast_to([128, 8, CH])
                    v.tensor_tensor(sb_dBx[:].rearrange("p (o c) -> p o c", o=8),
                                    sb_sdu[:, buf * 2 * HW + HW: buf * 2 * HW + 2 * HW].rearrange(
                                        "p (o c) -> p o c", o=8),
                                    Brep, op=AluOpType.mult)
                    v.drain()
                    for dt in range(8):
                        v.tensor_tensor_scan(
                            sb_h[:, dt * CH:(dt + 1) * CH],
                            sb_dA[:, da_buf * 8 * CH + dt * CH: da_buf * 8 * CH + (dt + 1) * CH],
                            sb_dBx[:, dt * CH:(dt + 1) * CH],
                            g.c_hprev[:, dt * NST + n: dt * NST + n + 1],
                            op0=AluOpType.mult, op1=AluOpType.add)
                    v.drain()
                    v.tensor_copy(
                        g.c_hprev[:].rearrange("p (d n2) -> p d n2", n2=NST)[:, 0:8, n:n + 1],
                        sb_h[:].rearrange("p (d c) -> p d c", c=CH)[:, :, CH - 1:CH])
                    v.drain()
                    ybuf = (sc % 2) * 8 * CH
                    dst = sb_yacc[:, ybuf:ybuf + 8 * CH] if n == 0 else sb_dBx[:]
                    v.tensor_tensor(dst.rearrange("p (o c) -> p o c", o=8),
                                    sb_h[:].rearrange("p (o c) -> p o c", o=8),
                                    Crep, op=AluOpType.mult)
                    v.drain()
                    if n > 0:
                        v.tensor_tensor(sb_yacc[:, ybuf:ybuf + 8 * CH],
                                        sb_yacc[:, ybuf:ybuf + 8 * CH],
                                        sb_dBx[:], op=AluOpType.add)
                        v.drain()
                    v.nop().then_inc(g.sVE.h, 1)
                g.sVE.n += 1
                scan_ms.append(g.sVE.n)
                grp_read_ms[ng] = g.sVE.n
                if n % 4 == 3:
                    blk.flush_all()
            sc_done = g.sVE.n
            sc_dones.append(g.sVE.n)
            se_dones.append(g.sSE.n)
            @blk.dsync
            def _(sync, sc=sc):
                sync.wait_ge(g.sVE.h, sc_done)
                ybuf = (sc % 2) * 8 * CH
                r = g.r_role
                with sync.If_eq(r, 0):
                    sync.dma_start(g.y_sh[(sc, 0)].ap(), sb_yacc[:, ybuf:ybuf + 8 * CH]).then_inc(g.sST.h, 16)
                    sync.dma_start(g.y_sn[(sc, 0)].ap(), g.c_nonrow[:]).then_inc(g.sST.h, 16)
                with sync.Else():
                    sync.dma_start(g.y_sh[(sc, 1)].ap(), sb_yacc[:, ybuf:ybuf + 8 * CH]).then_inc(g.sST.h, 16)
                    sync.dma_start(g.y_sn[(sc, 1)].ap(), g.c_nonrow[:]).then_inc(g.sST.h, 16)
                sync.end_ifs()
            g.sST.n += 32
            ysend_hist.append(g.sST.n)

            if sc >= 4:
                gi = sc - 4
                emit_gate(g, blk, gi, sb_y16, sb_gxi, sb_gres)
    g.ms_y3_stored = g.sST.n


def emit_gate(g, blk, gi, sb_y16, sb_gxi, sb_gres):
    nc = g.nc
    ve_prev = g.sVE.n
    @blk.sync
    def _(sync, vp=ve_prev):
        sync.wait_ge(g.sVE.h, vp)
    for part in range(2):
        @blk.dsync
        def _(sync, gi=gi, part=part):
            e = g.rv_nonce
            pr = g.r_probe
            r = g.r_role
            nread = [0]
            def spin_then_read(sync, reg_key):
                sync.sem_clear(g.sSP)
                sync.dma_start(g.c_probe[:], g.y_sn[reg_key].ap()).then_inc(g.sSP, 16)
                nread[0] = 1
                sync.wait_ge(g.sSP, 16)
                for ci_, row in enumerate(SPIN_CELLS):
                    sync.reg_load(pr, g.c_probe[row:row + 1, 0:1])
                    nout = 6 if ci_ == 0 else 1
                    for _o in range(nout):
                        with sync.If_ne(pr, e):
                            for _i in range(10 if ci_ == 0 else 1):
                                with sync.If_ne(pr, e):
                                    sync.nop(cycle_cnt=8000)
                                    sync.dma_start(g.c_probe[:], g.y_sn[reg_key].ap()).then_inc(g.sSP, 16)
                                    nread[0] += 1
                                    sync.wait_ge(g.sSP, 16 * nread[0])
                                    sync.reg_load(pr, g.c_probe[row:row + 1, 0:1])
                                sync.end_ifs()
                        sync.end_ifs()
                sync.dma_start(sb_y16[:, part * 8 * CH:(part + 1) * 8 * CH],
                               g.y_sh[reg_key].ap()).then_inc(g.sGT.h, 16)
            with sync.If_eq(r, 0):
                spin_then_read(sync, (gi, part))
            with sync.Else():
                spin_then_read(sync, (4 + gi, 1 - part))
            sync.end_ifs()
        g.sGT.n += 16
    y_l = g.sGT.n
    @blk.gpsimd
    def _(gp, gi=gi, vp=ve_prev):
        gp.wait_ge(g.sVE.h, vp)
        for dt in range(DIT):
            g.sLG.inc(gp.dma_start(
                sb_gxi[:, dt * CH:(dt + 1) * CH],
                g.xi_h.ap()[dt * 128:(dt + 1) * 128, gi * CH:(gi + 1) * CH]), 16)
            g.sLG.inc(gp.dma_start(
                sb_gres[:, dt * CH:(dt + 1) * CH],
                g.res_h.ap()[dt * 128:(dt + 1) * 128, gi * CH:(gi + 1) * CH]), 16)
    xi_l = g.sLG.n
    @blk.scalar
    def _(s):
        s.wait_ge(g.sLG.h, xi_l)
        s.activation(sb_gres[:], sb_gres[:], AF.Silu, bias=0.0, scale=1.0).then_inc(g.sSE.h, 1)
    g.sSE.n += 1
    sr_done = g.sSE.n
    @blk.vector
    def _(v, gi=gi):
        v.wait_ge(g.sSE.h, sr_done)
        v.wait_ge(g.sGT.h, y_l)
        v.tensor_tensor(sb_gxi[:].rearrange("p (d c) -> p d c", c=CH),
                        sb_gxi[:].rearrange("p (d c) -> p d c", c=CH),
                        g.c_D[:].rearrange("p (d o) -> p d o", o=1).broadcast_to([128, DIT, CH]),
                        op=AluOpType.mult)
        v.drain()
        v.tensor_tensor(sb_y16[:], sb_y16[:], sb_gxi[:], op=AluOpType.add)
        v.drain()
        v.tensor_tensor(sb_y16[:], sb_y16[:], sb_gres[:], op=AluOpType.mult)
        v.drain()
        v.nop().then_inc(g.sVE.h, 1)
    g.sVE.n += 1
    g3 = g.sVE.n
    @blk.sync
    def _(sync, gi=gi, ps0=g.p3c_prev_store[0]):
        if ps0:
            sync.wait_ge(g.sST.h, ps0)
        for r in range(4):
            g.sGT.inc(sync.dma_start(
                g.sb_cxc[:, r * DM:(r + 1) * DM],
                g.x2_h.ap()[gi * CH + 256 + r * 128: gi * CH + 256 + (r + 1) * 128, :]), 16)
    x2_l = g.sGT.n
    resid_ms = []
    for tt in range(4):
        @blk.tensor
        def _(t, tt=tt, gi=gi, pr3=g.p3c_prev_resid[0]):
            if tt == 0:
                t.wait_ge(g.sVE.h, g3)
                t.wait_ge(g.sGT.h, x2_l)
                t.wait_ge(g.sLG.h, g.ms_wo_loaded)
                if pr3:
                    t.wait_ge(g.sVE.h, pr3)
            if len(resid_ms) >= 2:
                t.wait_ge(g.sVE.h, resid_ms[-2])
            for fh in range(2):
                for kdt in range(DIT):
                    t.matmul(g.ps_co[:, ((tt % 2) * 2 + fh) * 512:((tt % 2) * 2 + fh) * 512 + 512],
                             sb_y16[:, kdt * CH + tt * 128: kdt * CH + tt * 128 + 128],
                             g.sb_wo[:, kdt * DM + fh * 512: kdt * DM + (fh + 1) * 512],
                             start=(kdt == 0), stop=False)
                inst = t.matmul(g.ps_co[:, ((tt % 2) * 2 + fh) * 512:((tt % 2) * 2 + fh) * 512 + 512],
                                g.c_ones[0:1, 0:128],
                                g.c_borow[0:1, fh * 512:(fh + 1) * 512],
                                start=False, stop=True)
            inst.then_inc(g.sTE.h, 1)
        g.sTE.n += 1
        mm3 = g.sTE.n
        @blk.vector
        def _(v, tt=tt, ps0=g.p3c_prev_store[0], mm3=mm3):
            v.wait_ge(g.sTE.h, mm3)
            if tt == 0 and ps0:
                v.wait_ge(g.sST.h, ps0)
            v.tensor_tensor(g.sb_cxc[:, tt * DM:(tt + 1) * DM],
                            g.sb_cxc[:, tt * DM:(tt + 1) * DM],
                            g.ps_co[:, (tt % 2) * 1024:(tt % 2) * 1024 + 1024],
                            op=AluOpType.add).then_inc(g.sVE.h, 1)
        g.sVE.n += 1
        resid_ms.append(g.sVE.n)
    g.p3c_prev_resid[0] = g.sVE.n
    done_v3 = g.sVE.n
    @blk.sync
    def _(sync, gi=gi, done_v3=done_v3):
        sync.wait_ge(g.sVE.h, done_v3)
        for r in range(4):
            g.sST.inc(sync.dma_start(
                g.x3_h.ap()[gi * CH + r * 128: gi * CH + (r + 1) * 128, :],
                g.sb_cxc[:, r * DM:(r + 1) * DM]), 16)
            if DEBUG:
                g.sST.inc(sync.dma_start(
                    g.dbg_x3.ap()[gi * CH + r * 128: gi * CH + (r + 1) * 128, :],
                    g.sb_cxc[:, r * DM:(r + 1) * DM]), 16)
    g.p3c_prev_store[0] = g.sST.n
    blk.flush_all()



def emit_p3c(g, blk):
    nc = g.nc
    din = g.din
    with ExitStack() as _es:
        sb_wo = _es.enter_context(nc.sbuf_tensor("sb_wo", [128, DIT * DM], BF16))
        sb_y3 = _es.enter_context(nc.sbuf_tensor("sb_y3", [128, DIT * CH], BF16))
        sb_xc = _es.enter_context(nc.sbuf_tensor("sb_cxc", [128, 4 * DM], F32))
        sb_o = _es.enter_context(nc.sbuf_tensor("sb_co", [128, 4 * DM], F32))
        ps_o = _es.enter_context(nc.psum_tensor("ps_co", [128, 2048], F32))

        @blk.gpsimd
        def _(gp):
            for i in range(DIT):
                g.sLG.inc(gp.dma_start(sb_wo[:, i * DM:(i + 1) * DM],
                                       din["woT"].ap()[i * 128:(i + 1) * 128, :]), 16)
        wo_l = g.sLG.n

        prev_resid = 0
        prev_store = g.sST.n
        resid_ms = []
        for c in range(NCH):
            te_prev3 = g.sTE.n
            @blk.sync
            def _(sync, c=c, pv=prev_resid, tp=te_prev3):
                if pv:
                    sync.wait_ge(g.sVE.h, pv)
                sync.wait_ge(g.sTE.h, tp)
                sync.wait_ge(g.sST.h, g.ms_y3_stored)
                for dt in range(DIT):
                    g.sLD.inc(sync.dma_start(
                        sb_y3[:, dt * CH:(dt + 1) * CH],
                        g.y3_h.ap()[dt * 128:(dt + 1) * 128, c * CH:(c + 1) * CH]), 16)
                for r in range(4):
                    g.sLD.inc(sync.dma_start(
                        sb_xc[:, r * DM:(r + 1) * DM],
                        g.x2_h.ap()[c * CH + 256 + r * 128: c * CH + 256 + (r + 1) * 128, :]), 16)
            yl = g.sLD.n
            for tt in range(4):
                @blk.tensor
                def _(t, tt=tt, pv=prev_resid):
                    if tt == 0:
                        t.wait_ge(g.sLD.h, yl)
                        t.wait_ge(g.sLG.h, wo_l)
                    if len(resid_ms) >= 2:
                        t.wait_ge(g.sVE.h, resid_ms[-2])
                    for fh in range(2):
                        for kdt in range(DIT):
                            t.matmul(ps_o[:, ((tt % 2) * 2 + fh) * 512:((tt % 2) * 2 + fh) * 512 + 512],
                                     sb_y3[:, kdt * CH + tt * 128: kdt * CH + tt * 128 + 128],
                                     sb_wo[:, kdt * DM + fh * 512: kdt * DM + (fh + 1) * 512],
                                     start=(kdt == 0), stop=False)
                        inst = t.matmul(ps_o[:, ((tt % 2) * 2 + fh) * 512:((tt % 2) * 2 + fh) * 512 + 512],
                                        g.c_ones[0:1, 0:128],
                                        g.c_borow[0:1, fh * 512:(fh + 1) * 512],
                                        start=False, stop=True)
                    inst.then_inc(g.sTE.h, 1)
                g.sTE.n += 1
                mm = g.sTE.n
                @blk.vector
                def _(v, tt=tt, c=c, ps0=prev_store):
                    v.wait_ge(g.sTE.h, mm)
                    if tt == 0:
                        v.wait_ge(g.sST.h, ps0)
                    v.tensor_tensor(sb_xc[:, tt * DM:(tt + 1) * DM],
                                    sb_xc[:, tt * DM:(tt + 1) * DM],
                                    ps_o[:, 0:1024],
                                    op=AluOpType.add).then_inc(g.sVE.h, 1)
                g.sVE.n += 1
                prev_resid = g.sVE.n
                resid_ms.append(g.sVE.n)
            done_v = g.sVE.n
            @blk.sync
            def _(sync, c=c):
                sync.wait_ge(g.sVE.h, done_v)
                for r in range(4):
                    g.sST.inc(sync.dma_start(
                        g.x3_h.ap()[c * CH + r * 128: c * CH + (r + 1) * 128, :],
                        sb_xc[:, r * DM:(r + 1) * DM]), 16)
                    if DEBUG:
                        g.sST.inc(sync.dma_start(
                            g.dbg_x3.ap()[c * CH + r * 128: c * CH + (r + 1) * 128, :],
                            sb_xc[:, r * DM:(r + 1) * DM]), 16)
            prev_store = g.sST.n
            blk.flush_all()


def emit(g):
    nc = g.nc
    with nc.Block() as rawblk:
        blk = RBlk(rawblk)
        emit_p0(g, blk)
        @blk.dsync
        def _(sync):
            sync.wait_ge(g.sLD.h, g.sLD.n)
            g.r_role = sync.to_reg(sync.value_load(g.c_rolei[0:1, 0:1]))
            g.rv_nonce = sync.value_load(g.c_noni[0:1, 0:1])
            g.r_probe = sync.alloc_register()
        barrier(g, blk)
        emit_p1(g, blk)
        barrier(g, blk)
        emit_ffn(g, blk, first=True)
        barrier(g, blk)
        emit_p3a(g, blk)
        barrier(g, blk)
        emit_p3b(g, blk)
        barrier(g, blk)
        emit_ffn(g, blk, first=False)
        blk.flush_all()
        @blk.sync
        def _(sync):
            sync.wait_ge(g.sST.h, g.sST.n)
        blk.flush_all()



def kernel(**inputs):
    ims = stage_inputs(inputs)
    nc, g = build_program()
    res = run_bass_kernel_spmd(nc, ims, core_ids=list(range(8)))
    outp = np.zeros((B, SEQ, DM), np.float32)
    for c in range(8):
        b_ = c // 2
        half = c % 2
        outp[b_, half * TOK:(half + 1) * TOK] = res.results[c]["out"]
    return outp
